# version 05
# baseline (speedup 1.0000x reference)
import numpy as np
import ml_dtypes
from contextlib import ExitStack

import concourse.bass as bass
import concourse.mybir as mybir
from concourse.bass_utils import run_bass_kernel_spmd

F32 = mybir.dt.float32
BF16 = mybir.dt.bfloat16
AF = mybir.ActivationFunctionType
ALU = mybir.AluOpType
AX = mybir.AxisListType

S = 4096
D = 1024
NT = S // 128
NCD = D // 128
BF = ml_dtypes.bfloat16
FCONSTS = [1e-5, -0.5, 0.5, 2.0, -1.0, -2.0, 1.0 / 1024, 8.0, 0.001, 0.998,
           1e-12, 1.0 / 6.0, 6.0, 0.125, 1.0 / 512, 1.0 / 2816, 1.0 / 64,
           -1e9, 0.25, 0.75, 1.25, 1.75, 2.5, 3.5, 5.0, 3.0, 4.0, 1.5]


def host_constants():
    c = {}
    c["ident_b"] = np.eye(128, dtype=np.float32).astype(BF)
    c["ident_f"] = np.eye(128, dtype=np.float32)
    c["fconsts"] = np.array(FCONSTS, dtype=np.float32).reshape(1, -1)
    DIM_HEAD, WINDOW, n = 64, 512, S
    half = np.arange(0, DIM_HEAD, 2, dtype=np.float32)
    inv_freq = 1.0 / 10000.0 ** (half / DIM_HEAD)
    t = np.arange(n, dtype=np.float32)
    freqs = t[:, None] * inv_freq[None, :]
    freqs = np.concatenate([freqs, freqs], axis=-1)
    base = (half + 0.4 * DIM_HEAD) / (1.4 * DIM_HEAD)
    power = (t - n // 2) / float(WINDOW // 2)
    scale = base[None, :] ** power[:, None]
    scale = np.concatenate([scale, scale], axis=-1)
    cos, sin = np.cos(freqs), np.sin(freqs)
    QK_SCALE = 8.0
    c["rope_cq"] = (cos * scale * QK_SCALE).astype(np.float32)
    c["rope_sq"] = (sin * scale * QK_SCALE).astype(np.float32)
    c["rope_ck"] = (cos / scale).astype(np.float32)
    c["rope_sk"] = (sin / scale).astype(np.float32)
    masks = np.zeros((8, 128, 1024), dtype=np.float32)
    for qp in range(4):
        qi = (np.arange(128) + 128 * qp)[:, None]
        kj = np.arange(1024)[None, :]
        valid = (qi <= kj) & (kj <= qi + 512)
        valid0 = (kj >= 512) & (kj <= qi + 512)
        masks[qp] = np.where(valid, 0.0, -1e9)
        masks[4 + qp] = np.where(valid0, 0.0, -1e9)
    c["lmask"] = masks.reshape(8 * 128, 1024)
    return c


class Ctx:
    def __init__(self, nc):
        self.nc = nc
        self.es = ExitStack()
        self.cnt = {}
        self.sems = {}
        self.sb = {}
        self.ps = {}
        self.mark = {}
        self.scope_id = 0

    def sem(self, name):
        s = self.es.enter_context(self.nc.semaphore(name))
        self.sems[name] = s
        self.cnt[name] = 0
        return s

    def inc(self, instr, name, n=1):
        self.cnt[name] += n
        instr.then_inc(self.sems[name], n)
        return self.cnt[name]

    def wait(self, eng, name, val=None):
        v = self.cnt[name] if val is None else val
        if v > 0:
            eng.wait_ge(self.sems[name], v)

    def rec(self, key, sem):
        self.mark[key] = self.cnt[sem]

    def at(self, key, default=0):
        return self.mark.get(key, default)

    def sbuf(self, name, shape, dtype):
        mangled = name if self.scope_id == 0 else "%s_s%d" % (name, self.scope_id)
        h = self.es.enter_context(self.nc.sbuf_tensor(mangled, shape, dtype))
        self.sb[name] = h
        return h

    def psum(self, name, shape, dtype):
        h = self.es.enter_context(self.nc.psum_tensor(name, shape, dtype))
        self.ps[name] = h
        return h

    def push_scope(self):
        self._scope = ExitStack()
        self._saved_es = self.es
        self.es = self._scope
        self.scope_id += 1

    def pop_scope(self):
        self._scope.close()
        self.es = self._saved_es

    def reset_counts(self):
        for s in ("ld", "st", "pe", "ve", "se"):
            self.cnt[s] = 0
        self.mark = {}


def emit_block_A(ctx, block, x_in, x_out, W_aps, b_aps, consts):
    sb, ps = ctx.sb, ctx.ps
    xlnT = sb["big0"]
    wT = sb["wT"]
    xt = sb["xt"]; sq = sb["sq"]; lnout = sb["lnout"]; stat = sb["stat"]
    bcol = sb["bcol"]; idb = sb["idb"]; idf = sb["idf"]
    wtile = sb["wtile"]; wtb = sb["wtb"]; gbuf = sb["gbuf"]
    afull = sb["afull"]; xsfull = sb["xsfull"]; hfull = sb["hfull"]
    rbuf = sb["rbuf"]; obuf = sb["obuf"]
    p0, p1, p2, p3 = ps["p0"], ps["p1"], ps["p2"], ps["p3"]
    pb0, pb1 = ps["pb0"], ps["pb1"]

    @block.sync
    def _(sync):
        ctx.inc(sync.dma_start(out=idb[:], in_=consts["ident_b"]), "ld", 16)
        ctx.inc(sync.dma_start(out=idf[:], in_=consts["ident_f"]), "ld", 16)
        for w, off in (("bg", 0), ("bv", 8), ("bd", 16)):
            for k in range(8):
                ctx.inc(sync.dma_start(out=bcol[:, off + k:off + k + 1],
                                       in_=b_aps[w][k * 128:(k + 1) * 128]), "ld", 16)

    for j in range(NT):
        bi = j % 2
        sc = bi * 4
        xv = xt[:, bi * 1024:(bi + 1) * 1024]
        lv = lnout[:, bi * 1024:(bi + 1) * 1024]

        @block.sync
        def _(sync, j=j, xv=xv):
            ctx.wait(sync, "ve", ctx.at("a1_apply_%d" % (j - 2)))
            ctx.inc(sync.dma_start(out=xv, in_=x_in[j * 128:(j + 1) * 128, :]), "ld", 16)

        ld_now = ctx.cnt["ld"]

        @block.vector
        def _(vector, xv=xv, ld_now=ld_now, sc=sc):
            ctx.wait(vector, "ld", ld_now)
            ctx.inc(vector.reduce_sum(stat[:, sc:sc + 1], xv, AX.X), "ve", 1)

        @block.scalar
        def _(scalar, xv=xv, ld_now=ld_now, sc=sc, j=j):
            ctx.wait(scalar, "ld", ld_now)
            ctx.inc(scalar.activation(sq[:], xv, AF.Square, bias=0.0, scale=1.0,
                                      accum_out=stat[:, sc + 1:sc + 2]), "se", 1)

        se_sq = ctx.cnt["se"]

        @block.vector
        def _(vector, se_sq=se_sq, sc=sc):
            ctx.wait(vector, "se", se_sq)
            ctx.wait(vector, "ve")
            ctx.inc(vector.tensor_scalar_mul(stat[:, sc:sc + 1], stat[:, sc:sc + 1],
                                             1.0 / D), "ve", 1)
            ctx.wait(vector, "ve")
            ctx.inc(vector.scalar_tensor_tensor(stat[:, sc + 3:sc + 4], stat[:, sc:sc + 1],
                                                1.0, stat[:, sc:sc + 1],
                                                ALU.bypass, ALU.mult), "ve", 1)
            ctx.wait(vector, "ve")
            ctx.inc(vector.scalar_tensor_tensor(stat[:, sc + 2:sc + 3], stat[:, sc + 1:sc + 2],
                                                1.0 / D, stat[:, sc + 3:sc + 4],
                                                ALU.mult, ALU.subtract), "ve", 1)

        ve_var = ctx.cnt["ve"]

        @block.scalar
        def _(scalar, ve_var=ve_var, sc=sc):
            ctx.wait(scalar, "ve", ve_var)
            ctx.inc(scalar.activation(stat[:, sc + 2:sc + 3], stat[:, sc + 2:sc + 3],
                                      AF.Ln, bias=1e-5, scale=1.0), "se", 1)
            ctx.wait(scalar, "se")
            ctx.inc(scalar.activation(stat[:, sc + 2:sc + 3], stat[:, sc + 2:sc + 3],
                                      AF.Exp, bias=0.0, scale=-0.5), "se", 1)

        se_rs = ctx.cnt["se"]

        @block.vector
        def _(vector, xv=xv, lv=lv, se_rs=se_rs, j=j, sc=sc):
            ctx.wait(vector, "se", se_rs)
            ctx.wait(vector, "pe", ctx.at("a1_tr_%d" % (j - 2)))
            i = vector.tensor_scalar(lv, xv, stat[:, sc:sc + 1], stat[:, sc + 2:sc + 3],
                                     ALU.subtract, ALU.mult)
            ctx.inc(i, "ve", 1)
            ctx.rec("a1_apply_%d" % j, "ve")

        ve_ln = ctx.cnt["ve"]

        @block.tensor
        def _(tensor, lv=lv, j=j, ve_ln=ve_ln):
            ctx.wait(tensor, "ve", ve_ln)
            ctx.wait(tensor, "se", ctx.at("a1_evac_%d" % (j - 2)))
            pbank = pb0 if (j % 2 == 0) else pb1
            for k in range(8):
                i = tensor.transpose(pbank[:, k * 128:(k + 1) * 128],
                                     lv[:, k * 128:(k + 1) * 128], idb[:])
            ctx.inc(i, "pe", 1)
            ctx.rec("a1_tr_%d" % j, "pe")

        pe_tr = ctx.cnt["pe"]

        @block.scalar
        def _(scalar, j=j, pe_tr=pe_tr):
            ctx.wait(scalar, "pe", pe_tr)
            pbank = pb0 if (j % 2 == 0) else pb1
            for k in range(8):
                i = scalar.copy(xlnT[:, k * 4096 + j * 128: k * 4096 + (j + 1) * 128],
                                pbank[:, k * 128:(k + 1) * 128])
            ctx.inc(i, "se", 1)
            ctx.rec("a1_evac_%d" % j, "se")

    ctx.rec("a2a_sebase", "se")
    ctx.rec("a2a_pebase", "pe")
    for wi, wname in enumerate(("Wg", "Wv", "Wd")):
        for i in range(8):
            for k in range(8):
                n = wi * 64 + i * 8 + k
                bi = n % 2
                wv = wtile[:, bi * 128:(bi + 1) * 128]
                wb = wtb[:, bi * 128:(bi + 1) * 128]

                @block.sync
                def _(sync, wname=wname, i=i, k=k, n=n, wv=wv):
                    ctx.wait(sync, "ve", ctx.at("a2a_cv_%d" % (n - 2)))
                    ctx.inc(sync.dma_start(
                        out=wv, in_=W_aps[wname][i * 128:(i + 1) * 128,
                                                 k * 128:(k + 1) * 128]), "ld", 16)

                ld_now = ctx.cnt["ld"]

                @block.vector
                def _(vector, wv=wv, wb=wb, ld_now=ld_now, n=n):
                    ctx.wait(vector, "ld", ld_now)
                    ctx.wait(vector, "pe", ctx.at("a2a_tr_%d" % (n - 2),
                                                  ctx.at("a2a_pebase")))
                    ctx.inc(vector.tensor_copy(wb, wv), "ve", 1)
                    ctx.rec("a2a_cv_%d" % n, "ve")

                ve_now = ctx.cnt["ve"]

                @block.tensor
                def _(tensor, wb=wb, ve_now=ve_now, n=n):
                    ctx.wait(tensor, "ve", ve_now)
                    ctx.wait(tensor, "se", ctx.at("a2a_ev_%d" % (n - 2),
                                                  ctx.at("a2a_sebase")))
                    pbank = pb0 if (n % 2 == 0) else pb1
                    ctx.inc(tensor.transpose(pbank[:, 0:128], wb, idb[:]), "pe", 1)
                    ctx.rec("a2a_tr_%d" % n, "pe")

                pe_now = ctx.cnt["pe"]

                @block.scalar
                def _(scalar, wi=wi, i=i, k=k, n=n, pe_now=pe_now):
                    ctx.wait(scalar, "pe", pe_now)
                    pbank = pb0 if (n % 2 == 0) else pb1
                    i2 = scalar.copy(wT[:, wi * 8192 + k * 1024 + i * 128:
                                        wi * 8192 + k * 1024 + (i + 1) * 128],
                                     pbank[:, 0:128])
                    ctx.inc(i2, "se", 1)
                    ctx.rec("a2a_ev_%d" % n, "se")

    ve_all = ctx.cnt["ve"]
    se_all = ctx.cnt["se"]
    for i in range(8):
        for cch in range(8):
            t0 = cch * 512

            @block.tensor
            def _(tensor, i=i, cch=cch, t0=t0):
                if i == 0 and cch == 0:
                    ctx.wait(tensor, "ve", ve_all)
                    ctx.wait(tensor, "se", se_all)
                ctx.wait(tensor, "se", ctx.at("a2b_gate_se"))
                for wi, bank in ((0, p0), (1, p1), (2, p2)):
                    for k in range(8):
                        ii = tensor.matmul(bank[:],
                                           wT[:, wi * 8192 + k * 1024 + i * 128:
                                              wi * 8192 + k * 1024 + (i + 1) * 128],
                                           xlnT[:, k * 4096 + t0:k * 4096 + t0 + 512],
                                           start=(k == 0), stop=(k == 7))
                ctx.inc(ii, "pe", 1)

            pe_now = ctx.cnt["pe"]

            @block.scalar
            def _(scalar, i=i, t0=t0, pe_now=pe_now):
                ctx.wait(scalar, "pe", pe_now)
                ctx.wait(scalar, "ve", ctx.at("a2b_gate_ve"))
                scalar.activation(gbuf[:, 0:512], p0[:], AF.Sigmoid,
                                  bias=bcol[:, 0 + i:1 + i], scale=1.0)
                scalar.activation(gbuf[:, 512:1024], p1[:], AF.Tanh,
                                  bias=bcol[:, 8 + i:9 + i], scale=1.0)
                ctx.inc(scalar.activation(gbuf[:, 1024:1536], p2[:], AF.Sigmoid,
                                          bias=bcol[:, 16 + i:17 + i], scale=1.0), "se", 1)
                ctx.rec("a2b_gate_se", "se")

            se_now = ctx.cnt["se"]

            @block.vector
            def _(vector, t0=t0, se_now=se_now):
                ctx.wait(vector, "se", se_now)
                vector.tensor_tensor(xsfull[:, t0:t0 + 512], gbuf[:, 0:512],
                                     gbuf[:, 512:1024], ALU.mult)
                i2 = vector.tensor_scalar(afull[:, t0:t0 + 512], gbuf[:, 1024:1536],
                                          0.998, 0.001, ALU.mult, ALU.add)
                ctx.inc(i2, "ve", 1)
                ctx.rec("a2b_gate_ve", "ve")

        @block.vector
        def _(vector, i=i):
            ctx.wait(vector, "ve")
            ctx.inc(vector.tensor_tensor_scan(hfull[:], afull[:], xsfull[:], 0.0,
                                              ALU.mult, ALU.add), "ve", 1)

        ve_scan = ctx.cnt["ve"]

        for q in range(8):
            @block.tensor
            def _(tensor, i=i, q=q, ve_scan=ve_scan):
                ctx.wait(tensor, "ve", ve_scan)
                ctx.wait(tensor, "ve", ctx.at("a2b_res_ve"))
                for r in range(4):
                    jt = q * 4 + r
                    ii = tensor.transpose(p3[:, r * 128:(r + 1) * 128],
                                          hfull[:, jt * 128:(jt + 1) * 128], idf[:])
                ctx.inc(ii, "pe", 1)

            pe_now = ctx.cnt["pe"]

            @block.sync
            def _(sync, i=i, q=q):
                ctx.wait(sync, "ve", ctx.at("a2b_res_ve"))
                for r in range(4):
                    jt = q * 4 + r
                    ctx.inc(sync.dma_start(
                        out=rbuf[:, r * 128:(r + 1) * 128],
                        in_=x_in[jt * 128:(jt + 1) * 128, i * 128:(i + 1) * 128]), "ld", 16)

            ld_now = ctx.cnt["ld"]

            @block.vector
            def _(vector, pe_now=pe_now, ld_now=ld_now):
                ctx.wait(vector, "pe", pe_now)
                ctx.wait(vector, "ld", ld_now)
                ctx.wait(vector, "st", ctx.at("a2b_res_st"))
                ctx.inc(vector.tensor_tensor(obuf[:, 0:512], p3[:], rbuf[:, 0:512],
                                             ALU.add), "ve", 1)
                ctx.rec("a2b_res_ve", "ve")

            ve_now = ctx.cnt["ve"]

            @block.sync
            def _(sync, i=i, q=q, ve_now=ve_now):
                ctx.wait(sync, "ve", ve_now)
                for r in range(4):
                    jt = q * 4 + r
                    ctx.inc(sync.dma_start(
                        out=x_out[jt * 128:(jt + 1) * 128, i * 128:(i + 1) * 128],
                        in_=obuf[:, r * 128:(r + 1) * 128]), "st", 16)
                ctx.rec("a2b_res_st", "st")


def alloc_shared(ctx):
    ctx.psum("p0", [128, 512], F32)
    ctx.psum("p1", [128, 512], F32)
    ctx.psum("p2", [128, 512], F32)
    ctx.psum("p3", [128, 512], F32)
    ctx.psum("pb0", [128, 1024], BF16)
    ctx.psum("pb1", [128, 1024], BF16)
    ctx.sbuf("cpool", [128, 64], F32)
    for s in ("ld", "st", "pe", "ve", "se", "gp", "gld"):
        ctx.sem(s)


def alloc_common(ctx):
    ctx.sbuf("big0", [128, 32768], BF16)
    ctx.sbuf("wT", [128, 24576], BF16)
    ctx.sbuf("xt", [128, 2048], F32)
    ctx.sbuf("sq", [128, 1024], F32)
    ctx.sbuf("lnout", [128, 2048], BF16)
    ctx.sbuf("stat", [128, 8], F32)
    ctx.sbuf("bcol", [128, 24], F32)
    ctx.sbuf("idb", [128, 128], BF16)
    ctx.sbuf("idf", [128, 128], F32)
    ctx.sbuf("wtile", [128, 256], F32)
    ctx.sbuf("wtb", [128, 256], BF16)
    ctx.sbuf("gbuf", [128, 1536], F32)
    ctx.sbuf("afull", [128, 4096], F32)
    ctx.sbuf("xsfull", [128, 4096], F32)
    ctx.sbuf("hfull", [128, 4096], F32)
    ctx.sbuf("rbuf", [128, 1024], F32)
    ctx.sbuf("obuf", [128, 1024], F32)


DEBUG_A = False

def build_A():
    nc = bass.Bass("TRN2", target_bir_lowering=False)
    x_in = nc.dram_tensor("x_in", [S, D], F32, kind="ExternalInput").ap()
    Wg = nc.dram_tensor("Wg", [D, D], F32, kind="ExternalInput").ap()
    Wv = nc.dram_tensor("Wv", [D, D], F32, kind="ExternalInput").ap()
    Wd = nc.dram_tensor("Wd", [D, D], F32, kind="ExternalInput").ap()
    bg = nc.dram_tensor("bg", [D], F32, kind="ExternalInput").ap()
    bv = nc.dram_tensor("bv", [D], F32, kind="ExternalInput").ap()
    bd = nc.dram_tensor("bd", [D], F32, kind="ExternalInput").ap()
    ident_b = nc.dram_tensor("ident_b", [128, 128], BF16, kind="ExternalInput").ap()
    ident_f = nc.dram_tensor("ident_f", [128, 128], F32, kind="ExternalInput").ap()
    x_out = nc.dram_tensor("x_out", [S, D], F32, kind="ExternalOutput").ap()
    if DEBUG_A:
        dbg_xlnT = nc.dram_tensor("dbg_xlnT", [128, 32768], BF16, kind="ExternalOutput").ap()
        dbg_bcol = nc.dram_tensor("dbg_bcol", [128, 24], F32, kind="ExternalOutput").ap()
        dbg_a = nc.dram_tensor("dbg_a", [128, 4096], F32, kind="ExternalOutput").ap()
        dbg_xs = nc.dram_tensor("dbg_xs", [128, 4096], F32, kind="ExternalOutput").ap()
        dbg_h = nc.dram_tensor("dbg_h", [128, 4096], F32, kind="ExternalOutput").ap()
        dbg_wT = nc.dram_tensor("dbg_wT", [128, 24576], BF16, kind="ExternalOutput").ap()

    fconsts = nc.dram_tensor("fconsts", [1, 64], F32, kind="ExternalInput").ap()
    ctx = Ctx(nc)
    with ctx.es:
        alloc_shared(ctx)
        alloc_common(ctx)
        cp = ctx.sb["cpool"]
        for i, v in enumerate(FCONSTS):
            nc.const_aps.register(F32, float(v), cp[:, i:i + 1])
        with nc.Block() as block:
            @block.sync
            def _(sync):
                fc_rep = bass.AP(fconsts.tensor, 0, [[0, 128], [1, len(FCONSTS)]])
                ctx.inc(sync.dma_start(out=cp[:, 0:len(FCONSTS)], in_=fc_rep), "ld", 16)

            emit_block_A(ctx, block, x_in, x_out,
                         {"Wg": Wg, "Wv": Wv, "Wd": Wd},
                         {"bg": bg, "bv": bv, "bd": bd},
                         {"ident_b": ident_b, "ident_f": ident_f})

            @block.gpsimd
            def _(gpsimd):
                ctx.wait(gpsimd, "st")
                ctx.wait(gpsimd, "ve")
                ctx.wait(gpsimd, "se")
                if DEBUG_A:
                    ctx.inc(gpsimd.dma_start(out=dbg_xlnT, in_=ctx.sb["big0"][:]), "st", 16)
                    ctx.inc(gpsimd.dma_start(out=dbg_bcol, in_=ctx.sb["bcol"][:]), "st", 16)
                    ctx.inc(gpsimd.dma_start(out=dbg_a, in_=ctx.sb["afull"][:]), "st", 16)
                    ctx.inc(gpsimd.dma_start(out=dbg_xs, in_=ctx.sb["xsfull"][:]), "st", 16)
                    ctx.inc(gpsimd.dma_start(out=dbg_h, in_=ctx.sb["hfull"][:]), "st", 16)
                    ctx.inc(gpsimd.dma_start(out=dbg_wT, in_=ctx.sb["wT"][:]), "st", 16)
                    ctx.wait(gpsimd, "st")
    return nc



def emit_ln_stats(ctx, block, xv, sqbuf, stat, sc, width, ld_wait=None, extra=None,
                  ve_wait=None):
    D_ = width

    @block.vector
    def _(vector):
        if ld_wait is not None:
            ctx.wait(vector, "ld", ld_wait)
        ctx.inc(vector.reduce_sum(stat[:, sc:sc + 1], xv, AX.X), "ve", 1)

    @block.scalar
    def _(scalar):
        if ld_wait is not None:
            ctx.wait(scalar, "ld", ld_wait)
        if ve_wait is not None:
            ctx.wait(scalar, "ve", ve_wait)
        if extra is not None:
            extra(scalar)
        ctx.inc(scalar.activation(sqbuf, xv, AF.Square, bias=0.0, scale=1.0,
                                  accum_out=stat[:, sc + 1:sc + 2]), "se", 1)

    se_sq = ctx.cnt["se"]

    @block.vector
    def _(vector):
        ctx.wait(vector, "se", se_sq)
        ctx.wait(vector, "ve")
        ctx.inc(vector.tensor_scalar_mul(stat[:, sc:sc + 1], stat[:, sc:sc + 1],
                                         1.0 / D_), "ve", 1)
        ctx.wait(vector, "ve")
        ctx.inc(vector.scalar_tensor_tensor(stat[:, sc + 3:sc + 4], stat[:, sc:sc + 1],
                                            1.0, stat[:, sc:sc + 1],
                                            ALU.bypass, ALU.mult), "ve", 1)
        ctx.wait(vector, "ve")
        ctx.inc(vector.scalar_tensor_tensor(stat[:, sc + 2:sc + 3], stat[:, sc + 1:sc + 2],
                                            1.0 / D_, stat[:, sc + 3:sc + 4],
                                            ALU.mult, ALU.subtract), "ve", 1)

    ve_var = ctx.cnt["ve"]

    @block.scalar
    def _(scalar):
        ctx.wait(scalar, "ve", ve_var)
        ctx.inc(scalar.activation(stat[:, sc + 2:sc + 3], stat[:, sc + 2:sc + 3],
                                  AF.Ln, bias=1e-5, scale=1.0), "se", 1)
        ctx.wait(scalar, "se")
        ctx.inc(scalar.activation(stat[:, sc + 2:sc + 3], stat[:, sc + 2:sc + 3],
                                  AF.Exp, bias=0.0, scale=-0.5), "se", 1)
    return ctx.cnt["se"]


def emit_fp4_quant(ctx, block, xv, out_b, W, bufs):
    G = W // 16
    qa, qb, qc, sgn, scb, rcb, blk = bufs

    @block.scalar
    def _(scalar):
        ctx.inc(scalar.activation(qa[:, 0:W], xv, AF.Abs, bias=0.0, scale=1.0), "se", 1)
        ctx.wait(scalar, "se")
        ctx.inc(scalar.activation(sgn[:, 0:W], xv, AF.Sign, bias=0.0, scale=1.0), "se", 1)

    se_abs = ctx.cnt["se"]

    @block.vector
    def _(vector):
        ctx.wait(vector, "se", se_abs)
        ctx.inc(vector.tensor_reduce(
            blk[:, 0:G], qa[:, 0:W].rearrange("p (g b) -> p g b", b=16),
            AX.X, ALU.max), "ve", 1)
        ctx.wait(vector, "ve")
        ctx.inc(vector.tensor_scalar(blk[:, G:2 * G], blk[:, 0:G], 1.0 / 6.0, 1e-12,
                                     ALU.mult, ALU.max), "ve", 1)
        ctx.wait(vector, "ve")
        ctx.inc(vector.reciprocal(blk[:, 0:G], blk[:, G:2 * G]), "ve", 1)
        ctx.wait(vector, "ve")
        scv = scb[:, 0:W].rearrange("p (g b) -> p g b", b=16)
        rcv = rcb[:, 0:W].rearrange("p (g b) -> p g b", b=16)
        s3 = blk[:, G:2 * G].rearrange("p (g b) -> p g b", b=1)
        r3 = blk[:, 0:G].rearrange("p (g b) -> p g b", b=1)
        for r in range(16):
            vector.tensor_copy(scv[:, :, r:r + 1], s3)
            i = vector.tensor_copy(rcv[:, :, r:r + 1], r3)
        ctx.inc(i, "ve", 1)
        ctx.wait(vector, "ve")
        ctx.inc(vector.tensor_mul(qa[:, 0:W], qa[:, 0:W], rcb[:, 0:W]), "ve", 1)
        ctx.wait(vector, "ve")
        vector.tensor_single_scalar(qb[:, 0:W], qa[:, 0:W], 0.25, ALU.is_gt)
        vector.tensor_single_scalar(qc[:, 0:W], qa[:, 0:W], 2.5, ALU.is_gt)
        vector.scalar_tensor_tensor(qb[:, 0:W], qa[:, 0:W], 0.75, qb[:, 0:W],
                                    ALU.is_gt, ALU.add)
        vector.scalar_tensor_tensor(qc[:, 0:W], qa[:, 0:W], 3.5, qc[:, 0:W],
                                    ALU.is_gt, ALU.add)
        vector.scalar_tensor_tensor(qb[:, 0:W], qa[:, 0:W], 1.25, qb[:, 0:W],
                                    ALU.is_gt, ALU.add)
        vector.scalar_tensor_tensor(qc[:, 0:W], qa[:, 0:W], 5.0, qc[:, 0:W],
                                    ALU.is_gt, ALU.add)
        vector.scalar_tensor_tensor(qb[:, 0:W], qa[:, 0:W], 1.75, qb[:, 0:W],
                                    ALU.is_gt, ALU.add)
        ctx.inc(vector.scalar_tensor_tensor(qc[:, 0:W], qa[:, 0:W], 5.0, qc[:, 0:W],
                                            ALU.is_gt, ALU.add), "ve", 1)
        ctx.wait(vector, "ve")
        ctx.inc(vector.scalar_tensor_tensor(qa[:, 0:W], qb[:, 0:W], 0.5, qc[:, 0:W],
                                            ALU.mult, ALU.add), "ve", 1)
        ctx.wait(vector, "ve")
        ctx.inc(vector.tensor_mul(qa[:, 0:W], qa[:, 0:W], sgn[:, 0:W]), "ve", 1)
        ctx.wait(vector, "ve")
        ctx.inc(vector.tensor_mul(out_b, qa[:, 0:W], scb[:, 0:W]), "ve", 1)
    return ctx.cnt["ve"]



def emit_transpose_evac(ctx, block, srcbuf, ntr, dst_fn, idb, pb0, pb1, ve_wait=None):
    for g0 in range(0, ntr, 8):
        gtiles = list(range(g0, min(g0 + 8, ntr)))

        @block.tensor
        def _(tensor, gtiles=gtiles, g0=g0, ve_wait=ve_wait):
            if g0 == 0 and ve_wait is not None:
                ctx.wait(tensor, "ve", ve_wait)
            ctx.wait(tensor, "se", ctx.at("trev_ev"))
            for k in gtiles:
                slot = (k % 8) // 2
                pbank = pb0 if (k % 2 == 0) else pb1
                ii = tensor.transpose(pbank[:, slot * 128:slot * 128 + 128],
                                      srcbuf[:, k * 128:(k + 1) * 128], idb[:])
            ctx.inc(ii, "pe", 1)
            ctx.rec("trev_tr", "pe")

        pe_now = ctx.cnt["pe"]

        @block.scalar
        def _(scalar, gtiles=gtiles, pe_now=pe_now):
            ctx.wait(scalar, "pe", pe_now)
            for k in gtiles:
                slot = (k % 8) // 2
                pbank = pb0 if (k % 2 == 0) else pb1
                ii = dst_fn(scalar, k, pbank[:, slot * 128:slot * 128 + 128])
            ctx.inc(ii, "se", 1)
            ctx.rec("trev_ev", "se")
    return ctx.cnt["se"]


IC = 2816
NIT = IC // 128

def emit_block_D(ctx, block, x_in, x_out, w_aps, consts):
    sb, ps = ctx.sb, ctx.ps
    wgqT, wuqT, wdqT = sb["wgqT"], sb["wuqT"], sb["wdqT"]
    xt = sb["xt"]; stat = sb["stat"]; idb = sb["idb"]
    qa, qb, qc = sb["qa"], sb["qb"], sb["qc"]
    sgn, scb, rcb, blk = sb["sgn"], sb["scb"], sb["rcb"], sb["blk"]
    hbuf = sb["hbuf"]; hq = sb["hq"]; hqT = sb["hqT"]
    xq = sb["hq"]
    xqT = sb["hqT"]
    sg = sb["sg"]
    obuf = sb["hbuf"]
    p0, p1, p2, p3 = ps["p0"], ps["p1"], ps["p2"], ps["p3"]
    pb0, pb1 = ps["pb0"], ps["pb1"]

    @block.sync
    def _(sync):
        ctx.inc(sync.dma_start(out=idb[:], in_=consts["ident_b"]), "ld", 16)

    jobs = [("wg", w_aps["wg"], NIT, D, wgqT, IC),
            ("wu", w_aps["wu"], NIT, D, wuqT, IC),
            ("wd", w_aps["wd"], NCD, IC, wdqT, D)]
    for name, wap, nrt, W, dstT, dcols in jobs:
        ntr = W // 128
        for r in range(nrt):
            @block.sync
            def _(sync, wap=wap, r=r, W=W):
                ctx.wait(sync, "ve", ctx.at("d0_q"))
                ctx.inc(sync.dma_start(out=hbuf[:, 0:W],
                                       in_=wap[r * 128:(r + 1) * 128, 0:W]), "ld", 16)

            ld_now = ctx.cnt["ld"]

            @block.vector
            def _(vector, ld_now=ld_now):
                ctx.wait(vector, "ld", ld_now)
                ctx.wait(vector, "pe", ctx.at("trev_tr"))
                vector.engine_nop()

            @block.scalar
            def _(scalar, ld_now=ld_now):
                ctx.wait(scalar, "ld", ld_now)

            emit_fp4_quant(ctx, block, hbuf[:, 0:W], hq[:, 0:W], W,
                           (qa, qb, qc, sgn, scb, rcb, blk))
            ve_q = ctx.cnt["ve"]
            ctx.rec("d0_q", "ve")

            def dst_fn(scalar, k, psrc, dstT=dstT, dcols=dcols, r=r):
                return scalar.copy(dstT[:, k * dcols + r * 128:
                                        k * dcols + (r + 1) * 128], psrc)
            emit_transpose_evac(ctx, block, hq, ntr, dst_fn, idb, pb0, pb1,
                                ve_wait=ve_q)
            ctx.rec("d0_tr", "pe")

    ICH = [(0, 512), (512, 512), (1024, 512), (1536, 512), (2048, 512), (2560, 256)]
    for j in range(NT):
        @block.sync
        def _(sync, j=j):
            ctx.wait(sync, "st", ctx.at("d1_store"))
            ctx.inc(sync.dma_start(out=xt[:, 0:1024],
                                   in_=x_in[j * 128:(j + 1) * 128, :]), "ld", 16)

        ld_now = ctx.cnt["ld"]
        emit_ln_stats(ctx, block, xt[:, 0:1024], qa[:, 0:1024], stat, 0, D,
                      ld_wait=ld_now)
        se_rs = ctx.cnt["se"]

        @block.vector
        def _(vector, se_rs=se_rs):
            ctx.wait(vector, "se", se_rs)
            ctx.inc(vector.tensor_scalar(hbuf[:, 0:1024], xt[:, 0:1024],
                                         stat[:, 0:1], stat[:, 2:3],
                                         ALU.subtract, ALU.mult), "ve", 1)
            ctx.wait(vector, "ve")

        emit_fp4_quant(ctx, block, hbuf[:, 0:1024], xq[:, 0:1024], D,
                       (qa, qb, qc, sgn, scb, rcb, blk))
        ve_q = ctx.cnt["ve"]

        def xq_dst(scalar, k, psrc):
            return scalar.copy(xqT[:, k * 128:(k + 1) * 128], psrc)
        emit_transpose_evac(ctx, block, xq, 8, xq_dst, idb, pb0, pb1, ve_wait=ve_q)
        se_xqT = ctx.cnt["se"]

        for ci, (i0, ilen) in enumerate(ICH):
            @block.tensor
            def _(tensor, i0=i0, ilen=ilen, ci=ci, se_xqT=se_xqT):
                if ci == 0:
                    ctx.wait(tensor, "se", se_xqT)
                ctx.wait(tensor, "ve", ctx.at("d1_gu"))
                for k in range(8):
                    tensor.matmul(p0[:, 0:ilen], xqT[:, k * 128:(k + 1) * 128],
                                  wgqT[:, k * IC + i0:k * IC + i0 + ilen],
                                  start=(k == 0), stop=(k == 7))
                for k in range(8):
                    ii = tensor.matmul(p1[:, 0:ilen], xqT[:, k * 128:(k + 1) * 128],
                                       wuqT[:, k * IC + i0:k * IC + i0 + ilen],
                                       start=(k == 0), stop=(k == 7))
                ctx.inc(ii, "pe", 1)

            pe_now = ctx.cnt["pe"]

            @block.scalar
            def _(scalar, i0=i0, ilen=ilen, pe_now=pe_now):
                ctx.wait(scalar, "pe", pe_now)
                ctx.wait(scalar, "ve", ctx.at("d1_h"))
                ctx.inc(scalar.activation(sg[:, 0:ilen], p0[:, 0:ilen], AF.Silu,
                                          bias=0.0, scale=1.0), "se", 1)

            se_now = ctx.cnt["se"]

            @block.vector
            def _(vector, i0=i0, ilen=ilen, se_now=se_now):
                ctx.wait(vector, "se", se_now)
                ctx.inc(vector.tensor_mul(hbuf[:, i0:i0 + ilen], sg[:, 0:ilen],
                                          p1[:, 0:ilen]), "ve", 1)
                ctx.rec("d1_gu", "ve")
                ctx.rec("d1_h", "ve")

        @block.vector
        def _(vector):
            ctx.wait(vector, "ve")

        emit_fp4_quant(ctx, block, hbuf[:, 0:IC], hq[:, 0:IC], IC,
                       (qa, qb, qc, sgn, scb, rcb, blk))
        ve_hq = ctx.cnt["ve"]

        def hq_dst(scalar, k, psrc):
            return scalar.copy(hqT[:, k * 128:(k + 1) * 128], psrc)
        emit_transpose_evac(ctx, block, hq, NIT, hq_dst, idb, pb0, pb1, ve_wait=ve_hq)
        se_hqT = ctx.cnt["se"]

        for oc in range(2):
            @block.tensor
            def _(tensor, oc=oc, se_hqT=se_hqT):
                if oc == 0:
                    ctx.wait(tensor, "se", se_hqT)
                ctx.wait(tensor, "ve", ctx.at("d1_res"))
                for m in range(NIT):
                    ii = tensor.matmul(p2[:], hqT[:, m * 128:(m + 1) * 128],
                                       wdqT[:, m * D + oc * 512:m * D + oc * 512 + 512],
                                       start=(m == 0), stop=(m == NIT - 1))
                ctx.inc(ii, "pe", 1)

            pe_now = ctx.cnt["pe"]

            @block.vector
            def _(vector, oc=oc, pe_now=pe_now):
                ctx.wait(vector, "pe", pe_now)
                ctx.wait(vector, "st", ctx.at("d1_store"))
                ctx.inc(vector.tensor_add(obuf[:, oc * 512:(oc + 1) * 512], p2[:],
                                          xt[:, oc * 512:(oc + 1) * 512]), "ve", 1)
                ctx.rec("d1_res", "ve")

            ve_now = ctx.cnt["ve"]

        @block.sync
        def _(sync, j=j):
            ctx.wait(sync, "ve")
            ctx.inc(sync.dma_start(out=x_out[j * 128:(j + 1) * 128, :],
                                   in_=obuf[:, 0:1024]), "st", 16)
            ctx.rec("d1_store", "st")


def alloc_D(ctx):
    ctx.sbuf("wgqT", [128, 8 * IC], BF16)
    ctx.sbuf("wuqT", [128, 8 * IC], BF16)
    ctx.sbuf("wdqT", [128, NIT * D], BF16)
    ctx.sbuf("xt", [128, 1024], F32)
    ctx.sbuf("stat", [128, 8], F32)
    ctx.sbuf("idb", [128, 128], BF16)
    ctx.sbuf("qa", [128, IC], F32)
    ctx.sbuf("qb", [128, IC], BF16)
    ctx.sbuf("qc", [128, IC], BF16)
    ctx.sbuf("sgn", [128, IC], BF16)
    ctx.sbuf("scb", [128, IC], BF16)
    ctx.sbuf("rcb", [128, IC], F32)
    ctx.sbuf("blk", [128, 2 * (IC // 16)], F32)
    ctx.sbuf("hbuf", [128, IC], F32)
    ctx.sbuf("hq", [128, IC], BF16)
    ctx.sbuf("hqT", [128, NIT * 128], BF16)
    ctx.sbuf("sg", [128, 512], F32)


def build_D():
    nc = bass.Bass("TRN2", target_bir_lowering=False)
    x_in = nc.dram_tensor("x_in", [S, D], F32, kind="ExternalInput").ap()
    wg = nc.dram_tensor("wg", [IC, D], F32, kind="ExternalInput").ap()
    wu = nc.dram_tensor("wu", [IC, D], F32, kind="ExternalInput").ap()
    wd = nc.dram_tensor("wd", [D, IC], F32, kind="ExternalInput").ap()
    ident_b = nc.dram_tensor("ident_b", [128, 128], BF16, kind="ExternalInput").ap()
    fconsts = nc.dram_tensor("fconsts", [1, 64], F32, kind="ExternalInput").ap()
    x_out = nc.dram_tensor("x_out", [S, D], F32, kind="ExternalOutput").ap()

    ctx = Ctx(nc)
    with ctx.es:
        alloc_shared(ctx)
        alloc_D(ctx)
        cp = ctx.sb["cpool"]
        for i, v in enumerate(FCONSTS):
            nc.const_aps.register(F32, float(v), cp[:, i:i + 1])
        with nc.Block() as block:
            @block.sync
            def _(sync):
                fc_rep = bass.AP(fconsts.tensor, 0, [[0, 128], [1, len(FCONSTS)]])
                ctx.inc(sync.dma_start(out=cp[:, 0:len(FCONSTS)], in_=fc_rep), "ld", 16)

            emit_block_D(ctx, block, x_in, x_out,
                         {"wg": wg, "wu": wu, "wd": wd},
                         {"ident_b": ident_b})

            @block.gpsimd
            def _(gpsimd):
                ctx.wait(gpsimd, "st")
    return nc




def emit_block_C(ctx, block, x_in, x_out, w_aps, b_aps, consts, scratch):
    sb, ps = ctx.sb, ctx.ps
    xt = sb["xt"]; stat = sb["stat"]; idb = sb["idb"]; sq = sb["sq"]
    lnout = sb["lnout"]; c0st = sb["c0st"]; wtile = sb["wtile"]; wtb = sb["wtb"]
    wbT = sb["wbT"]; v_sb = sb["v_sb"]; xch = sb["xch"]; qkblk = sb["qkblk"]
    attT = sb["attT"]; stg = sb["stg"]; stgb = sb["stgb"]
    pmq = sb["pmq"]; ccol = sb["ccol"]; acolS = sb["acolS"]; alph = sb["alph"]
    bvb = sb["bvb"]; bpb = sb["bpb"]; bcol = sb["bcol"]
    onef = sb["onef"]; onefr = sb["onefr"]
    dn = stg_dn = None
    ystg = sb["ystg"]
    p0, p1, p2, p3 = ps["p0"], ps["p1"], ps["p2"], ps["p3"]
    pb0, pb1 = ps["pb0"], ps["pb1"]
    xlnT_d, qT_d, kT_d, yT_d, cscr = scratch
    dn = stg[0:1, 512:1024]
    avs = stg[0:64, 0:512]
    H = 16

    @block.sync
    def _(sync):
        ctx.inc(sync.dma_start(out=idb[:], in_=consts["ident_b"]), "ld", 16)
        for w, off in (("bq", 0), ("bk", 8)):
            for k in range(8):
                ctx.inc(sync.dma_start(out=bcol[:, off + k:off + k + 1],
                                       in_=b_aps[w][k * 128:(k + 1) * 128]), "ld", 16)
        ctx.inc(sync.dma_start(out=stg[0:1, 0:1024], in_=b_aps["bv"]), "ld", 16)

    ld_i = ctx.cnt["ld"]

    @block.vector
    def _(vector):
        ctx.wait(vector, "ld", ld_i)
        vector.memset(onef[:], 1.0)
        vector.memset(onefr[:], 1.0)
        ctx.inc(vector.memset(v_sb[:], 1.0), "ve", 1)

    ve_ones = ctx.cnt["ve"]

    @block.tensor
    def _(tensor):
        ctx.wait(tensor, "ve", ve_ones)
        tensor.matmul(p0[:], onefr[0:1, :], stg[0:1, 0:512], start=True, stop=True)
        ctx.inc(tensor.matmul(p1[:], onefr[0:1, :], stg[0:1, 512:1024],
                              start=True, stop=True), "pe", 1)

    pe_b = ctx.cnt["pe"]

    @block.scalar
    def _(scalar):
        ctx.wait(scalar, "pe", pe_b)
        scalar.copy(bvb[:, 0:512], p0[:])
        ctx.inc(scalar.copy(bvb[:, 512:1024], p1[:]), "se", 1)

    se_bv = ctx.cnt["se"]

    @block.sync
    def _(sync):
        ctx.wait(sync, "se", se_bv)
        ctx.inc(sync.dma_start(out=stg[0:1, 0:1024], in_=b_aps["bp"]), "ld", 16)

    ld_bp = ctx.cnt["ld"]

    @block.tensor
    def _(tensor):
        ctx.wait(tensor, "ld", ld_bp)
        tensor.matmul(p2[:], onefr[0:1, :], stg[0:1, 0:512], start=True, stop=True)
        ctx.inc(tensor.matmul(p3[:], onefr[0:1, :], stg[0:1, 512:1024],
                              start=True, stop=True), "pe", 1)

    pe_b2 = ctx.cnt["pe"]

    @block.scalar
    def _(scalar):
        ctx.wait(scalar, "pe", pe_b2)
        scalar.copy(bpb[:, 0:512], p2[:])
        ctx.inc(scalar.copy(bpb[:, 512:1024], p3[:]), "se", 1)

    ctx.rec("c_binit", "se")

    for j in range(NT):
        bi = j % 2
        sc = bi * 4
        xv = xt[:, bi * 1024:(bi + 1) * 1024]
        lv = lnout[:, bi * 1024:(bi + 1) * 1024]
        cv = c0st[:, bi * 1024:(bi + 1) * 1024]

        @block.sync
        def _(sync, j=j, xv=xv):
            ctx.wait(sync, "ve", ctx.at("c0_apply_%d" % (j - 2)))
            ctx.inc(sync.dma_start(out=xv, in_=x_in[j * 128:(j + 1) * 128, :]), "ld", 16)

        ld_now = ctx.cnt["ld"]
        emit_ln_stats(ctx, block, xv, sq[:, 0:1024], stat, sc, D, ld_wait=ld_now)
        se_rs = ctx.cnt["se"]

        @block.vector
        def _(vector, xv=xv, lv=lv, se_rs=se_rs, j=j, sc=sc):
            ctx.wait(vector, "se", se_rs)
            ctx.wait(vector, "pe", ctx.at("c0_tr_%d" % (j - 2)))
            ctx.inc(vector.tensor_scalar(lv, xv, stat[:, sc:sc + 1], stat[:, sc + 2:sc + 3],
                                         ALU.subtract, ALU.mult), "ve", 1)
            ctx.rec("c0_apply_%d" % j, "ve")

        ve_ln = ctx.cnt["ve"]

        @block.tensor
        def _(tensor, lv=lv, j=j, ve_ln=ve_ln):
            ctx.wait(tensor, "ve", ve_ln)
            ctx.wait(tensor, "se", ctx.at("c0_ev_%d" % (j - 2)))
            pbank = pb0 if (j % 2 == 0) else pb1
            for k in range(8):
                i = tensor.transpose(pbank[:, k * 128:(k + 1) * 128],
                                     lv[:, k * 128:(k + 1) * 128], idb[:])
            ctx.inc(i, "pe", 1)
            ctx.rec("c0_tr_%d" % j, "pe")

        pe_tr = ctx.cnt["pe"]

        @block.scalar
        def _(scalar, j=j, pe_tr=pe_tr, cv=cv):
            ctx.wait(scalar, "pe", pe_tr)
            ctx.wait(scalar, "st", ctx.at("c0_st_%d" % (j - 2)))
            pbank = pb0 if (j % 2 == 0) else pb1
            ctx.inc(scalar.copy(cv, pbank[:]), "se", 1)
            ctx.rec("c0_ev_%d" % j, "se")

        se_ev = ctx.cnt["se"]

        @block.sync
        def _(sync, j=j, se_ev=se_ev, cv=cv):
            ctx.wait(sync, "se", se_ev)
            for k in range(8):
                ctx.inc(sync.dma_start(
                    out=xlnT_d[k * 128:(k + 1) * 128, j * 128:(j + 1) * 128],
                    in_=cv[:, k * 128:(k + 1) * 128]), "st", 16)
            ctx.rec("c0_st_%d" % j, "st")

    ctx.rec("c0_allst", "st")

    for wi, wname in enumerate(("Wq", "Wk", "Wv", "Wp")):
        for n in range(64):
            i, k = n // 8, n % 8
            bi = n % 2
            wv = wtile[:, bi * 128:(bi + 1) * 128]
            wb = wtb[:, bi * 128:(bi + 1) * 128]

            @block.sync
            def _(sync, wname=wname, i=i, k=k, n=n, wv=wv):
                ctx.wait(sync, "se", ctx.at("c1_abs_%d" % (n - 2)))
                ctx.wait(sync, "ve", ctx.at("c1_cv_%d" % (n - 2)))
                ctx.inc(sync.dma_start(
                    out=wv, in_=w_aps[wname][i * 128:(i + 1) * 128,
                                             k * 128:(k + 1) * 128]), "ld", 16)

            ld_now = ctx.cnt["ld"]

            @block.scalar
            def _(scalar, wv=wv, n=n, wi=wi, ld_now=ld_now):
                ctx.wait(scalar, "ld", ld_now)
                ctx.inc(scalar.activation(sq[:, 0:128], wv, AF.Abs,
                                          bias=0.0, scale=1.0,
                                          accum_out=acolS[:, wi * 64 + n:wi * 64 + n + 1]),
                        "se", 1)
                ctx.rec("c1_abs_%d" % n, "se")

            se_abs = ctx.cnt["se"]

            @block.vector
            def _(vector, wv=wv, wb=wb, n=n, se_abs=se_abs):
                ctx.wait(vector, "se", se_abs)
                ctx.wait(vector, "pe", ctx.at("c1_tr_%d" % (n - 2)))
                ctx.inc(vector.tensor_copy(wb, wv), "ve", 1)
                ctx.rec("c1_cv_%d" % n, "ve")

            ve_cv = ctx.cnt["ve"]

            @block.tensor
            def _(tensor, wb=wb, n=n, ve_cv=ve_cv):
                ctx.wait(tensor, "ve", ve_cv)
                ctx.wait(tensor, "se", ctx.at("c1_sg_%d" % (n - 2)))
                pbank = pb0 if (n % 2 == 0) else pb1
                ctx.inc(tensor.transpose(pbank[:, 0:128], wb, idb[:]), "pe", 1)
                ctx.rec("c1_tr_%d" % n, "pe")

            pe_now = ctx.cnt["pe"]

            @block.scalar
            def _(scalar, wi=wi, i=i, k=k, n=n, pe_now=pe_now):
                ctx.wait(scalar, "pe", pe_now)
                pbank = pb0 if (n % 2 == 0) else pb1
                ctx.inc(scalar.activation(
                    wbT[:, wi * 8192 + k * 1024 + i * 128:
                        wi * 8192 + k * 1024 + (i + 1) * 128],
                    pbank[:, 0:128], AF.Sign, bias=0.0, scale=1.0), "se", 1)
                ctx.rec("c1_sg_%d" % n, "se")

    se_allabs = ctx.cnt["se"]

    @block.vector
    def _(vector, se_allabs=se_allabs):
        ctx.wait(vector, "se", se_allabs)
        ctx.inc(vector.tensor_reduce(pmq[:, 0:4],
                                     acolS[:, 0:256].rearrange("p (a b) -> p a b", b=64),
                                     AX.X, ALU.add), "ve", 1)

    ve_rs = ctx.cnt["ve"]

    @block.tensor
    def _(tensor, ve_rs=ve_rs):
        ctx.wait(tensor, "ve", ve_rs)
        ctx.inc(tensor.matmul(p0[0:1, 0:4], onef[:, 0:1], pmq[:, 0:4],
                              start=True, stop=True), "pe", 1)

    pe_s = ctx.cnt["pe"]

    @block.vector
    def _(vector, pe_s=pe_s):
        ctx.wait(vector, "pe", pe_s)
        ctx.inc(vector.tensor_scalar_mul(stg[0:1, 32:36], p0[0:1, 0:4],
                                         1.0 / (1024.0 * 1024.0)), "ve", 1)
        ctx.wait(vector, "ve")

    ve_al = ctx.cnt["ve"]

    @block.tensor
    def _(tensor, ve_al=ve_al):
        ctx.wait(tensor, "ve", ve_al)
        ctx.inc(tensor.matmul(p0[:, 8:12], onefr[0:1, :], stg[0:1, 32:36],
                              start=True, stop=True), "pe", 1)

    pe_al = ctx.cnt["pe"]

    @block.scalar
    def _(scalar, pe_al=pe_al):
        ctx.wait(scalar, "pe", pe_al)
        ctx.inc(scalar.copy(alph[:, 0:4], p0[:, 8:12]), "se", 1)

    ctx.rec("c1_done_se", "se")

    for tch in range(8):
        bi = tch % 2
        xv = xch[:, bi * 4096:(bi + 1) * 4096]

        @block.sync
        def _(sync, tch=tch, xv=xv):
            ctx.wait(sync, "st", ctx.at("c0_allst"))
            ctx.wait(sync, "pe", ctx.at("c2_done_%d" % (tch - 2)))
            for k in range(8):
                ctx.inc(sync.dma_start(
                    out=xv[:, k * 512:(k + 1) * 512],
                    in_=xlnT_d[k * 128:(k + 1) * 128, tch * 512:(tch + 1) * 512]),
                    "ld", 16)

        ld_now = ctx.cnt["ld"]

        for wi, dst in ((0, qT_d), (1, kT_d)):
            for ot in range(8):
                @block.tensor
                def _(tensor, wi=wi, ot=ot, xv=xv, ld_now=ld_now, tch=tch):
                    if wi == 0 and ot == 0:
                        ctx.wait(tensor, "ld", ld_now)
                        if tch == 0:
                            ctx.wait(tensor, "se", ctx.at("c1_done_se"))
                    ctx.wait(tensor, "ve", ctx.at("c2_ep"))
                    bank = p0 if (ot % 2 == 0) else p1
                    for k in range(8):
                        ii = tensor.matmul(bank[:],
                                           wbT[:, wi * 8192 + k * 1024 + ot * 128:
                                               wi * 8192 + k * 1024 + (ot + 1) * 128],
                                           xv[:, k * 512:(k + 1) * 512],
                                           start=(k == 0), stop=(k == 7))
                    ctx.inc(ii, "pe", 1)

                pe_now = ctx.cnt["pe"]

                @block.vector
                def _(vector, wi=wi, ot=ot, tch=tch, pe_now=pe_now):
                    ctx.wait(vector, "pe", pe_now)
                    ctx.wait(vector, "st", ctx.at("c2_st"))
                    bank = p0 if (ot % 2 == 0) else p1
                    ctx.inc(vector.tensor_scalar(stgb[:, 0:512], bank[:],
                                                 alph[:, wi:wi + 1],
                                                 bcol[:, wi * 8 + ot:wi * 8 + ot + 1],
                                                 ALU.mult, ALU.add), "ve", 1)
                    ctx.wait(vector, "ve")
                    ctx.inc(vector.tensor_reduce(
                        pmq[:, 8 + wi * 64 + ot * 8 + tch:9 + wi * 64 + ot * 8 + tch],
                        stgb[:, 0:512], AX.X, ALU.max,
                        apply_absolute_value=True), "ve", 1)
                    ctx.rec("c2_ep", "ve")

                ve_ep = ctx.cnt["ve"]

                @block.sync
                def _(sync, dst=dst, ot=ot, tch=tch, ve_ep=ve_ep):
                    ctx.wait(sync, "ve", ve_ep)
                    ctx.inc(sync.dma_start(
                        out=dst[ot * 128:(ot + 1) * 128, tch * 512:(tch + 1) * 512],
                        in_=stgb[:, 0:512]), "st", 16)
                    ctx.rec("c2_st", "st")

        for r in range(4):
            tt = tch * 4 + r
            for oc in range(2):
                @block.tensor
                def _(tensor, tt=tt, oc=oc, xv=xv, r=r):
                    ctx.wait(tensor, "ve", ctx.at("c2_vep"))
                    bank = p2 if (oc == 0) else p3
                    for k in range(8):
                        ii = tensor.matmul(bank[:],
                                           xv[:, k * 512 + r * 128:k * 512 + (r + 1) * 128],
                                           wbT[:, 2 * 8192 + k * 1024 + oc * 512:
                                               2 * 8192 + k * 1024 + (oc + 1) * 512],
                                           start=(k == 0), stop=(k == 7))
                    ctx.inc(ii, "pe", 1)

                pe_now = ctx.cnt["pe"]

                @block.vector
                def _(vector, tt=tt, oc=oc, pe_now=pe_now):
                    ctx.wait(vector, "pe", pe_now)
                    bank = p2 if (oc == 0) else p3
                    for hh in range(8):
                        h = oc * 8 + hh
                        ii = vector.scalar_tensor_tensor(
                            v_sb[:, tt * 1040 + 65 * h:tt * 1040 + 65 * h + 64],
                            bank[:, hh * 64:(hh + 1) * 64], alph[:, 2:3],
                            bvb[:, h * 64:(h + 1) * 64], ALU.mult, ALU.add)
                    ctx.inc(ii, "ve", 1)
                    ctx.rec("c2_vep", "ve")

        @block.tensor
        def _(tensor, tch=tch):
            ctx.wait(tensor, "ve", ctx.at("c2_vep"))
            tensor.nop()
            ctx.rec("c2_done_%d" % tch, "pe")

    ctx.rec("c2_allst", "st")

    ve_pm = ctx.cnt["ve"]

    @block.vector
    def _(vector, ve_pm=ve_pm):
        ctx.wait(vector, "ve", ve_pm)
        ctx.inc(vector.tensor_reduce(pmq[:, 4:6],
                                     pmq[:, 8:136].rearrange("p (a b) -> p a b", b=64),
                                     AX.X, ALU.max), "ve", 1)

    ve_red = ctx.cnt["ve"]

    @block.sync
    def _(sync, ve_red=ve_red):
        ctx.wait(sync, "ve", ve_red)
        ctx.inc(sync.dma_start(out=cscr[0:128, 0:2], in_=pmq[:, 4:6]), "st", 16)

    st_c = ctx.cnt["st"]

    @block.sync
    def _(sync, st_c=st_c):
        ctx.wait(sync, "st", st_c)
        cs = bass.AP(cscr.tensor, 0, [[0, 1], [1, 256]])
        ctx.inc(sync.dma_start(out=stg[0:1, 64:320], in_=cs), "ld", 16)

    ld_c = ctx.cnt["ld"]

    @block.vector
    def _(vector, ld_c=ld_c):
        ctx.wait(vector, "ld", ld_c)
        base = stg[0:1, 64:320]
        v2 = bass.AP(base.tensor, base.offset, [base.ap[0], [1, 2], [2, 128]])
        ctx.inc(vector.tensor_reduce(stg[0:1, 0:2], v2, AX.X, ALU.max), "ve", 1)
        ctx.wait(vector, "ve")
        ctx.inc(vector.scalar_tensor_tensor(stg[0:1, 2:3], stg[0:1, 0:1], 8.0,
                                            stg[0:1, 1:2], ALU.mult, ALU.mult), "ve", 1)
        ctx.wait(vector, "ve")
        ctx.inc(vector.tensor_scalar(stg[0:1, 3:4], stg[0:1, 2:3], -1.0, -2.0,
                                     ALU.mult, ALU.add), "ve", 1)

    ve_c = ctx.cnt["ve"]

    @block.tensor
    def _(tensor, ve_c=ve_c):
        ctx.wait(tensor, "ve", ve_c)
        ctx.inc(tensor.matmul(p0[:, 16:17], onefr[0:1, :], stg[0:1, 3:4],
                              start=True, stop=True), "pe", 1)

    pe_c = ctx.cnt["pe"]

    @block.scalar
    def _(scalar, pe_c=pe_c):
        ctx.wait(scalar, "pe", pe_c)
        ctx.inc(scalar.copy(ccol[:, 0:1], p0[:, 16:17]), "se", 1)

    ctx.rec("c2b_done", "se")

    for h in range(H):
        @block.sync
        def _(sync, h=h):
            ctx.wait(sync, "st", ctx.at("c2_allst"))
            ctx.wait(sync, "pe", ctx.at("c3_hdone_%d" % (h - 1)))
            ctx.inc(sync.dma_start(out=qkblk[0:64, 0:4096],
                                   in_=kT_d[h * 64:(h + 1) * 64, :]), "ld", 16)
            ctx.inc(sync.dma_start(out=qkblk[0:64, 4096:8192],
                                   in_=qT_d[h * 64:(h + 1) * 64, :]), "ld", 16)

        ld_h = ctx.cnt["ld"]

        for qc in range(8):
            pe_base = ctx.cnt["pe"]
            se_base = ctx.cnt["se"]
            ve_base = ctx.cnt["ve"]
            st_base = ctx.cnt["st"]

            def pe_of_S(kt, pe_base=pe_base):
                return pe_base + (1 if kt == 0 else 2 * kt)

            @block.tensor
            def _(tensor, h=h, qc=qc, ld_h=ld_h, pe_base=pe_base, se_base=se_base,
                  ve_base=ve_base):
                if qc == 0:
                    tensor.wait_ge(ctx.sems["ld"], ld_h)
                for kt in range(32):
                    bank = p0 if (kt % 2 == 0) else p1
                    if kt >= 2:
                        tensor.wait_ge(ctx.sems["se"], se_base + kt - 1)
                    ii = tensor.matmul(bank[:], qkblk[0:64, kt * 128:(kt + 1) * 128],
                                       qkblk[0:64, 4096 + qc * 512:4096 + (qc + 1) * 512],
                                       start=True, stop=True)
                    ctx.inc(ii, "pe", 1)
                    if kt > 0:
                        if kt == 1:
                            ctx.wait(tensor, "ve", ctx.at("c3_dn_ve"))
                        tensor.wait_ge(ctx.sems["se"], se_base + kt)
                        ii = tensor.matmul(p2[0:65, :],
                                           v_sb[:, (kt - 1) * 1040 + 65 * h:
                                                (kt - 1) * 1040 + 65 * h + 65],
                                           attT[:, ((kt - 1) % 2) * 512:
                                                ((kt - 1) % 2) * 512 + 512],
                                           start=(kt == 1), stop=False)
                        ctx.inc(ii, "pe", 1)
                tensor.wait_ge(ctx.sems["se"], se_base + 32)
                ii = tensor.matmul(p2[0:65, :],
                                   v_sb[:, 31 * 1040 + 65 * h:31 * 1040 + 65 * h + 65],
                                   attT[:, 512:1024],
                                   start=False, stop=True)
                ctx.inc(ii, "pe", 1)
                tensor.wait_ge(ctx.sems["ve"], ve_base + 2)
                ctx.inc(tensor.matmul(p3[0:64, :], onefr[0:1, 0:64], dn[0:1, :],
                                      start=True, stop=True), "pe", 1)

            pe_av = pe_base + 64
            pe_dn = ctx.cnt["pe"]

            @block.scalar
            def _(scalar, h=h, pe_base=pe_base, se_base=se_base, pe_av=pe_av):
                if h == 0:
                    ctx.wait(scalar, "se", ctx.at("c2b_done"))
                for kt in range(32):
                    bank = p0 if (kt % 2 == 0) else p1
                    scalar.wait_ge(ctx.sems["pe"], pe_of_S(kt))
                    ctx.inc(scalar.activation(
                        attT[:, (kt % 2) * 512:(kt % 2) * 512 + 512],
                        bank[:], AF.Exp, bias=ccol[:, 0:1], scale=1.0), "se", 1)
                scalar.wait_ge(ctx.sems["pe"], pe_av)
                ctx.wait(scalar, "ve", ctx.at("c3_ymul_ve"))
                ctx.inc(scalar.copy(avs[0:64, :], p2[0:64, :]), "se", 1)
                ctx.rec("c3_avs_se", "se")

            se_avs = ctx.cnt["se"]

            @block.vector
            def _(vector, pe_av=pe_av, se_avs=se_avs, pe_dn=pe_dn):
                vector.wait_ge(ctx.sems["pe"], pe_av)
                ctx.inc(vector.tensor_copy(dn[0:1, :], p2[64:65, :]), "ve", 1)
                ctx.wait(vector, "ve")
                ctx.inc(vector.reciprocal(dn[0:1, :], dn[0:1, :]), "ve", 1)
                ctx.rec("c3_dn_ve", "ve")
                vector.wait_ge(ctx.sems["se"], se_avs)
                vector.wait_ge(ctx.sems["pe"], pe_dn)
                ctx.wait(vector, "st", ctx.at("c3_yst"))
                ctx.inc(vector.tensor_mul(ystg[0:64, :], avs[0:64, :], p3[0:64, :]),
                        "ve", 1)
                ctx.rec("c3_ymul_ve", "ve")

            ve_y = ctx.cnt["ve"]

            @block.sync
            def _(sync, h=h, qc=qc, ve_y=ve_y):
                ctx.wait(sync, "ve", ve_y)
                ctx.inc(sync.dma_start(
                    out=yT_d[h * 64:(h + 1) * 64, qc * 512:(qc + 1) * 512],
                    in_=ystg[0:64, :]), "st", 16)
                ctx.rec("c3_yst", "st")

        ctx.rec("c3_hdone_%d" % h, "pe")

    ctx.rec("c3_allst", "st")

    for tt in range(NT):
        bi = tt % 2
        yv = xch[:, bi * 4096:bi * 4096 + 1024]
        xv = xt[:, bi * 1024:(bi + 1) * 1024]

        @block.sync
        def _(sync, tt=tt, yv=yv, xv=xv):
            ctx.wait(sync, "st", ctx.at("c3_allst"))
            ctx.wait(sync, "pe", ctx.at("c4_mm_%d" % (tt - 2)))
            ctx.wait(sync, "ve", ctx.at("c4_add_%d" % (tt - 2)))
            for k in range(8):
                ctx.inc(sync.dma_start(
                    out=yv[:, k * 128:(k + 1) * 128],
                    in_=yT_d[k * 128:(k + 1) * 128, tt * 128:(tt + 1) * 128]), "ld", 16)
            ctx.inc(sync.dma_start(out=xv, in_=x_in[tt * 128:(tt + 1) * 128, :]), "ld", 16)

        ld_now = ctx.cnt["ld"]

        for oc in range(2):
            @block.tensor
            def _(tensor, tt=tt, oc=oc, yv=yv, ld_now=ld_now):
                if oc == 0:
                    ctx.wait(tensor, "ld", ld_now)
                ctx.wait(tensor, "ve", ctx.at("c4_add_%d" % (tt - 1)))
                bank = p0 if oc == 0 else p1
                for k in range(8):
                    ii = tensor.matmul(bank[:],
                                       yv[:, k * 128:(k + 1) * 128],
                                       wbT[:, 3 * 8192 + k * 1024 + oc * 512:
                                           3 * 8192 + k * 1024 + (oc + 1) * 512],
                                       start=(k == 0), stop=(k == 7))
                ctx.inc(ii, "pe", 1)
                if oc == 1:
                    ctx.rec("c4_mm_%d" % tt, "pe")

        pe_now = ctx.cnt["pe"]

        @block.vector
        def _(vector, tt=tt, pe_now=pe_now, xv=xv):
            ctx.wait(vector, "pe", pe_now)
            ctx.wait(vector, "st", ctx.at("c4_store_%d" % (tt - 1)))
            bank = p0
            vector.scalar_tensor_tensor(stg[:, 0:512], p0[:],
                                        alph[:, 3:4], bpb[:, 0:512],
                                        ALU.mult, ALU.add)
            ctx.inc(vector.scalar_tensor_tensor(stg[:, 512:1024], p1[:],
                                                alph[:, 3:4], bpb[:, 512:1024],
                                                ALU.mult, ALU.add), "ve", 1)
            ctx.wait(vector, "ve")
            ctx.inc(vector.tensor_add(stg[:, 0:1024], stg[:, 0:1024], xv), "ve", 1)
            ctx.rec("c4_add_%d" % tt, "ve")

        ve_o = ctx.cnt["ve"]

        @block.sync
        def _(sync, tt=tt, ve_o=ve_o):
            ctx.wait(sync, "ve", ve_o)
            ctx.inc(sync.dma_start(out=x_out[tt * 128:(tt + 1) * 128, :],
                                   in_=stg[:, 0:1024]), "st", 16)
            ctx.rec("c4_store_%d" % tt, "st")


def alloc_C(ctx):
    ctx.sbuf("xt", [128, 2048], F32)
    ctx.sbuf("sq", [128, 1024], F32)
    ctx.sbuf("lnout", [128, 2048], BF16)
    ctx.sbuf("c0st", [128, 2048], BF16)
    ctx.sbuf("stat", [128, 8], F32)
    ctx.sbuf("idb", [128, 128], BF16)
    ctx.sbuf("wtile", [128, 256], F32)
    ctx.sbuf("wtb", [128, 256], BF16)
    ctx.sbuf("wbT", [128, 4 * 8192], BF16)
    ctx.sbuf("v_sb", [128, 32 * 1040], BF16)
    ctx.sbuf("xch", [128, 8192], BF16)
    ctx.sbuf("qkblk", [64, 8192], BF16)
    ctx.sbuf("attT", [128, 1024], BF16)
    ctx.sbuf("stg", [128, 1024], F32)
    ctx.sbuf("stgb", [128, 1024], BF16)
    ctx.sbuf("pmq", [128, 144], F32)
    ctx.sbuf("ccol", [128, 4], F32)
    ctx.sbuf("acolS", [128, 256], F32)
    ctx.sbuf("alph", [128, 8], F32)
    ctx.sbuf("bvb", [128, 1024], F32)
    ctx.sbuf("bpb", [128, 1024], F32)
    ctx.sbuf("bcol", [128, 24], F32)
    ctx.sbuf("onef", [128, 2], F32)
    ctx.sbuf("onefr", [1, 128], F32)
    ctx.sbuf("ystg", [64, 512], BF16)


def build_C():
    nc = bass.Bass("TRN2", target_bir_lowering=False)
    x_in = nc.dram_tensor("x_in", [S, D], F32, kind="ExternalInput").ap()
    aps = {}
    for w in ("Wq", "Wk", "Wv", "Wp"):
        aps[w] = nc.dram_tensor(w, [D, D], F32, kind="ExternalInput").ap()
    bps = {}
    for b in ("bq", "bk", "bv", "bp"):
        bps[b] = nc.dram_tensor(b, [D], F32, kind="ExternalInput").ap()
    ident_b = nc.dram_tensor("ident_b", [128, 128], BF16, kind="ExternalInput").ap()
    fconsts = nc.dram_tensor("fconsts", [1, 64], F32, kind="ExternalInput").ap()
    x_out = nc.dram_tensor("x_out", [S, D], F32, kind="ExternalOutput").ap()
    xlnT_d = nc.dram_tensor("xlnT_d", [D, S], BF16, kind="Internal").ap()
    qT_d = nc.dram_tensor("qT_d", [D, S], BF16, kind="Internal").ap()
    kT_d = nc.dram_tensor("kT_d", [D, S], BF16, kind="Internal").ap()
    yT_d = nc.dram_tensor("yT_d", [D, S], BF16, kind="Internal").ap()
    cscr = nc.dram_tensor("cscr", [128, 2], F32, kind="Internal").ap()

    ctx = Ctx(nc)
    with ctx.es:
        alloc_shared(ctx)
        alloc_C(ctx)
        cp = ctx.sb["cpool"]
        for i, v in enumerate(FCONSTS):
            nc.const_aps.register(F32, float(v), cp[:, i:i + 1])
        with nc.Block() as block:
            @block.sync
            def _(sync):
                fc_rep = bass.AP(fconsts.tensor, 0, [[0, 128], [1, len(FCONSTS)]])
                ctx.inc(sync.dma_start(out=cp[:, 0:len(FCONSTS)], in_=fc_rep), "ld", 16)

            emit_block_C(ctx, block, x_in, x_out, aps, bps,
                         {"ident_b": ident_b},
                         (xlnT_d, qT_d, kT_d, yT_d, cscr))

            @block.gpsimd
            def _(gpsimd):
                ctx.wait(gpsimd, "st")
    return nc



INNER = 512
HID2 = 5460
HID = 2730
UCH = [(0, 512), (512, 512), (1024, 512), (1536, 512), (2048, 512),
       (2560, 170), (2730, 342), (3072, 512), (3584, 512), (4096, 512),
       (4608, 512), (5120, 340)]


def emit_battn(ctx, block, sub, x_in, xi_in, xi_out, w_aps, consts, do_ln, yT_d):
    sb, ps = ctx.sb, ctx.ps
    xt = sb["xt"]; stat = sb["stat"]; idb = sb["idb"]; sq = sb["sq"]
    htile = sb["htile"]
    hbt = sb["hbt"]
    hT = sb["hT"]
    wqkv_sb = sb["wqkv_sb"]
    wo_sb = sb["wo_sb"]
    mcur = sb["mcur"]
    ropes = sb["ropes"]
    qsb = sb["qsb"]
    sqf = sb["sqf"]
    ssb = sb["ssb"]
    qn = sb["qn"]
    kn = sb["kn"]
    rot = sb["rot"]
    qrotT = sb["qrotT"]
    krotT = sb["krotT"]
    v_sb = sb["v_sb"]
    ystg = sb["ystg"]; ybuf = sb["ybuf"]
    satt = sb["satt"]
    attb = sb["attb"]
    attT = sb["attT"]
    den = sb["den"]
    obuf = sb["obuf"]
    onefr = sb["onefr"]
    qsrow = sb["qsrow"]
    p0, p1, p2, p3 = ps["p0"], ps["p1"], ps["p2"], ps["p3"]
    pb0, pb1 = ps["pb0"], ps["pb1"]

    @block.sync
    def _(sync):
        ctx.inc(sync.dma_start(out=idb[:], in_=consts["ident_b"]), "ld", 16)

    @block.gpsimd
    def _(gpsimd):
        for k in range(8):
            ctx.inc(gpsimd.dma_start(out=wqkv_sb[:, k * 1536:(k + 1) * 1536],
                                     in_=w_aps["wqkv"][k * 128:(k + 1) * 128, :]),
                    "gld", 16)
        for r in range(4):
            ctx.inc(gpsimd.dma_start(out=wo_sb[:, r * 1024:(r + 1) * 1024],
                                     in_=w_aps["wo"][r * 128:(r + 1) * 128, :]),
                    "gld", 16)
        qs_rep = bass.AP(w_aps["qs"].tensor, w_aps["qs"].offset,
                         [[0, 1], [0, 8], [1, 64]])
        ks_rep = bass.AP(w_aps["ks"].tensor, w_aps["ks"].offset,
                         [[0, 1], [0, 8], [1, 64]])
        ctx.inc(gpsimd.dma_start(out=qsrow[0:1, 0:512], in_=qs_rep), "gld", 16)
        ctx.inc(gpsimd.dma_start(out=qsrow[0:1, 512:1024], in_=ks_rep), "gld", 16)

    st_w = ctx.cnt["gld"]
    ld_w = ctx.cnt["ld"]

    @block.vector
    def _(vector):
        ctx.wait(vector, "ld", ld_w)
        ctx.inc(vector.memset(onefr[:], 1.0), "ve", 1)

    ve_one = ctx.cnt["ve"]

    @block.tensor
    def _(tensor):
        ctx.wait(tensor, "ve", ve_one)
        ctx.wait(tensor, "gld", st_w)
        tensor.matmul(p0[:], onefr[0:1, :], qsrow[0:1, 0:512], start=True, stop=True)
        ctx.inc(tensor.matmul(p1[:], onefr[0:1, :], qsrow[0:1, 512:1024],
                              start=True, stop=True), "pe", 1)

    pe_qs = ctx.cnt["pe"]

    @block.scalar
    def _(scalar):
        ctx.wait(scalar, "pe", pe_qs)
        scalar.copy(qsb[:, 0:512], p0[:])
        ctx.inc(scalar.copy(qsb[:, 512:1024], p1[:]), "se", 1)

    ctx.rec("ba_init", "se")

    for j in range(NT):
        @block.sync
        def _(sync, j=j):
            ctx.wait(sync, "ve", ctx.at("ba_xt_free_%d" % (j - 1)))
            if do_ln:
                ctx.inc(sync.dma_start(out=xt[:, 0:1024],
                                       in_=x_in[j * 128:(j + 1) * 128, :]), "ld", 16)
            else:
                ctx.inc(sync.dma_start(out=xt[:, 0:1024],
                                       in_=xi_in[j * 128:(j + 1) * 128, :]), "ld", 16)

        ld_now = ctx.cnt["ld"]

        if do_ln:
            emit_ln_stats(ctx, block, xt[:, 0:1024], sq[:, 0:1024], stat, 0, D,
                          ld_wait=ld_now)
            se_rs = ctx.cnt["se"]

            @block.vector
            def _(vector, se_rs=se_rs, j=j):
                ctx.wait(vector, "se", se_rs)
                ctx.wait(vector, "st", ctx.at("ba_xist_%d" % (j - 1)))
                ctx.inc(vector.tensor_scalar(xt[:, 1024:2048], xt[:, 0:1024],
                                             stat[:, 0:1], stat[:, 2:3],
                                             ALU.subtract, ALU.mult), "ve", 1)

            ve_xi = ctx.cnt["ve"]

            @block.sync
            def _(sync, j=j, ve_xi=ve_xi):
                ctx.wait(sync, "ve", ve_xi)
                ctx.inc(sync.dma_start(out=xi_in[j * 128:(j + 1) * 128, :],
                                       in_=xt[:, 1024:2048]), "st", 16)
                ctx.rec("ba_xist_%d" % j, "st")
            xiview = xt[:, 1024:2048]
        else:
            xiview = xt[:, 0:1024]

        @block.vector
        def _(vector, j=j, xiview=xiview):
            if do_ln:
                ctx.wait(vector, "ve")
            else:
                ctx.wait(vector, "ld", ctx.cnt["ld"])
            ctx.wait(vector, "pe", ctx.at("ba_htr_%d" % (j - 1)))
            if j == 0:
                vector.memset(sb["zcol"][:], 0.0)
            ctx.inc(vector.tensor_copy(hbt[:], xiview[:, 0:1024]), "ve", 1)
            ctx.rec("ba_xt_free_%d" % j, "ve")

        ve_h = ctx.cnt["ve"]

        @block.tensor
        def _(tensor, ve_h=ve_h, j=j):
            ctx.wait(tensor, "ve", ve_h)
            ctx.wait(tensor, "se", ctx.at("ba_pbev"))
            for k in range(8):
                ii = tensor.transpose(pb0[:, (k % 4) * 128:(k % 4) * 128 + 128] if k < 4
                                      else pb1[:, (k % 4) * 128:(k % 4) * 128 + 128],
                                      hbt[:, k * 128:(k + 1) * 128], idb[:])
            ctx.inc(ii, "pe", 1)
            ctx.rec("ba_htr_%d" % j, "pe")

        pe_tr = ctx.cnt["pe"]

        @block.scalar
        def _(scalar, pe_tr=pe_tr, j=j):
            ctx.wait(scalar, "pe", pe_tr)
            pc_r = sb["prevcol"][:, ((j + 1) % 2) * 4:((j + 1) % 2) * 4 + 4]
            pc_w = sb["prevcol"][:, (j % 2) * 4:(j % 2) * 4 + 4]
            for k in range(8):
                pbank = pb0 if k < 4 else pb1
                pslc = pbank[:, (k % 4) * 128:(k % 4) * 128 + 128]
                if k < 4:
                    ii = scalar.copy(hT[:, k * 128:(k + 1) * 128], pslc)
                else:
                    scalar.copy(hT[:, k * 128 + 1:(k + 1) * 128],
                                pbank[:, (k % 4) * 128:(k % 4) * 128 + 127])
                    if j == 0:
                        scalar.copy(hT[:, k * 128:k * 128 + 1], sb["zcol"][:])
                    else:
                        scalar.copy(hT[:, k * 128:k * 128 + 1],
                                    pc_r[:, k - 4:k - 3])
                    ii = scalar.copy(pc_w[:, k - 4:k - 3],
                                     pbank[:, (k % 4) * 128 + 127:(k % 4) * 128 + 128])
            ctx.inc(ii, "se", 1)
            ctx.rec("ba_pbev", "se")

        se_hT = ctx.cnt["se"]

        @block.sync
        def _(sync, j=j):
            ctx.wait(sync, "ve", ctx.at("ba_rope_used_%d" % (j - 1)))
            for ri, nm in enumerate(("rope_cq", "rope_sq", "rope_ck", "rope_sk")):
                ap = consts[nm]
                rep = bass.AP(ap.tensor, ap.offset + j * 128 * 64,
                              [[64, 128], [0, 8], [1, 64]])
                ctx.inc(sync.dma_start(out=ropes[:, ri * 512:(ri + 1) * 512], in_=rep),
                        "ld", 16)

        ld_rope = ctx.cnt["ld"]

        @block.tensor
        def _(tensor, j=j, se_hT=se_hT):
            if j == 0:
                ctx.wait(tensor, "se", ctx.at("ba_init"))
            ctx.wait(tensor, "se", se_hT)
            ctx.wait(tensor, "ve", ctx.at("ba_qk_done_%d" % (j - 1)))
            ctx.wait(tensor, "ve", ctx.at("ba_v_done_%d" % (j - 1)))
            for wi, bank in ((0, p0), (1, p1), (2, p2)):
                for k in range(8):
                    ii = tensor.matmul(bank[:], hT[:, k * 128:(k + 1) * 128],
                                       wqkv_sb[:, k * 1536 + wi * 512:
                                               k * 1536 + (wi + 1) * 512],
                                       start=(k == 0), stop=(k == 7))
            ctx.inc(ii, "pe", 1)

        pe_qkv = ctx.cnt["pe"]

        @block.vector
        def _(vector, j=j, pe_qkv=pe_qkv):
            ctx.wait(vector, "pe", pe_qkv)
            ctx.inc(vector.tensor_copy(v_sb[:, j * 512:(j + 1) * 512], p2[:]), "ve", 1)
            ctx.rec("ba_v_done_%d" % j, "ve")

        for qk in range(2):
            bank = p0 if qk == 0 else p1
            nbuf = qn if qk == 0 else kn
            dstT = qrotT if qk == 0 else krotT

            @block.scalar
            def _(scalar, bank=bank, pe_qkv=pe_qkv, qk=qk, j=j):
                ctx.wait(scalar, "pe", pe_qkv)
                if qk == 0:
                    ctx.wait(scalar, "ve", ctx.at("ba_sqf_used_%d_%d" % (j - 1, 1)))
                else:
                    ctx.wait(scalar, "ve", ctx.at("ba_sqf_used_%d_%d" % (j, 0)))
                ctx.inc(scalar.activation(sqf[:], bank[:], AF.Square,
                                          bias=0.0, scale=1.0), "se", 1)

            se_sq = ctx.cnt["se"]

            @block.vector
            def _(vector, qk=qk, se_sq=se_sq, j=j):
                ctx.wait(vector, "se", se_sq)
                sc = qk * 8
                ctx.inc(vector.tensor_reduce(ssb[:, sc:sc + 8],
                                             sqf[:].rearrange("p (a b) -> p a b", b=64),
                                             AX.X, ALU.add), "ve", 1)
                ctx.rec("ba_sqf_used_%d_%d" % (j, qk), "ve")

            ve_ss = ctx.cnt["ve"]

            @block.scalar
            def _(scalar, qk=qk, ve_ss=ve_ss):
                ctx.wait(scalar, "ve", ve_ss)
                sc = qk * 8
                ctx.inc(scalar.activation(ssb[:, 16 + sc:24 + sc], ssb[:, sc:sc + 8],
                                          AF.Sqrt, bias=0.0, scale=1.0), "se", 1)

            se_nr = ctx.cnt["se"]

            @block.vector
            def _(vector, qk=qk, se_nr=se_nr, bank=bank, nbuf=nbuf, ld_rope=ld_rope):
                ctx.wait(vector, "se", se_nr)
                sc = qk * 8
                ctx.inc(vector.tensor_scalar_max(ssb[:, 16 + sc:24 + sc],
                                                 ssb[:, 16 + sc:24 + sc], 1e-12),
                        "ve", 1)
                ctx.wait(vector, "ve")
                ctx.inc(vector.reciprocal(ssb[:, 16 + sc:24 + sc],
                                          ssb[:, 16 + sc:24 + sc]), "ve", 1)
                ctx.wait(vector, "ve")
                for h in range(8):
                    ii = vector.tensor_scalar_mul(nbuf[:, h * 64:(h + 1) * 64],
                                                  bank[:, h * 64:(h + 1) * 64],
                                                  ssb[:, 16 + sc + h:17 + sc + h])
                ctx.inc(ii, "ve", 1)
                ctx.rec("ba_qk_done_%d" % 0, "ve")
                ctx.wait(vector, "ve")
                ctx.inc(vector.tensor_mul(nbuf[:], nbuf[:], qsb[:, qk * 512:(qk + 1) * 512]),
                        "ve", 1)
                ctx.wait(vector, "ve")
                ctx.wait(vector, "ld", ld_rope)
                for h in range(8):
                    vector.tensor_scalar_mul(rot[:, h * 64:h * 64 + 32],
                                             nbuf[:, h * 64 + 32:h * 64 + 64], -1.0)
                    ii = vector.tensor_copy(rot[:, h * 64 + 32:h * 64 + 64],
                                            nbuf[:, h * 64:h * 64 + 32])
                ctx.inc(ii, "ve", 1)
                ctx.wait(vector, "ve")
                co = ropes[:, (qk * 2) * 512:(qk * 2 + 1) * 512]
                si = ropes[:, (qk * 2 + 1) * 512:(qk * 2 + 2) * 512]
                vector.tensor_mul(satt[:, 0:512], nbuf[:], co)
                ctx.inc(vector.tensor_mul(rot[:], rot[:], si), "ve", 1)
                ctx.wait(vector, "ve")
                ctx.wait(vector, "pe", ctx.at("ba_attbtr"))
                ctx.inc(vector.tensor_add(attb[:, 0:512], satt[:, 0:512], rot[:]),
                        "ve", 1)

            ve_rope = ctx.cnt["ve"]

            @block.tensor
            def _(tensor, qk=qk, j=j, ve_rope=ve_rope):
                ctx.wait(tensor, "ve", ve_rope)
                ctx.wait(tensor, "se", ctx.at("ba_pbev"))
                for r in range(4):
                    ii = tensor.transpose(pb0[:, r * 128:(r + 1) * 128],
                                          attb[:, r * 128:(r + 1) * 128], idb[:])
                ctx.inc(ii, "pe", 1)
                ctx.rec("ba_attbtr", "pe")

            pe_qtr = ctx.cnt["pe"]

            @block.scalar
            def _(scalar, qk=qk, j=j, pe_qtr=pe_qtr, dstT=dstT):
                ctx.wait(scalar, "pe", pe_qtr)
                for r in range(4):
                    ii = scalar.copy(dstT[:, r * 4096 + j * 128:r * 4096 + (j + 1) * 128],
                                     pb0[:, r * 128:(r + 1) * 128])
                ctx.inc(ii, "se", 1)
                ctx.rec("ba_pbev", "se")

        ctx.rec("ba_qk_done_%d" % j, "ve")
        ctx.rec("ba_rope_used_%d" % j, "ve")
        ctx.rec("ba_attb_pe", "pe")

    ctx.rec("ba_prep_done_se", "se")
    ctx.rec("ba_prep_done_ve", "ve")

    for tt in range(NT):
        w = tt // 4
        qp = tt % 4
        w0 = (w == 0)
        mb = tt % 2

        @block.sync
        def _(sync, tt=tt, qp=qp, w0=w0, mb=mb):
            ctx.wait(sync, "ve", ctx.at("ba_mask_used_%d" % (tt - 2)))
            mi = (4 + qp) if w0 else qp
            ctx.inc(sync.dma_start(out=mcur[:, mb * 1024:(mb + 1) * 1024],
                                   in_=consts["lmask_b"][mi * 128:(mi + 1) * 128, :]),
                    "ld", 16)

        ld_mask = ctx.cnt["ld"]
        for h in range(8):
            r, po = h // 2, (h % 2) * 64

            @block.tensor
            def _(tensor, tt=tt, h=h, r=r, po=po, w=w, w0=w0):
                if tt == 0 and h == 0:
                    ctx.wait(tensor, "se", ctx.at("ba_prep_done_se"))
                    ctx.wait(tensor, "ve", ctx.at("ba_prep_done_ve"))
                ctx.wait(tensor, "ve", ctx.at("ba_satt_ve"))
                if not w0:
                    tensor.matmul(p0[:], qrotT[po:po + 64, r * 4096 + tt * 128:
                                               r * 4096 + (tt + 1) * 128],
                                  krotT[po:po + 64, r * 4096 + (w - 1) * 512:
                                        r * 4096 + w * 512],
                                  start=True, stop=True)
                ii = tensor.matmul(p1[:], qrotT[po:po + 64, r * 4096 + tt * 128:
                                                r * 4096 + (tt + 1) * 128],
                                   krotT[po:po + 64, r * 4096 + w * 512:
                                         r * 4096 + (w + 1) * 512],
                                   start=True, stop=True)
                ctx.inc(ii, "pe", 1)

            pe_s = ctx.cnt["pe"]

            @block.vector
            def _(vector, qp=qp, w0=w0, pe_s=pe_s, mb=mb, ld_mask=ld_mask, tt=tt):
                ctx.wait(vector, "pe", pe_s)
                ctx.wait(vector, "ld", ld_mask)
                ctx.wait(vector, "se", ctx.at("ba_exp_se"))
                if not w0:
                    vector.tensor_add(satt[:, 0:512], p0[:],
                                      mcur[:, mb * 1024:mb * 1024 + 512])
                ii = vector.tensor_add(satt[:, 512:1024], p1[:],
                                       mcur[:, mb * 1024 + 512:(mb + 1) * 1024])
                ctx.inc(ii, "ve", 1)
                ctx.rec("ba_mask_used_%d" % tt, "ve")
                ctx.wait(vector, "ve")
                lo = 512 if w0 else 0
                ctx.inc(vector.tensor_reduce(den[:, 0:1], satt[:, lo:1024], AX.X,
                                             ALU.max, negate=True), "ve", 1)
                ctx.rec("ba_satt_ve", "ve")

            ve_mx = ctx.cnt["ve"]

            @block.scalar
            def _(scalar, w0=w0, ve_mx=ve_mx):
                ctx.wait(scalar, "ve", ve_mx)
                lo = 512 if w0 else 0
                ctx.inc(scalar.activation(satt[:, lo:1024], satt[:, lo:1024], AF.Exp,
                                          bias=den[:, 0:1], scale=1.0,
                                          accum_out=den[:, 1:2]), "se", 1)
                ctx.rec("ba_exp_se", "se")

            se_exp = ctx.cnt["se"]

            @block.vector
            def _(vector, w0=w0, se_exp=se_exp):
                ctx.wait(vector, "se", se_exp)
                ctx.wait(vector, "pe", ctx.at("ba_attb_pe"))
                ctx.inc(vector.reciprocal(den[:, 2:3], den[:, 1:2]), "ve", 1)
                ctx.wait(vector, "ve")
                if w0:
                    vector.memset(attb[:, 0:512], 0.0)
                lo = 512 if w0 else 0
                ctx.inc(vector.tensor_scalar_mul(attb[:, lo:1024], satt[:, lo:1024],
                                                 den[:, 2:3]), "ve", 1)

            ve_attb = ctx.cnt["ve"]

            @block.tensor
            def _(tensor, ve_attb=ve_attb):
                ctx.wait(tensor, "ve", ve_attb)
                ctx.wait(tensor, "se", ctx.at("ba_pbev"))
                ctx.wait(tensor, "se", ctx.at("ba_attT_se"))
                for s in range(8):
                    ii = tensor.transpose(pb0[:, (s % 4) * 128:(s % 4) * 128 + 128] if s < 4
                                          else pb1[:, (s % 4) * 128:(s % 4) * 128 + 128],
                                          attb[:, s * 128:(s + 1) * 128], idb[:])
                ctx.inc(ii, "pe", 1)
                ctx.rec("ba_attb_pe", "pe")

            pe_at = ctx.cnt["pe"]

            @block.scalar
            def _(scalar, pe_at=pe_at):
                ctx.wait(scalar, "pe", pe_at)
                ctx.wait(scalar, "pe", ctx.at("ba_av_pe"))
                for s in range(8):
                    pbank = pb0 if s < 4 else pb1
                    ii = scalar.copy(attT[:, s * 128:(s + 1) * 128],
                                     pbank[:, (s % 4) * 128:(s % 4) * 128 + 128])
                ctx.inc(ii, "se", 1)
                ctx.rec("ba_attT_se", "se")

            se_attT = ctx.cnt["se"]

            @block.tensor
            def _(tensor, tt=tt, h=h, w=w, w0=w0, se_attT=se_attT):
                ctx.wait(tensor, "se", se_attT)
                ctx.wait(tensor, "se", ctx.at("ba_yev_se"))
                srange = range(4, 8) if w0 else range(8)
                first = 4 if w0 else 0
                for s in srange:
                    kt = (w - 1) * 4 + s
                    ii = tensor.matmul(p3[0:64, 0:128],
                                       v_sb[:, kt * 512 + h * 64:kt * 512 + (h + 1) * 64],
                                       attT[:, s * 128:(s + 1) * 128],
                                       start=(s == first), stop=(s == 7))
                ctx.inc(ii, "pe", 1)
                ctx.rec("ba_av_pe", "pe")

            pe_av = ctx.cnt["pe"]

            @block.scalar
            def _(scalar, pe_av=pe_av):
                ctx.wait(scalar, "pe", pe_av)
                ctx.wait(scalar, "st", ctx.at("ba_yst_st"))
                ctx.inc(scalar.copy(ystg[0:64, 0:128], p3[0:64, 0:128]), "se", 1)
                ctx.rec("ba_yev_se", "se")

            se_y = ctx.cnt["se"]

            @block.sync
            def _(sync, tt=tt, h=h, se_y=se_y):
                ctx.wait(sync, "se", se_y)
                ctx.inc(sync.dma_start(
                    out=yT_d[h * 64:(h + 1) * 64, tt * 128:(tt + 1) * 128],
                    in_=ystg[0:64, 0:128]), "st", 16)
                ctx.rec("ba_yst_st", "st")

    ctx.rec("ba_attn_done_st", "st")

    for tt in range(NT):
        bb = tt % 2

        @block.sync
        def _(sync, tt=tt, bb=bb):
            ctx.wait(sync, "st", ctx.at("ba_attn_done_st"))
            ctx.wait(sync, "ve", ctx.at("ba_ores_%d" % (tt - 1)))
            ctx.wait(sync, "pe", ctx.at("ba_womm_%d" % (tt - 2)))
            for r in range(4):
                ctx.inc(sync.dma_start(
                    out=ybuf[:, bb * 512 + r * 128:bb * 512 + (r + 1) * 128],
                    in_=yT_d[r * 128:(r + 1) * 128, tt * 128:(tt + 1) * 128]), "ld", 16)
            ctx.inc(sync.dma_start(out=xt[:, bb * 1024:(bb + 1) * 1024],
                                   in_=xi_in[tt * 128:(tt + 1) * 128, :]), "ld", 16)

        ld_now = ctx.cnt["ld"]

        for oc in range(2):
            @block.tensor
            def _(tensor, tt=tt, oc=oc, ld_now=ld_now, bb=bb):
                if oc == 0:
                    ctx.wait(tensor, "ld", ld_now)
                ctx.wait(tensor, "ve", ctx.at("ba_ores_%d" % (tt - 1)))
                bank = p0 if oc == 0 else p1
                for r in range(4):
                    ii = tensor.matmul(bank[:],
                                       ybuf[:, bb * 512 + r * 128:bb * 512 + (r + 1) * 128],
                                       wo_sb[:, r * 1024 + oc * 512:r * 1024 + (oc + 1) * 512],
                                       start=(r == 0), stop=(r == 3))
                ctx.inc(ii, "pe", 1)
                if oc == 1:
                    ctx.rec("ba_womm_%d" % tt, "pe")

        pe_wo = ctx.cnt["pe"]

        @block.vector
        def _(vector, tt=tt, pe_wo=pe_wo, ld_now=ld_now, bb=bb):
            ctx.wait(vector, "pe", pe_wo)
            ctx.wait(vector, "ld", ld_now)
            ctx.wait(vector, "st", ctx.at("ba_ost_%d" % (tt - 1)))
            vector.tensor_add(obuf[:, 0:512], p0[:], xt[:, bb * 1024:bb * 1024 + 512])
            ctx.inc(vector.tensor_add(obuf[:, 512:1024], p1[:],
                                      xt[:, bb * 1024 + 512:(bb + 1) * 1024]),
                    "ve", 1)
            ctx.rec("ba_ores_%d" % tt, "ve")

        ve_o = ctx.cnt["ve"]

        @block.sync
        def _(sync, tt=tt, ve_o=ve_o):
            ctx.wait(sync, "ve", ve_o)
            ctx.inc(sync.dma_start(out=xi_out[tt * 128:(tt + 1) * 128, :],
                                   in_=obuf[:, 0:1024]), "st", 16)
            ctx.rec("ba_ost_%d" % tt, "st")


def alloc_battn(ctx):
    ctx.sbuf("xt", [128, 2048], F32)
    ctx.sbuf("sq", [128, 1024], F32)
    ctx.sbuf("stat", [128, 8], F32)
    ctx.sbuf("idb", [128, 128], BF16)
    ctx.sbuf("htile", [128, 1024], F32)
    ctx.sbuf("hbt", [128, 1024], BF16)
    ctx.sbuf("hT", [128, 1024], BF16)
    ctx.sbuf("wqkv_sb", [128, 8 * 1536], BF16)
    ctx.sbuf("wo_sb", [128, 4 * 1024], BF16)
    ctx.sbuf("mcur", [128, 2048], BF16)
    ctx.sbuf("ropes", [128, 4 * 512], F32)
    ctx.sbuf("qsb", [128, 1024], F32)
    ctx.sbuf("sqf", [128, 512], F32)
    ctx.sbuf("ssb", [128, 32], F32)
    ctx.sbuf("qn", [128, 512], F32)
    ctx.sbuf("kn", [128, 512], F32)
    ctx.sbuf("rot", [128, 512], F32)
    ctx.sbuf("qrotT", [128, 4 * 4096], BF16)
    ctx.sbuf("krotT", [128, 4 * 4096], BF16)
    ctx.sbuf("v_sb", [128, 32 * 512], BF16)
    ctx.sbuf("ystg", [64, 128], BF16)
    ctx.sbuf("ybuf", [128, 1024], BF16)
    ctx.sbuf("satt", [128, 1024], F32)
    ctx.sbuf("attb", [128, 1024], BF16)
    ctx.sbuf("attT", [128, 1024], BF16)
    ctx.sbuf("den", [128, 4], F32)
    ctx.sbuf("obuf", [128, 1024], F32)
    ctx.sbuf("onefr", [1, 128], F32)
    ctx.sbuf("qsrow", [1, 1024], F32)
    ctx.sbuf("prevcol", [128, 8], BF16)
    ctx.sbuf("zcol", [128, 1], BF16)


DEBUG_BA = False

def build_battn(sub, do_ln):
    nc = bass.Bass("TRN2", target_bir_lowering=False)
    x_in = nc.dram_tensor("x_in", [S, D], F32, kind="ExternalInput").ap()
    xi_in = nc.dram_tensor("xi_in", [S, D], F32,
                           kind="ExternalInput" if not do_ln else "ExternalOutput").ap()
    wqkv = nc.dram_tensor("wqkv", [D, 1536], F32, kind="ExternalInput").ap()
    qs = nc.dram_tensor("qs", [64], F32, kind="ExternalInput").ap()
    ks = nc.dram_tensor("ks", [64], F32, kind="ExternalInput").ap()
    wo = nc.dram_tensor("wo", [INNER, D], F32, kind="ExternalInput").ap()
    ident_b = nc.dram_tensor("ident_b", [128, 128], BF16, kind="ExternalInput").ap()
    lmask_b = nc.dram_tensor("lmask_b", [1024, 1024], BF16, kind="ExternalInput").ap()
    rope_cq = nc.dram_tensor("rope_cq", [S, 64], F32, kind="ExternalInput").ap()
    rope_sq = nc.dram_tensor("rope_sq", [S, 64], F32, kind="ExternalInput").ap()
    rope_ck = nc.dram_tensor("rope_ck", [S, 64], F32, kind="ExternalInput").ap()
    rope_sk = nc.dram_tensor("rope_sk", [S, 64], F32, kind="ExternalInput").ap()
    fconsts = nc.dram_tensor("fconsts", [1, 64], F32, kind="ExternalInput").ap()
    xi_out = nc.dram_tensor("xi_out", [S, D], F32, kind="ExternalOutput").ap()
    yT_d = nc.dram_tensor("yT_d", [INNER, S], BF16,
                          kind="ExternalOutput" if DEBUG_BA else "Internal").ap()
    if DEBUG_BA:
        dbg_qT = nc.dram_tensor("dbg_qT", [128, 4 * 4096], BF16, kind="ExternalOutput").ap()
        dbg_kT = nc.dram_tensor("dbg_kT", [128, 4 * 4096], BF16, kind="ExternalOutput").ap()
        dbg_v = nc.dram_tensor("dbg_v", [128, 32 * 512], BF16, kind="ExternalOutput").ap()
        dbg_htile = nc.dram_tensor("dbg_htile", [128, 1024], F32, kind="ExternalOutput").ap()
        dbg_hbt = nc.dram_tensor("dbg_hbt", [128, 1024], BF16, kind="ExternalOutput").ap()

    ctx = Ctx(nc)
    with ctx.es:
        alloc_shared(ctx)
        alloc_battn(ctx)
        cp = ctx.sb["cpool"]
        for i, v in enumerate(FCONSTS):
            nc.const_aps.register(F32, float(v), cp[:, i:i + 1])
        with nc.Block() as block:
            @block.sync
            def _(sync):
                fc_rep = bass.AP(fconsts.tensor, 0, [[0, 128], [1, len(FCONSTS)]])
                ctx.inc(sync.dma_start(out=cp[:, 0:len(FCONSTS)], in_=fc_rep), "ld", 16)

            emit_battn(ctx, block, sub, x_in, xi_in, xi_out,
                       {"wqkv": wqkv, "qs": qs, "ks": ks, "wo": wo},
                       {"ident_b": ident_b, "lmask_b": lmask_b,
                        "rope_cq": rope_cq, "rope_sq": rope_sq,
                        "rope_ck": rope_ck, "rope_sk": rope_sk,
                        "dbg_htile": dbg_htile if DEBUG_BA else None,
                        "dbg_hbt": dbg_hbt if DEBUG_BA else None}, do_ln, yT_d)

            @block.gpsimd
            def _(gpsimd):
                ctx.wait(gpsimd, "st")
                ctx.wait(gpsimd, "ve")
                ctx.wait(gpsimd, "se")
                if DEBUG_BA:
                    ctx.inc(gpsimd.dma_start(out=dbg_qT, in_=ctx.sb["qrotT"][:]), "st", 16)
                    ctx.inc(gpsimd.dma_start(out=dbg_kT, in_=ctx.sb["krotT"][:]), "st", 16)
                    ctx.inc(gpsimd.dma_start(out=dbg_v, in_=ctx.sb["v_sb"][:]), "st", 16)
                    ctx.wait(gpsimd, "st")
    return nc



def emit_bffn(ctx, block, sub, xi_in, xi_out, x1_in, w_aps, consts, final):
    sb, ps = ctx.sb, ctx.ps
    xt = sb["xt"]; stat = sb["stat"]; idb = sb["idb"]; sq = sb["sq"]
    htile = sb["htile"]; hbt = sb["hbt"]; hT = sb["hT"]
    w1_sb = sb["w1_sb"]; w2_sb = sb["w2_sb"]
    b1b = sb["b1b"]; lgb = sb["lgb"]; lbb = sb["lbb"]; b2b = sb["b2b"]
    u = sb["u"]; hmb = sb["hmb"]; hmT = sb["hmT"]
    obuf = sb["obuf"]; onefr = sb["onefr"]; brow = sb["brow"]
    p0, p1, p2, p3 = ps["p0"], ps["p1"], ps["p2"], ps["p3"]
    pb0, pb1 = ps["pb0"], ps["pb1"]

    @block.sync
    def _(sync):
        ctx.inc(sync.dma_start(out=idb[:], in_=consts["ident_b"]), "ld", 16)

    @block.gpsimd
    def _(gpsimd):
        for k in range(8):
            ctx.inc(gpsimd.dma_start(out=w1_sb[:, k * HID2:(k + 1) * HID2],
                                     in_=w_aps["w1"][k * 128:(k + 1) * 128, :]),
                    "gld", 16)
        for k in range(21):
            ctx.inc(gpsimd.dma_start(out=w2_sb[:, k * 1024:(k + 1) * 1024],
                                     in_=w_aps["w2"][k * 128:(k + 1) * 128, :]),
                    "gld", 16)
        ctx.inc(gpsimd.dma_start(out=w2_sb[0:42, 21 * 1024:22 * 1024],
                                 in_=w_aps["w2"][21 * 128:2730, :]), "gld", 16)

    st_w = ctx.cnt["gld"]

    @block.vector
    def _(vector):
        ctx.inc(vector.memset(onefr[:], 1.0), "ve", 1)

    ve_one = ctx.cnt["ve"]

    bjobs = []
    for (u0, ulen) in UCH:
        bjobs.append((w_aps["b1"], u0, ulen, b1b, u0))
    for oc in range(2):
        bjobs.append((w_aps["lg"], oc * 512, 512, lgb, oc * 512))
        bjobs.append((w_aps["lb"], oc * 512, 512, lbb, oc * 512))
        bjobs.append((w_aps["b2"], oc * 512, 512, b2b, oc * 512))
    for bi, (ap, off, ln_, dstb, doff) in enumerate(bjobs):
        @block.sync
        def _(sync, ap=ap, off=off, ln_=ln_, bi=bi):
            ctx.wait(sync, "pe", ctx.at("bf_bmm_%d" % (bi - 1)))
            ctx.inc(sync.dma_start(out=brow[0:1, 0:ln_], in_=ap[off:off + ln_]),
                    "ld", 16)

        ld_b = ctx.cnt["ld"]

        @block.tensor
        def _(tensor, bi=bi, ln_=ln_, ld_b=ld_b):
            ctx.wait(tensor, "ld", ld_b)
            if bi == 0:
                ctx.wait(tensor, "ve", ve_one)
            ctx.wait(tensor, "se", ctx.at("bf_bev_%d" % (bi - 1)))
            ctx.inc(tensor.matmul(p0[:, 0:ln_], onefr[0:1, :], brow[0:1, 0:ln_],
                                  start=True, stop=True), "pe", 1)
            ctx.rec("bf_bmm_%d" % bi, "pe")

        pe_b = ctx.cnt["pe"]

        @block.scalar
        def _(scalar, bi=bi, dstb=dstb, doff=doff, ln_=ln_, pe_b=pe_b):
            ctx.wait(scalar, "pe", pe_b)
            ctx.inc(scalar.copy(dstb[:, doff:doff + ln_], p0[:, 0:ln_]), "se", 1)
            ctx.rec("bf_bev_%d" % bi, "se")

    ctx.rec("bf_init_se", "se")

    for j in range(NT):
        @block.sync
        def _(sync, j=j):
            ctx.wait(sync, "ve", ctx.at("bf_xt_used_%d" % (j - 1)))
            ctx.inc(sync.dma_start(out=xt[:, 0:1024],
                                   in_=xi_in[j * 128:(j + 1) * 128, :]), "ld", 16)
            if j == 0:
                ctx.inc(sync.dma_start(out=htile[1:128, 512:1024],
                                       in_=xi_in[0:127, 512:1024]), "ld", 16)
            else:
                ctx.inc(sync.dma_start(out=htile[0:128, 512:1024],
                                       in_=xi_in[j * 128 - 1:j * 128 + 127, 512:1024]),
                        "ld", 16)
            if final:
                ctx.inc(sync.dma_start(out=xt[:, 1024:2048],
                                       in_=x1_in[j * 128:(j + 1) * 128, :]), "ld", 16)

        ld_now = ctx.cnt["ld"]

        @block.vector
        def _(vector, j=j, ld_now=ld_now):
            ctx.wait(vector, "ld", ld_now)
            ctx.wait(vector, "pe", ctx.at("bf_htr_%d" % (j - 1)))
            if j == 0:
                vector.memset(htile[0:1, 512:1024], 0.0)
            ctx.inc(vector.tensor_copy(htile[:, 0:512], xt[:, 0:512]), "ve", 1)

        ve_ht = ctx.cnt["ve"]
        emit_ln_stats(ctx, block, htile[:], sq[:, 0:1024], stat, 0, D, ve_wait=ve_ht)
        se_rs = ctx.cnt["se"]

        @block.vector
        def _(vector, j=j, se_rs=se_rs):
            ctx.wait(vector, "se", se_rs)
            ctx.inc(vector.tensor_scalar(sq[:, 0:1024], htile[:], stat[:, 0:1],
                                         stat[:, 2:3], ALU.subtract, ALU.mult),
                    "ve", 1)
            ctx.wait(vector, "ve")
            ctx.inc(vector.tensor_mul(sq[:, 0:1024], sq[:, 0:1024], lgb[:]), "ve", 1)
            ctx.wait(vector, "ve")
            ctx.inc(vector.tensor_add(hbt[:], sq[:, 0:1024], lbb[:]), "ve", 1)

        ve_h = ctx.cnt["ve"]

        @block.tensor
        def _(tensor, ve_h=ve_h, j=j):
            ctx.wait(tensor, "ve", ve_h)
            ctx.wait(tensor, "se", ctx.at("bf_pbev"))
            for k in range(8):
                ii = tensor.transpose(pb0[:, (k % 4) * 128:(k % 4) * 128 + 128] if k < 4
                                      else pb1[:, (k % 4) * 128:(k % 4) * 128 + 128],
                                      hbt[:, k * 128:(k + 1) * 128], idb[:])
            ctx.inc(ii, "pe", 1)
            ctx.rec("bf_htr_%d" % j, "pe")

        pe_tr = ctx.cnt["pe"]

        @block.scalar
        def _(scalar, pe_tr=pe_tr):
            ctx.wait(scalar, "pe", pe_tr)
            for k in range(8):
                pbank = pb0 if k < 4 else pb1
                ii = scalar.copy(hT[:, k * 128:(k + 1) * 128],
                                 pbank[:, (k % 4) * 128:(k % 4) * 128 + 128])
            ctx.inc(ii, "se", 1)
            ctx.rec("bf_pbev", "se")

        se_hT = ctx.cnt["se"]

        for ci, (u0, ulen) in enumerate(UCH):
            @block.tensor
            def _(tensor, ci=ci, u0=u0, ulen=ulen, se_hT=se_hT, j=j):
                if ci == 0:
                    ctx.wait(tensor, "se", se_hT)
                    if j == 0:
                        ctx.wait(tensor, "se", ctx.at("bf_init_se"))
                bank = p0 if (ci % 2 == 0) else p1
                ctx.wait(tensor, "ve", ctx.at("bf_uev_%s" % ("e" if ci % 2 == 0 else "o")))
                if ci == 0 and j == 0:
                    ctx.wait(tensor, "gld", st_w)
                for k in range(8):
                    ii = tensor.matmul(bank[:, 0:ulen],
                                       hT[:, k * 128:(k + 1) * 128],
                                       w1_sb[:, k * HID2 + u0:k * HID2 + u0 + ulen],
                                       start=(k == 0), stop=(k == 7))
                ctx.inc(ii, "pe", 1)

            pe_u = ctx.cnt["pe"]

            @block.vector
            def _(vector, ci=ci, u0=u0, ulen=ulen, pe_u=pe_u):
                ctx.wait(vector, "pe", pe_u)
                bank = p0 if (ci % 2 == 0) else p1
                ctx.inc(vector.scalar_tensor_tensor(u[:, u0:u0 + ulen], bank[:, 0:ulen],
                                                    1.0, b1b[:, u0:u0 + ulen],
                                                    ALU.bypass, ALU.add), "ve", 1)
                ctx.rec("bf_uev_%s" % ("e" if ci % 2 == 0 else "o"), "ve")

        ve_u = ctx.cnt["ve"]

        @block.scalar
        def _(scalar, ve_u=ve_u):
            ctx.wait(scalar, "ve", ve_u)
            ctx.inc(scalar.activation(u[:, HID:HID2], u[:, HID:HID2], AF.Gelu,
                                      bias=0.0, scale=1.0), "se", 1)

        se_g = ctx.cnt["se"]

        @block.vector
        def _(vector, se_g=se_g):
            ctx.wait(vector, "se", se_g)
            ctx.wait(vector, "pe", ctx.at("bf_hmtr_0"))
            ctx.inc(vector.tensor_mul(hmb[:], u[:, 0:HID], u[:, HID:HID2]), "ve", 1)

        ve_hm = ctx.cnt["ve"]

        for g0 in range(0, 22, 8):
            gt = list(range(g0, min(g0 + 8, 22)))

            @block.tensor
            def _(tensor, gt=gt, g0=g0, ve_hm=ve_hm, j=j):
                if g0 == 0:
                    ctx.wait(tensor, "ve", ve_hm)
                ctx.wait(tensor, "se", ctx.at("bf_pbev"))
                for k in gt:
                    w_ = min(128, HID - k * 128)
                    slot = (k % 8) // 2
                    pbank = pb0 if k % 2 == 0 else pb1
                    ii = tensor.transpose(pbank[0:w_, slot * 128:slot * 128 + 128],
                                          hmb[:, k * 128:k * 128 + w_], idb[:])
                ctx.inc(ii, "pe", 1)
                if g0 == 16:
                    ctx.rec("bf_hmtr_0", "pe")

            pe_g = ctx.cnt["pe"]

            @block.scalar
            def _(scalar, gt=gt, pe_g=pe_g):
                ctx.wait(scalar, "pe", pe_g)
                for k in gt:
                    w_ = min(128, HID - k * 128)
                    slot = (k % 8) // 2
                    pbank = pb0 if k % 2 == 0 else pb1
                    ii = scalar.copy(hmT[0:w_, k * 128:(k + 1) * 128],
                                     pbank[0:w_, slot * 128:slot * 128 + 128])
                ctx.inc(ii, "se", 1)
                ctx.rec("bf_pbev", "se")

        se_hmT = ctx.cnt["se"]

        for oc in range(2):
            @block.tensor
            def _(tensor, oc=oc, se_hmT=se_hmT):
                if oc == 0:
                    ctx.wait(tensor, "se", se_hmT)
                ctx.wait(tensor, "ve", ctx.at("bf_res_ve"))
                bank = p2 if oc == 0 else p3
                for k in range(22):
                    kw = min(128, HID - k * 128)
                    ii = tensor.matmul(bank[:],
                                       hmT[0:kw, k * 128:(k + 1) * 128],
                                       w2_sb[0:kw, k * 1024 + oc * 512:
                                             k * 1024 + (oc + 1) * 512],
                                       start=(k == 0), stop=(k == 21))
                ctx.inc(ii, "pe", 1)

        pe_w2 = ctx.cnt["pe"]

        @block.vector
        def _(vector, j=j, pe_w2=pe_w2, ld_now=ld_now):
            ctx.wait(vector, "pe", pe_w2)
            ctx.wait(vector, "st", ctx.at("bf_ost_%d" % (j - 1)))
            vector.scalar_tensor_tensor(obuf[:, 0:512], p2[:], 1.0, b2b[:, 0:512],
                                        ALU.bypass, ALU.add)
            ctx.inc(vector.scalar_tensor_tensor(obuf[:, 512:1024], p3[:], 1.0,
                                                b2b[:, 512:1024], ALU.bypass, ALU.add),
                    "ve", 1)
            ctx.wait(vector, "ve")
            ctx.inc(vector.tensor_add(obuf[:, 0:1024], obuf[:, 0:1024], xt[:, 0:1024]),
                    "ve", 1)
            if final:
                ctx.wait(vector, "ve")
                ctx.inc(vector.tensor_add(obuf[:, 0:1024], obuf[:, 0:1024],
                                          xt[:, 1024:2048]), "ve", 1)
            ctx.rec("bf_res_ve", "ve")
            ctx.rec("bf_xt_used_%d" % j, "ve")

        ve_o = ctx.cnt["ve"]

        @block.sync
        def _(sync, j=j, ve_o=ve_o):
            ctx.wait(sync, "ve", ve_o)
            ctx.inc(sync.dma_start(out=xi_out[j * 128:(j + 1) * 128, :],
                                   in_=obuf[:, 0:1024]), "st", 16)
            ctx.rec("bf_ost_%d" % j, "st")


def alloc_bffn(ctx):
    ctx.sbuf("xt", [128, 2048], F32)
    ctx.sbuf("sq", [128, 1024], F32)
    ctx.sbuf("stat", [128, 8], F32)
    ctx.sbuf("idb", [128, 128], BF16)
    ctx.sbuf("htile", [128, 1024], F32)
    ctx.sbuf("hbt", [128, 1024], BF16)
    ctx.sbuf("hT", [128, 1024], BF16)
    ctx.sbuf("w1_sb", [128, 8 * HID2], BF16)
    ctx.sbuf("w2_sb", [128, 22 * 1024], BF16)
    ctx.sbuf("b1b", [128, HID2], BF16)
    ctx.sbuf("lgb", [128, 1024], BF16)
    ctx.sbuf("lbb", [128, 1024], BF16)
    ctx.sbuf("b2b", [128, 1024], BF16)
    ctx.sbuf("u", [128, HID2], F32)
    ctx.sbuf("hmb", [128, HID], BF16)
    ctx.sbuf("hmT", [128, 22 * 128], BF16)
    ctx.sbuf("obuf", [128, 1024], F32)
    ctx.sbuf("onefr", [1, 128], F32)
    ctx.sbuf("brow", [1, 512], F32)


def build_bffn(sub, final):
    nc = bass.Bass("TRN2", target_bir_lowering=False)
    xi_in = nc.dram_tensor("xi_in", [S, D], F32, kind="ExternalInput").ap()
    x1_in = nc.dram_tensor("x1_in", [S, D], F32, kind="ExternalInput").ap()
    w1 = nc.dram_tensor("w1", [D, HID2], F32, kind="ExternalInput").ap()
    b1 = nc.dram_tensor("b1", [HID2], F32, kind="ExternalInput").ap()
    w2 = nc.dram_tensor("w2", [HID, D], F32, kind="ExternalInput").ap()
    b2 = nc.dram_tensor("b2", [D], F32, kind="ExternalInput").ap()
    lg = nc.dram_tensor("lg", [D], F32, kind="ExternalInput").ap()
    lb = nc.dram_tensor("lb", [D], F32, kind="ExternalInput").ap()
    ident_b = nc.dram_tensor("ident_b", [128, 128], BF16, kind="ExternalInput").ap()
    fconsts = nc.dram_tensor("fconsts", [1, 64], F32, kind="ExternalInput").ap()
    xi_out = nc.dram_tensor("xi_out", [S, D], F32, kind="ExternalOutput").ap()

    ctx = Ctx(nc)
    with ctx.es:
        alloc_shared(ctx)
        alloc_bffn(ctx)
        cp = ctx.sb["cpool"]
        for i, v in enumerate(FCONSTS):
            nc.const_aps.register(F32, float(v), cp[:, i:i + 1])
        with nc.Block() as block:
            @block.sync
            def _(sync):
                fc_rep = bass.AP(fconsts.tensor, 0, [[0, 128], [1, len(FCONSTS)]])
                ctx.inc(sync.dma_start(out=cp[:, 0:len(FCONSTS)], in_=fc_rep), "ld", 16)

            emit_bffn(ctx, block, sub, xi_in, xi_out, x1_in,
                      {"w1": w1, "b1": b1, "w2": w2, "b2": b2, "lg": lg, "lb": lb},
                      {"ident_b": ident_b}, final)

            @block.gpsimd
            def _(gpsimd):
                ctx.wait(gpsimd, "st")
    return nc


def emit_boundary(ctx, block, idx):
    tot = {s: ctx.cnt[s] for s in ("ld", "st", "pe", "ve", "se")}

    @block.sync
    def _(sync):
        for s in ("ld", "st", "pe", "ve", "se"):
            if tot[s] > 0:
                sync.wait_ge(ctx.sems[s], tot[s])
        if ctx.cnt["gld"] > 0:
            sync.wait_ge(ctx.sems["gld"], ctx.cnt["gld"])
        for s in ("ld", "st", "pe", "ve", "se"):
            sync.sem_clear(ctx.sems[s])
        sync.sem_inc(ctx.sems["gp"], 1)

    ctx.cnt["gp"] += 1
    gval = ctx.cnt["gp"]

    for engname in ("tensor", "vector", "scalar", "gpsimd"):
        def mk(eng, gval=gval):
            eng.wait_ge(ctx.sems["gp"], gval)
        getattr(block, engname)(mk)
    ctx.reset_counts()


def build_fused(start=0, upto=7, nofinal=False):
    nc = bass.Bass("TRN2", target_bir_lowering=False)
    g = {}
    def ein(name, shape, dtype=F32):
        g[name] = nc.dram_tensor(name, shape, dtype, kind="ExternalInput").ap()
        return g[name]

    def state(name, produced_in, consumed_in):
        prod = [s for s in produced_in if start <= s < upto]
        cons = [s for s in consumed_in if start <= s < upto]
        if not prod and not cons:
            kind = None
        elif prod and (max(consumed_in, default=-1) >= upto or not cons):
            kind = "ExternalOutput"
        elif not prod and cons:
            kind = "ExternalInput"
        else:
            kind = "Internal"
        if kind:
            g[name] = nc.dram_tensor(name, [S, D], F32, kind=kind).ap()

    if start <= 0 < upto or start <= 1 < upto:
        pass
    ein("x_in", [S, D])
    if start <= 0 < upto:
        for w in ("Wg", "Wv", "Wd"):
            ein(w, [D, D])
        for b in ("bg", "bv", "bd"):
            ein(b, [D])
    for sub in range(2):
        if start <= 1 + 2 * sub < upto:
            ein("wqkv%d" % sub, [D, 1536]); ein("qs%d" % sub, [64]); ein("ks%d" % sub, [64])
            ein("wo%d" % sub, [INNER, D])
        if start <= 2 + 2 * sub < upto:
            ein("w1_%d" % sub, [D, HID2]); ein("b1_%d" % sub, [HID2])
            ein("w2_%d" % sub, [HID, D]); ein("b2_%d" % sub, [D])
            ein("lg%d" % sub, [D]); ein("lb%d" % sub, [D])
    if start <= 5 < upto:
        for w in ("CWq", "CWk", "CWv", "CWp"):
            ein(w, [D, D])
        for b in ("cbq", "cbk", "cbv", "cbp"):
            ein(b, [D])
    if start <= 6 < upto:
        ein("wg", [IC, D]); ein("wu", [IC, D]); ein("wd", [D, IC])
    ein("ident_b", [128, 128], BF16); ein("ident_f", [128, 128])
    ein("fconsts", [1, 64]); ein("lmask_b", [1024, 1024], BF16)
    for r in ("rope_cq", "rope_sq", "rope_ck", "rope_sk"):
        ein(r, [S, 64])

    state("xs0", [0], [1, 4])
    state("xi_a", [1, 2], [1, 3])
    state("xi_b", [1, 3], [2, 4])
    state("x2_d", [4], [5])
    state("x3_d", [5], [6])
    if upto >= 7:
        x_out = nc.dram_tensor("x_out", [S, D], F32, kind="ExternalOutput").ap()
    else:
        x_out = None

    def intl(name, shape, dtype=F32):
        g[name] = nc.dram_tensor(name, shape, dtype, kind="Internal").ap()
    if start <= 5 < upto:
        intl("xlnT_d", [D, S], BF16); intl("qT_d", [D, S], BF16)
        intl("kT_d", [D, S], BF16); intl("yT_dC", [D, S], BF16)
        intl("cscr", [128, 2])
    if any(start <= s < upto for s in (1, 3)):
        intl("yT_dB", [INNER, S], BF16)

    ctx = Ctx(nc)
    with ctx.es:
        alloc_shared(ctx)
        cp = ctx.sb["cpool"]
        for i, v in enumerate(FCONSTS):
            nc.const_aps.register(F32, float(v), cp[:, i:i + 1])
        with nc.Block() as block:
            @block.sync
            def _(sync):
                fc_rep = bass.AP(g["fconsts"].tensor, 0, [[0, 128], [1, len(FCONSTS)]])
                ctx.inc(sync.dma_start(out=cp[:, 0:len(FCONSTS)], in_=fc_rep), "ld", 16)

            first = True
            bno = 0

            def boundary():
                nonlocal first, bno
                if not first:
                    pass
                bno += 1
                emit_boundary(ctx, block, bno)

            if start <= 0 < upto:
                ctx.push_scope()
                alloc_common(ctx)
                emit_block_A(ctx, block, g["x_in"], g["xs0"],
                             {"Wg": g["Wg"], "Wv": g["Wv"], "Wd": g["Wd"]},
                             {"bg": g["bg"], "bv": g["bv"], "bd": g["bd"]},
                             {"ident_b": g["ident_b"], "ident_f": g["ident_f"]})
                ctx.pop_scope()
                if upto > 1:
                    boundary()

            for sub in range(2):
                if start <= 1 + 2 * sub < upto:
                    bcst = {"ident_b": g["ident_b"], "lmask_b": g["lmask_b"],
                            "rope_cq": g["rope_cq"], "rope_sq": g["rope_sq"],
                            "rope_ck": g["rope_ck"], "rope_sk": g["rope_sk"],
                            "dbg_htile": None, "dbg_hbt": None}
                    ctx.push_scope()
                    alloc_battn(ctx)
                    emit_battn(ctx, block, sub,
                               g.get("xs0"), g["xi_a"], g["xi_b"],
                               {"wqkv": g["wqkv%d" % sub], "qs": g["qs%d" % sub],
                                "ks": g["ks%d" % sub], "wo": g["wo%d" % sub]},
                               bcst, sub == 0, g["yT_dB"])
                    ctx.pop_scope()
                    if upto > 2 + 2 * sub:
                        boundary()
                if start <= 2 + 2 * sub < upto:
                    ctx.push_scope()
                    alloc_bffn(ctx)
                    emit_bffn(ctx, block, sub, g["xi_b"],
                              g["xi_a"] if sub == 0 else g["x2_d"],
                              g.get("xs0"),
                              {"w1": g["w1_%d" % sub], "b1": g["b1_%d" % sub],
                               "w2": g["w2_%d" % sub], "b2": g["b2_%d" % sub],
                               "lg": g["lg%d" % sub], "lb": g["lb%d" % sub]},
                              {"ident_b": g["ident_b"]},
                              (sub == 1) and not nofinal)
                    ctx.pop_scope()
                    if upto > 3 + 2 * sub:
                        boundary()

            if start <= 5 < upto:
                ctx.push_scope()
                alloc_C(ctx)
                emit_block_C(ctx, block, g["x2_d"], g["x3_d"],
                             {"Wq": g["CWq"], "Wk": g["CWk"], "Wv": g["CWv"], "Wp": g["CWp"]},
                             {"bq": g["cbq"], "bk": g["cbk"], "bv": g["cbv"], "bp": g["cbp"]},
                             {"ident_b": g["ident_b"]},
                             (g["xlnT_d"], g["qT_d"], g["kT_d"], g["yT_dC"], g["cscr"]))
                ctx.pop_scope()
                if upto > 6:
                    boundary()

            if start <= 6 < upto:
                ctx.push_scope()
                alloc_D(ctx)
                emit_block_D(ctx, block, g["x3_d"], x_out,
                             {"wg": g["wg"], "wu": g["wu"], "wd": g["wd"]},
                             {"ident_b": g["ident_b"]})
                ctx.pop_scope()

            @block.sync
            def _(sync):
                ctx.wait(sync, "st")
                ctx.wait(sync, "ld")
    return nc


def kernel(**inputs):
    consts = host_constants()
    fc64 = np.zeros((1, 64), dtype=np.float32)
    fc64[0, :len(FCONSTS)] = np.array(FCONSTS, dtype=np.float32)
    lmask_b = consts["lmask"].astype(np.float32).astype(BF)
    x = np.ascontiguousarray(inputs["L0_attn_x"]).astype(np.float32)
    cores = list(range(8))
    base = {
        "Wg": inputs["L0_attn_Wg"], "Wv": inputs["L0_attn_Wv"], "Wd": inputs["L0_attn_Wd"],
        "bg": inputs["L0_attn_bg"], "bv": inputs["L0_attn_bv"], "bd": inputs["L0_attn_bd"],
        "CWq": inputs["L1_attn_Wq"], "CWk": inputs["L1_attn_Wk"],
        "CWv": inputs["L1_attn_Wv"], "CWp": inputs["L1_attn_Wp"],
        "cbq": inputs["L1_attn_bq"], "cbk": inputs["L1_attn_bk"],
        "cbv": inputs["L1_attn_bv"], "cbp": inputs["L1_attn_bp"],
        "wg": inputs["L1_ffn_w_gate"], "wu": inputs["L1_ffn_w_up"],
        "wd": inputs["L1_ffn_w_down"],
        "ident_b": consts["ident_b"], "ident_f": consts["ident_f"],
        "fconsts": fc64, "lmask_b": lmask_b,
        "rope_cq": consts["rope_cq"], "rope_sq": consts["rope_sq"],
        "rope_ck": consts["rope_ck"], "rope_sk": consts["rope_sk"],
    }
    for sub in range(2):
        base["wqkv%d" % sub] = np.ascontiguousarray(inputs["L0_ffn_wqkv"][sub])
        base["qs%d" % sub] = np.ascontiguousarray(inputs["L0_ffn_q_scale"][sub])
        base["ks%d" % sub] = np.ascontiguousarray(inputs["L0_ffn_k_scale"][sub])
        base["wo%d" % sub] = np.ascontiguousarray(inputs["L0_ffn_wo"][sub])
        base["w1_%d" % sub] = np.ascontiguousarray(inputs["L0_ffn_w1"][sub])
        base["b1_%d" % sub] = np.ascontiguousarray(inputs["L0_ffn_b1"][sub])
        base["w2_%d" % sub] = np.ascontiguousarray(inputs["L0_ffn_w2"][sub])
        base["b2_%d" % sub] = np.ascontiguousarray(inputs["L0_ffn_b2"][sub])
        base["lg%d" % sub] = np.ascontiguousarray(inputs["L0_ffn_ln_g"][sub])
        base["lb%d" % sub] = np.ascontiguousarray(inputs["L0_ffn_ln_b"][sub])
    nc = build_fused(0, 7)
    km = [dict(base, x_in=x[c]) for c in cores]
    res = run_bass_kernel_spmd(nc, km, core_ids=cores)
    return np.stack([r["x_out"] for r in res.results], axis=0)


# version 08
# speedup vs baseline: 1.0437x; 1.0437x over previous
import numpy as np
import ml_dtypes
from contextlib import ExitStack

import concourse.bass as bass
import concourse.mybir as mybir
from concourse.bass_utils import run_bass_kernel_spmd

F32 = mybir.dt.float32
BF16 = mybir.dt.bfloat16
AF = mybir.ActivationFunctionType
ALU = mybir.AluOpType
AX = mybir.AxisListType

S = 4096
D = 1024
NT = S // 128
NCD = D // 128
BF = ml_dtypes.bfloat16
FCONSTS = [1e-5, -0.5, 0.5, 2.0, -1.0, -2.0, 1.0 / 1024, 8.0, 0.001, 0.998,
           1e-12, 1.0 / 6.0, 6.0, 0.125, 1.0 / 512, 1.0 / 2816, 1.0 / 64,
           -1e9, 0.25, 0.75, 1.25, 1.75, 2.5, 3.5, 5.0, 3.0, 4.0, 1.5]


def host_constants():
    c = {}
    c["ident_b"] = np.eye(128, dtype=np.float32).astype(BF)
    c["ident_f"] = np.eye(128, dtype=np.float32)
    c["fconsts"] = np.array(FCONSTS, dtype=np.float32).reshape(1, -1)
    DIM_HEAD, WINDOW, n = 64, 512, S
    half = np.arange(0, DIM_HEAD, 2, dtype=np.float32)
    inv_freq = 1.0 / 10000.0 ** (half / DIM_HEAD)
    t = np.arange(n, dtype=np.float32)
    freqs = t[:, None] * inv_freq[None, :]
    freqs = np.concatenate([freqs, freqs], axis=-1)
    base = (half + 0.4 * DIM_HEAD) / (1.4 * DIM_HEAD)
    power = (t - n // 2) / float(WINDOW // 2)
    scale = base[None, :] ** power[:, None]
    scale = np.concatenate([scale, scale], axis=-1)
    cos, sin = np.cos(freqs), np.sin(freqs)
    QK_SCALE = 8.0
    c["rope_cq"] = (cos * scale * QK_SCALE).astype(np.float32)
    c["rope_sq"] = (sin * scale * QK_SCALE).astype(np.float32)
    c["rope_ck"] = (cos / scale).astype(np.float32)
    c["rope_sk"] = (sin / scale).astype(np.float32)
    masks = np.zeros((8, 128, 1024), dtype=np.float32)
    for qp in range(4):
        qi = (np.arange(128) + 128 * qp)[:, None]
        kj = np.arange(1024)[None, :]
        valid = (qi <= kj) & (kj <= qi + 512)
        valid0 = (kj >= 512) & (kj <= qi + 512)
        masks[qp] = np.where(valid, 0.0, -1e9)
        masks[4 + qp] = np.where(valid0, 0.0, -1e9)
    c["lmask"] = masks.reshape(8 * 128, 1024)
    return c


class Ctx:
    def __init__(self, nc):
        self.nc = nc
        self.es = ExitStack()
        self.cnt = {}
        self.sems = {}
        self.sb = {}
        self.ps = {}
        self.mark = {}
        self.scope_id = 0

    def sem(self, name):
        s = self.es.enter_context(self.nc.semaphore(name))
        self.sems[name] = s
        self.cnt[name] = 0
        return s

    def inc(self, instr, name, n=1):
        self.cnt[name] += n
        instr.then_inc(self.sems[name], n)
        return self.cnt[name]

    def wait(self, eng, name, val=None):
        v = self.cnt[name] if val is None else val
        if v > 0:
            eng.wait_ge(self.sems[name], v)

    def rec(self, key, sem):
        self.mark[key] = self.cnt[sem]

    def at(self, key, default=0):
        return self.mark.get(key, default)

    def sbuf(self, name, shape, dtype):
        mangled = name if self.scope_id == 0 else "%s_s%d" % (name, self.scope_id)
        h = self.es.enter_context(self.nc.sbuf_tensor(mangled, shape, dtype))
        self.sb[name] = h
        return h

    def psum(self, name, shape, dtype):
        h = self.es.enter_context(self.nc.psum_tensor(name, shape, dtype))
        self.ps[name] = h
        return h

    def push_scope(self):
        self._scope = ExitStack()
        self._saved_es = self.es
        self.es = self._scope
        self.scope_id += 1

    def pop_scope(self):
        self._scope.close()
        self.es = self._saved_es

    def reset_counts(self):
        for s in ("ld", "st", "pe", "ve", "se"):
            self.cnt[s] = 0
        self.mark = {}


def emit_block_A(ctx, block, x_in, x_out, W_aps, b_aps, consts):
    sb, ps = ctx.sb, ctx.ps
    xlnT = sb["big0"]
    wT = sb["wT"]
    xt = sb["xt"]; sq = sb["sq"]; lnout = sb["lnout"]; stat = sb["stat"]
    bcol = sb["bcol"]; idb = sb["idb"]; idf = sb["idf"]
    wtile = sb["wtile"]; wtb = sb["wtb"]; gbuf = sb["gbuf"]
    afull = sb["afull"]; xsfull = sb["xsfull"]; hfull = sb["hfull"]
    rbuf = sb["rbuf"]; obuf = sb["obuf"]
    p0, p1, p2, p3 = ps["p0"], ps["p1"], ps["p2"], ps["p3"]
    pb0, pb1 = ps["pb0"], ps["pb1"]

    @block.sync
    def _(sync):
        ctx.inc(sync.dma_start(out=idb[:], in_=consts["ident_b"]), "ld", 16)
        ctx.inc(sync.dma_start(out=idf[:], in_=consts["ident_f"]), "ld", 16)
        for w, off in (("bg", 0), ("bv", 8), ("bd", 16)):
            for k in range(8):
                ctx.inc(sync.dma_start(out=bcol[:, off + k:off + k + 1],
                                       in_=b_aps[w][k * 128:(k + 1) * 128]), "ld", 16)

    for j in range(NT):
        bi = j % 2
        sc = bi * 4
        xv = xt[:, bi * 1024:(bi + 1) * 1024]
        lv = lnout[:, bi * 1024:(bi + 1) * 1024]

        @block.sync
        def _(sync, j=j, xv=xv):
            ctx.wait(sync, "ve", ctx.at("a1_apply_%d" % (j - 2)))
            ctx.inc(sync.dma_start(out=xv, in_=x_in[j * 128:(j + 1) * 128, :]), "ld", 16)

        ld_now = ctx.cnt["ld"]

        @block.vector
        def _(vector, xv=xv, ld_now=ld_now, sc=sc):
            ctx.wait(vector, "ld", ld_now)
            ctx.inc(vector.reduce_sum(stat[:, sc:sc + 1], xv, AX.X), "ve", 1)

        @block.scalar
        def _(scalar, xv=xv, ld_now=ld_now, sc=sc, j=j):
            ctx.wait(scalar, "ld", ld_now)
            ctx.inc(scalar.activation(sq[:], xv, AF.Square, bias=0.0, scale=1.0,
                                      accum_out=stat[:, sc + 1:sc + 2]), "se", 1)

        se_sq = ctx.cnt["se"]

        @block.vector
        def _(vector, se_sq=se_sq, sc=sc):
            ctx.wait(vector, "se", se_sq)
            ctx.wait(vector, "ve")
            ctx.inc(vector.tensor_scalar_mul(stat[:, sc:sc + 1], stat[:, sc:sc + 1],
                                             1.0 / D), "ve", 1)
            ctx.wait(vector, "ve")
            ctx.inc(vector.scalar_tensor_tensor(stat[:, sc + 3:sc + 4], stat[:, sc:sc + 1],
                                                1.0, stat[:, sc:sc + 1],
                                                ALU.bypass, ALU.mult), "ve", 1)
            ctx.wait(vector, "ve")
            ctx.inc(vector.scalar_tensor_tensor(stat[:, sc + 2:sc + 3], stat[:, sc + 1:sc + 2],
                                                1.0 / D, stat[:, sc + 3:sc + 4],
                                                ALU.mult, ALU.subtract), "ve", 1)

        ve_var = ctx.cnt["ve"]

        @block.scalar
        def _(scalar, ve_var=ve_var, sc=sc):
            ctx.wait(scalar, "ve", ve_var)
            ctx.inc(scalar.activation(stat[:, sc + 2:sc + 3], stat[:, sc + 2:sc + 3],
                                      AF.Ln, bias=1e-5, scale=1.0), "se", 1)
            ctx.wait(scalar, "se")
            ctx.inc(scalar.activation(stat[:, sc + 2:sc + 3], stat[:, sc + 2:sc + 3],
                                      AF.Exp, bias=0.0, scale=-0.5), "se", 1)

        se_rs = ctx.cnt["se"]

        @block.vector
        def _(vector, xv=xv, lv=lv, se_rs=se_rs, j=j, sc=sc):
            ctx.wait(vector, "se", se_rs)
            ctx.wait(vector, "pe", ctx.at("a1_tr_%d" % (j - 2)))
            i = vector.tensor_scalar(lv, xv, stat[:, sc:sc + 1], stat[:, sc + 2:sc + 3],
                                     ALU.subtract, ALU.mult)
            ctx.inc(i, "ve", 1)
            ctx.rec("a1_apply_%d" % j, "ve")

        ve_ln = ctx.cnt["ve"]

        @block.tensor
        def _(tensor, lv=lv, j=j, ve_ln=ve_ln):
            ctx.wait(tensor, "ve", ve_ln)
            ctx.wait(tensor, "se", ctx.at("a1_evac_%d" % (j - 2)))
            pbank = pb0 if (j % 2 == 0) else pb1
            for k in range(8):
                i = tensor.transpose(pbank[:, k * 128:(k + 1) * 128],
                                     lv[:, k * 128:(k + 1) * 128], idb[:])
            ctx.inc(i, "pe", 1)
            ctx.rec("a1_tr_%d" % j, "pe")

        pe_tr = ctx.cnt["pe"]

        @block.scalar
        def _(scalar, j=j, pe_tr=pe_tr):
            ctx.wait(scalar, "pe", pe_tr)
            pbank = pb0 if (j % 2 == 0) else pb1
            for k in range(8):
                i = scalar.copy(xlnT[:, k * 4096 + j * 128: k * 4096 + (j + 1) * 128],
                                pbank[:, k * 128:(k + 1) * 128])
            ctx.inc(i, "se", 1)
            ctx.rec("a1_evac_%d" % j, "se")

    ctx.rec("a2a_sebase", "se")
    ctx.rec("a2a_pebase", "pe")
    for wi, wname in enumerate(("Wg", "Wv", "Wd")):
        for i in range(8):
            for k in range(8):
                n = wi * 64 + i * 8 + k
                bi = n % 2
                wv = wtile[:, bi * 128:(bi + 1) * 128]
                wb = wtb[:, bi * 128:(bi + 1) * 128]

                @block.sync
                def _(sync, wname=wname, i=i, k=k, n=n, wv=wv):
                    ctx.wait(sync, "ve", ctx.at("a2a_cv_%d" % (n - 2)))
                    ctx.inc(sync.dma_start(
                        out=wv, in_=W_aps[wname][i * 128:(i + 1) * 128,
                                                 k * 128:(k + 1) * 128]), "ld", 16)

                ld_now = ctx.cnt["ld"]

                @block.vector
                def _(vector, wv=wv, wb=wb, ld_now=ld_now, n=n):
                    ctx.wait(vector, "ld", ld_now)
                    ctx.wait(vector, "pe", ctx.at("a2a_tr_%d" % (n - 2),
                                                  ctx.at("a2a_pebase")))
                    ctx.inc(vector.tensor_copy(wb, wv), "ve", 1)
                    ctx.rec("a2a_cv_%d" % n, "ve")

                ve_now = ctx.cnt["ve"]

                @block.tensor
                def _(tensor, wb=wb, ve_now=ve_now, n=n):
                    ctx.wait(tensor, "ve", ve_now)
                    ctx.wait(tensor, "se", ctx.at("a2a_ev_%d" % (n - 2),
                                                  ctx.at("a2a_sebase")))
                    pbank = pb0 if (n % 2 == 0) else pb1
                    ctx.inc(tensor.transpose(pbank[:, 0:128], wb, idb[:]), "pe", 1)
                    ctx.rec("a2a_tr_%d" % n, "pe")

                pe_now = ctx.cnt["pe"]

                @block.scalar
                def _(scalar, wi=wi, i=i, k=k, n=n, pe_now=pe_now):
                    ctx.wait(scalar, "pe", pe_now)
                    pbank = pb0 if (n % 2 == 0) else pb1
                    i2 = scalar.copy(wT[:, wi * 8192 + k * 1024 + i * 128:
                                        wi * 8192 + k * 1024 + (i + 1) * 128],
                                     pbank[:, 0:128])
                    ctx.inc(i2, "se", 1)
                    ctx.rec("a2a_ev_%d" % n, "se")

    ve_all = ctx.cnt["ve"]
    se_all = ctx.cnt["se"]
    for i in range(8):
        for cch in range(8):
            t0 = cch * 512

            @block.tensor
            def _(tensor, i=i, cch=cch, t0=t0):
                if i == 0 and cch == 0:
                    ctx.wait(tensor, "ve", ve_all)
                    ctx.wait(tensor, "se", se_all)
                ctx.wait(tensor, "se", ctx.at("a2b_gate_se"))
                for wi, bank in ((0, p0), (1, p1), (2, p2)):
                    for k in range(8):
                        ii = tensor.matmul(bank[:],
                                           wT[:, wi * 8192 + k * 1024 + i * 128:
                                              wi * 8192 + k * 1024 + (i + 1) * 128],
                                           xlnT[:, k * 4096 + t0:k * 4096 + t0 + 512],
                                           start=(k == 0), stop=(k == 7))
                ctx.inc(ii, "pe", 1)

            pe_now = ctx.cnt["pe"]

            @block.scalar
            def _(scalar, i=i, t0=t0, pe_now=pe_now):
                ctx.wait(scalar, "pe", pe_now)
                ctx.wait(scalar, "ve", ctx.at("a2b_gate_ve"))
                scalar.activation(gbuf[:, 0:512], p0[:], AF.Sigmoid,
                                  bias=bcol[:, 0 + i:1 + i], scale=1.0)
                scalar.activation(gbuf[:, 512:1024], p1[:], AF.Tanh,
                                  bias=bcol[:, 8 + i:9 + i], scale=1.0)
                ctx.inc(scalar.activation(gbuf[:, 1024:1536], p2[:], AF.Sigmoid,
                                          bias=bcol[:, 16 + i:17 + i], scale=1.0), "se", 1)
                ctx.rec("a2b_gate_se", "se")

            se_now = ctx.cnt["se"]

            @block.vector
            def _(vector, t0=t0, se_now=se_now):
                ctx.wait(vector, "se", se_now)
                vector.tensor_tensor(xsfull[:, t0:t0 + 512], gbuf[:, 0:512],
                                     gbuf[:, 512:1024], ALU.mult)
                i2 = vector.tensor_scalar(afull[:, t0:t0 + 512], gbuf[:, 1024:1536],
                                          0.998, 0.001, ALU.mult, ALU.add)
                ctx.inc(i2, "ve", 1)
                ctx.rec("a2b_gate_ve", "ve")

        @block.vector
        def _(vector, i=i):
            ctx.wait(vector, "ve")
            ctx.inc(vector.tensor_tensor_scan(hfull[:], afull[:], xsfull[:], 0.0,
                                              ALU.mult, ALU.add), "ve", 1)

        ve_scan = ctx.cnt["ve"]

        for q in range(8):
            @block.tensor
            def _(tensor, i=i, q=q, ve_scan=ve_scan):
                ctx.wait(tensor, "ve", ve_scan)
                ctx.wait(tensor, "ve", ctx.at("a2b_res_ve"))
                for r in range(4):
                    jt = q * 4 + r
                    ii = tensor.transpose(p3[:, r * 128:(r + 1) * 128],
                                          hfull[:, jt * 128:(jt + 1) * 128], idf[:])
                ctx.inc(ii, "pe", 1)

            pe_now = ctx.cnt["pe"]

            @block.sync
            def _(sync, i=i, q=q):
                ctx.wait(sync, "ve", ctx.at("a2b_res_ve"))
                for r in range(4):
                    jt = q * 4 + r
                    ctx.inc(sync.dma_start(
                        out=rbuf[:, r * 128:(r + 1) * 128],
                        in_=x_in[jt * 128:(jt + 1) * 128, i * 128:(i + 1) * 128]), "ld", 16)

            ld_now = ctx.cnt["ld"]

            @block.vector
            def _(vector, pe_now=pe_now, ld_now=ld_now):
                ctx.wait(vector, "pe", pe_now)
                ctx.wait(vector, "ld", ld_now)
                ctx.wait(vector, "st", ctx.at("a2b_res_st"))
                ctx.inc(vector.tensor_tensor(obuf[:, 0:512], p3[:], rbuf[:, 0:512],
                                             ALU.add), "ve", 1)
                ctx.rec("a2b_res_ve", "ve")

            ve_now = ctx.cnt["ve"]

            @block.sync
            def _(sync, i=i, q=q, ve_now=ve_now):
                ctx.wait(sync, "ve", ve_now)
                for r in range(4):
                    jt = q * 4 + r
                    ctx.inc(sync.dma_start(
                        out=x_out[jt * 128:(jt + 1) * 128, i * 128:(i + 1) * 128],
                        in_=obuf[:, r * 128:(r + 1) * 128]), "st", 16)
                ctx.rec("a2b_res_st", "st")


def alloc_shared(ctx):
    ctx.psum("p0", [128, 512], F32)
    ctx.psum("p1", [128, 512], F32)
    ctx.psum("p2", [128, 512], F32)
    ctx.psum("p3", [128, 512], F32)
    ctx.psum("pb0", [128, 1024], BF16)
    ctx.psum("pb1", [128, 1024], BF16)
    ctx.sbuf("cpool", [128, 64], F32)
    for s in ("ld", "st", "pe", "ve", "se", "gp", "gld"):
        ctx.sem(s)


def alloc_common(ctx):
    ctx.sbuf("big0", [128, 32768], BF16)
    ctx.sbuf("wT", [128, 24576], BF16)
    ctx.sbuf("xt", [128, 2048], F32)
    ctx.sbuf("sq", [128, 1024], F32)
    ctx.sbuf("lnout", [128, 2048], BF16)
    ctx.sbuf("stat", [128, 8], F32)
    ctx.sbuf("bcol", [128, 24], F32)
    ctx.sbuf("idb", [128, 128], BF16)
    ctx.sbuf("idf", [128, 128], F32)
    ctx.sbuf("wtile", [128, 256], F32)
    ctx.sbuf("wtb", [128, 256], BF16)
    ctx.sbuf("gbuf", [128, 1536], F32)
    ctx.sbuf("afull", [128, 4096], F32)
    ctx.sbuf("xsfull", [128, 4096], F32)
    ctx.sbuf("hfull", [128, 4096], F32)
    ctx.sbuf("rbuf", [128, 1024], F32)
    ctx.sbuf("obuf", [128, 1024], F32)


DEBUG_A = False

def build_A():
    nc = bass.Bass("TRN2", target_bir_lowering=False)
    x_in = nc.dram_tensor("x_in", [S, D], F32, kind="ExternalInput").ap()
    Wg = nc.dram_tensor("Wg", [D, D], F32, kind="ExternalInput").ap()
    Wv = nc.dram_tensor("Wv", [D, D], F32, kind="ExternalInput").ap()
    Wd = nc.dram_tensor("Wd", [D, D], F32, kind="ExternalInput").ap()
    bg = nc.dram_tensor("bg", [D], F32, kind="ExternalInput").ap()
    bv = nc.dram_tensor("bv", [D], F32, kind="ExternalInput").ap()
    bd = nc.dram_tensor("bd", [D], F32, kind="ExternalInput").ap()
    ident_b = nc.dram_tensor("ident_b", [128, 128], BF16, kind="ExternalInput").ap()
    ident_f = nc.dram_tensor("ident_f", [128, 128], F32, kind="ExternalInput").ap()
    x_out = nc.dram_tensor("x_out", [S, D], F32, kind="ExternalOutput").ap()
    if DEBUG_A:
        dbg_xlnT = nc.dram_tensor("dbg_xlnT", [128, 32768], BF16, kind="ExternalOutput").ap()
        dbg_bcol = nc.dram_tensor("dbg_bcol", [128, 24], F32, kind="ExternalOutput").ap()
        dbg_a = nc.dram_tensor("dbg_a", [128, 4096], F32, kind="ExternalOutput").ap()
        dbg_xs = nc.dram_tensor("dbg_xs", [128, 4096], F32, kind="ExternalOutput").ap()
        dbg_h = nc.dram_tensor("dbg_h", [128, 4096], F32, kind="ExternalOutput").ap()
        dbg_wT = nc.dram_tensor("dbg_wT", [128, 24576], BF16, kind="ExternalOutput").ap()

    fconsts = nc.dram_tensor("fconsts", [1, 64], F32, kind="ExternalInput").ap()
    ctx = Ctx(nc)
    with ctx.es:
        alloc_shared(ctx)
        alloc_common(ctx)
        cp = ctx.sb["cpool"]
        for i, v in enumerate(FCONSTS):
            nc.const_aps.register(F32, float(v), cp[:, i:i + 1])
        with nc.Block() as block:
            @block.sync
            def _(sync):
                fc_rep = bass.AP(fconsts.tensor, 0, [[0, 128], [1, len(FCONSTS)]])
                ctx.inc(sync.dma_start(out=cp[:, 0:len(FCONSTS)], in_=fc_rep), "ld", 16)

            emit_block_A(ctx, block, x_in, x_out,
                         {"Wg": Wg, "Wv": Wv, "Wd": Wd},
                         {"bg": bg, "bv": bv, "bd": bd},
                         {"ident_b": ident_b, "ident_f": ident_f})

            @block.gpsimd
            def _(gpsimd):
                ctx.wait(gpsimd, "st")
                ctx.wait(gpsimd, "ve")
                ctx.wait(gpsimd, "se")
                if DEBUG_A:
                    ctx.inc(gpsimd.dma_start(out=dbg_xlnT, in_=ctx.sb["big0"][:]), "st", 16)
                    ctx.inc(gpsimd.dma_start(out=dbg_bcol, in_=ctx.sb["bcol"][:]), "st", 16)
                    ctx.inc(gpsimd.dma_start(out=dbg_a, in_=ctx.sb["afull"][:]), "st", 16)
                    ctx.inc(gpsimd.dma_start(out=dbg_xs, in_=ctx.sb["xsfull"][:]), "st", 16)
                    ctx.inc(gpsimd.dma_start(out=dbg_h, in_=ctx.sb["hfull"][:]), "st", 16)
                    ctx.inc(gpsimd.dma_start(out=dbg_wT, in_=ctx.sb["wT"][:]), "st", 16)
                    ctx.wait(gpsimd, "st")
    return nc



def emit_ln_stats(ctx, block, xv, sqbuf, stat, sc, width, ld_wait=None, extra=None,
                  ve_wait=None):
    D_ = width
    ve_base = ctx.cnt["ve"]
    se_base = ctx.cnt["se"]

    @block.scalar
    def _(scalar):
        if ld_wait is not None:
            ctx.wait(scalar, "ld", ld_wait)
        if ve_wait is not None:
            ctx.wait(scalar, "ve", ve_wait)
        if extra is not None:
            extra(scalar)
        ctx.inc(scalar.activation(sqbuf, xv, AF.Square, bias=0.0, scale=1.0,
                                  accum_out=stat[:, sc + 1:sc + 2]), "se", 1)
        scalar.wait_ge(ctx.sems["ve"], ve_base + 4)
        ctx.inc(scalar.activation(stat[:, sc + 2:sc + 3], stat[:, sc + 2:sc + 3],
                                  AF.Ln, bias=1e-5, scale=1.0), "se", 1)
        ctx.wait(scalar, "se")
        ctx.inc(scalar.activation(stat[:, sc + 2:sc + 3], stat[:, sc + 2:sc + 3],
                                  AF.Exp, bias=0.0, scale=-0.5), "se", 1)

    @block.vector
    def _(vector):
        if ld_wait is not None:
            ctx.wait(vector, "ld", ld_wait)
        ctx.inc(vector.reduce_sum(stat[:, sc:sc + 1], xv, AX.X), "ve", 1)
        vector.wait_ge(ctx.sems["se"], se_base + 1)
        ctx.wait(vector, "ve")
        ctx.inc(vector.tensor_scalar_mul(stat[:, sc:sc + 1], stat[:, sc:sc + 1],
                                         1.0 / D_), "ve", 1)
        ctx.wait(vector, "ve")
        ctx.inc(vector.scalar_tensor_tensor(stat[:, sc + 3:sc + 4], stat[:, sc:sc + 1],
                                            1.0, stat[:, sc:sc + 1],
                                            ALU.bypass, ALU.mult), "ve", 1)
        ctx.wait(vector, "ve")
        ctx.inc(vector.scalar_tensor_tensor(stat[:, sc + 2:sc + 3], stat[:, sc + 1:sc + 2],
                                            1.0 / D_, stat[:, sc + 3:sc + 4],
                                            ALU.mult, ALU.subtract), "ve", 1)
    return ctx.cnt["se"]


def emit_fp4_quant(ctx, block, xv, out_b, W, bufs):
    G = W // 16
    qa, qb, qc, sgn, scb, rcb, blk = bufs

    @block.scalar
    def _(scalar):
        ctx.inc(scalar.activation(qa[:, 0:W], xv, AF.Abs, bias=0.0, scale=1.0), "se", 1)
        ctx.wait(scalar, "se")
        ctx.inc(scalar.activation(sgn[:, 0:W], xv, AF.Sign, bias=0.0, scale=1.0), "se", 1)

    se_abs = ctx.cnt["se"]

    @block.vector
    def _(vector):
        ctx.wait(vector, "se", se_abs)
        ctx.inc(vector.tensor_reduce(
            blk[:, 0:G], qa[:, 0:W].rearrange("p (g b) -> p g b", b=16),
            AX.X, ALU.max), "ve", 1)
        ctx.wait(vector, "ve")
        ctx.inc(vector.tensor_scalar(blk[:, G:2 * G], blk[:, 0:G], 1.0 / 6.0, 1e-12,
                                     ALU.mult, ALU.max), "ve", 1)
        ctx.wait(vector, "ve")
        ctx.inc(vector.reciprocal(blk[:, 0:G], blk[:, G:2 * G]), "ve", 1)
        ctx.wait(vector, "ve")
        scv = scb[:, 0:W].rearrange("p (g b) -> p g b", b=16)
        rcv = rcb[:, 0:W].rearrange("p (g b) -> p g b", b=16)
        s3 = blk[:, G:2 * G].rearrange("p (g b) -> p g b", b=1)
        r3 = blk[:, 0:G].rearrange("p (g b) -> p g b", b=1)
        for r in range(16):
            vector.tensor_copy(scv[:, :, r:r + 1], s3)
            i = vector.tensor_copy(rcv[:, :, r:r + 1], r3)
        ctx.inc(i, "ve", 1)
        ctx.wait(vector, "ve")
        ctx.inc(vector.tensor_mul(qa[:, 0:W], qa[:, 0:W], rcb[:, 0:W]), "ve", 1)
        ctx.wait(vector, "ve")
        vector.tensor_single_scalar(qb[:, 0:W], qa[:, 0:W], 0.25, ALU.is_gt)
        vector.tensor_single_scalar(qc[:, 0:W], qa[:, 0:W], 2.5, ALU.is_gt)
        vector.scalar_tensor_tensor(qb[:, 0:W], qa[:, 0:W], 0.75, qb[:, 0:W],
                                    ALU.is_gt, ALU.add)
        vector.scalar_tensor_tensor(qc[:, 0:W], qa[:, 0:W], 3.5, qc[:, 0:W],
                                    ALU.is_gt, ALU.add)
        vector.scalar_tensor_tensor(qb[:, 0:W], qa[:, 0:W], 1.25, qb[:, 0:W],
                                    ALU.is_gt, ALU.add)
        vector.scalar_tensor_tensor(qc[:, 0:W], qa[:, 0:W], 5.0, qc[:, 0:W],
                                    ALU.is_gt, ALU.add)
        vector.scalar_tensor_tensor(qb[:, 0:W], qa[:, 0:W], 1.75, qb[:, 0:W],
                                    ALU.is_gt, ALU.add)
        ctx.inc(vector.scalar_tensor_tensor(qc[:, 0:W], qa[:, 0:W], 5.0, qc[:, 0:W],
                                            ALU.is_gt, ALU.add), "ve", 1)
        ctx.wait(vector, "ve")
        ctx.inc(vector.scalar_tensor_tensor(qa[:, 0:W], qb[:, 0:W], 0.5, qc[:, 0:W],
                                            ALU.mult, ALU.add), "ve", 1)
        ctx.wait(vector, "ve")
        ctx.inc(vector.tensor_mul(qa[:, 0:W], qa[:, 0:W], sgn[:, 0:W]), "ve", 1)
        ctx.wait(vector, "ve")
        ctx.inc(vector.tensor_mul(out_b, qa[:, 0:W], scb[:, 0:W]), "ve", 1)
    return ctx.cnt["ve"]



def emit_transpose_evac(ctx, block, srcbuf, ntr, dst_fn, idb, pb0, pb1, ve_wait=None):
    for g0 in range(0, ntr, 8):
        gtiles = list(range(g0, min(g0 + 8, ntr)))

        @block.tensor
        def _(tensor, gtiles=gtiles, g0=g0, ve_wait=ve_wait):
            if g0 == 0 and ve_wait is not None:
                ctx.wait(tensor, "ve", ve_wait)
            ctx.wait(tensor, "se", ctx.at("trev_ev"))
            for k in gtiles:
                slot = (k % 8) // 2
                pbank = pb0 if (k % 2 == 0) else pb1
                ii = tensor.transpose(pbank[:, slot * 128:slot * 128 + 128],
                                      srcbuf[:, k * 128:(k + 1) * 128], idb[:])
            ctx.inc(ii, "pe", 1)
            ctx.rec("trev_tr", "pe")

        pe_now = ctx.cnt["pe"]

        @block.scalar
        def _(scalar, gtiles=gtiles, pe_now=pe_now):
            ctx.wait(scalar, "pe", pe_now)
            for k in gtiles:
                slot = (k % 8) // 2
                pbank = pb0 if (k % 2 == 0) else pb1
                ii = dst_fn(scalar, k, pbank[:, slot * 128:slot * 128 + 128])
            ctx.inc(ii, "se", 1)
            ctx.rec("trev_ev", "se")
    return ctx.cnt["se"]


IC = 2816
NIT = IC // 128

def emit_block_D(ctx, block, x_in, x_out, w_aps, consts):
    sb, ps = ctx.sb, ctx.ps
    wgqT, wuqT, wdqT = sb["wgqT"], sb["wuqT"], sb["wdqT"]
    xt = sb["xt"]; stat = sb["stat"]; idb = sb["idb"]
    qa, qb, qc = sb["qa"], sb["qb"], sb["qc"]
    sgn, scb, rcb, blk = sb["sgn"], sb["scb"], sb["rcb"], sb["blk"]
    hbuf = sb["hbuf"]; hq = sb["hq"]; hqT = sb["hqT"]
    xq = sb["hq"]
    xqT = sb["hqT"]
    sg = sb["sg"]
    obuf = sb["hbuf"]
    p0, p1, p2, p3 = ps["p0"], ps["p1"], ps["p2"], ps["p3"]
    pb0, pb1 = ps["pb0"], ps["pb1"]

    @block.sync
    def _(sync):
        ctx.inc(sync.dma_start(out=idb[:], in_=consts["ident_b"]), "ld", 16)

    jobs = [("wg", w_aps["wg"], NIT, D, wgqT, IC),
            ("wu", w_aps["wu"], NIT, D, wuqT, IC),
            ("wd", w_aps["wd"], NCD, IC, wdqT, D)]
    for name, wap, nrt, W, dstT, dcols in jobs:
        ntr = W // 128
        for r in range(nrt):
            @block.sync
            def _(sync, wap=wap, r=r, W=W):
                ctx.wait(sync, "ve", ctx.at("d0_q"))
                ctx.inc(sync.dma_start(out=hbuf[:, 0:W],
                                       in_=wap[r * 128:(r + 1) * 128, 0:W]), "ld", 16)

            ld_now = ctx.cnt["ld"]

            @block.vector
            def _(vector, ld_now=ld_now):
                ctx.wait(vector, "ld", ld_now)
                ctx.wait(vector, "pe", ctx.at("trev_tr"))
                vector.engine_nop()

            @block.scalar
            def _(scalar, ld_now=ld_now):
                ctx.wait(scalar, "ld", ld_now)

            emit_fp4_quant(ctx, block, hbuf[:, 0:W], hq[:, 0:W], W,
                           (qa, qb, qc, sgn, scb, rcb, blk))
            ve_q = ctx.cnt["ve"]
            ctx.rec("d0_q", "ve")

            def dst_fn(eng, k, psrc, dstT=dstT, dcols=dcols, r=r):
                return eng.copy(dstT[:, k * dcols + r * 128:
                                     k * dcols + (r + 1) * 128], psrc)
            emit_transpose_evac(ctx, block, hq, ntr, dst_fn, idb, pb0, pb1,
                                ve_wait=ve_q)
            ctx.rec("d0_tr", "pe")

    ICH = [(0, 512), (512, 512), (1024, 512), (1536, 512), (2048, 512), (2560, 256)]
    for j in range(NT):
        @block.sync
        def _(sync, j=j):
            ctx.wait(sync, "st", ctx.at("d1_store"))
            ctx.inc(sync.dma_start(out=xt[:, 0:1024],
                                   in_=x_in[j * 128:(j + 1) * 128, :]), "ld", 16)

        ld_now = ctx.cnt["ld"]
        emit_ln_stats(ctx, block, xt[:, 0:1024], qa[:, 0:1024], stat, 0, D,
                      ld_wait=ld_now)
        se_rs = ctx.cnt["se"]

        @block.vector
        def _(vector, se_rs=se_rs):
            ctx.wait(vector, "se", se_rs)
            ctx.inc(vector.tensor_scalar(hbuf[:, 0:1024], xt[:, 0:1024],
                                         stat[:, 0:1], stat[:, 2:3],
                                         ALU.subtract, ALU.mult), "ve", 1)
            ctx.wait(vector, "ve")

        emit_fp4_quant(ctx, block, hbuf[:, 0:1024], xq[:, 0:1024], D,
                       (qa, qb, qc, sgn, scb, rcb, blk))
        ve_q = ctx.cnt["ve"]

        def xq_dst(eng, k, psrc):
            return eng.copy(xqT[:, k * 128:(k + 1) * 128], psrc)
        emit_transpose_evac(ctx, block, xq, 8, xq_dst, idb, pb0, pb1, ve_wait=ve_q)
        se_xqT = ctx.cnt["se"]

        pe_gu = ctx.cnt["pe"]
        se_gu = ctx.cnt["se"]
        ve_gu = ctx.cnt["ve"]

        @block.tensor
        def _(tensor, se_xqT=se_xqT, ve_gu=ve_gu, se_gu=se_gu):
            ctx.wait(tensor, "se", se_xqT)
            ctx.wait(tensor, "ve", ctx.at("d1_gu"))
            for ci, (i0, ilen) in enumerate(ICH):
                if ci >= 1:
                    tensor.wait_ge(ctx.sems["se"], se_gu + ci)
                    tensor.wait_ge(ctx.sems["ve"], ve_gu + ci)
                for k in range(8):
                    tensor.matmul(p0[:, 0:ilen], xqT[:, k * 128:(k + 1) * 128],
                                  wgqT[:, k * IC + i0:k * IC + i0 + ilen],
                                  start=(k == 0), stop=(k == 7))
                for k in range(8):
                    ii = tensor.matmul(p1[:, 0:ilen], xqT[:, k * 128:(k + 1) * 128],
                                       wuqT[:, k * IC + i0:k * IC + i0 + ilen],
                                       start=(k == 0), stop=(k == 7))
                ctx.inc(ii, "pe", 1)

        @block.scalar
        def _(scalar, pe_gu=pe_gu, ve_gu=ve_gu):
            for ci, (i0, ilen) in enumerate(ICH):
                scalar.wait_ge(ctx.sems["pe"], pe_gu + ci + 1)
                if ci >= 1:
                    scalar.wait_ge(ctx.sems["ve"], ve_gu + ci)
                ctx.inc(scalar.activation(sg[:, 0:ilen], p0[:, 0:ilen], AF.Silu,
                                          bias=0.0, scale=1.0), "se", 1)

        @block.vector
        def _(vector, se_gu=se_gu):
            for ci, (i0, ilen) in enumerate(ICH):
                vector.wait_ge(ctx.sems["se"], se_gu + ci + 1)
                ctx.inc(vector.tensor_mul(hbuf[:, i0:i0 + ilen], sg[:, 0:ilen],
                                          p1[:, 0:ilen]), "ve", 1)
            ctx.rec("d1_gu", "ve")

        @block.vector
        def _(vector):
            ctx.wait(vector, "ve")

        emit_fp4_quant(ctx, block, hbuf[:, 0:IC], hq[:, 0:IC], IC,
                       (qa, qb, qc, sgn, scb, rcb, blk))
        ve_hq = ctx.cnt["ve"]

        def hq_dst(eng, k, psrc):
            return eng.copy(hqT[:, k * 128:(k + 1) * 128], psrc)
        emit_transpose_evac(ctx, block, hq, NIT, hq_dst, idb, pb0, pb1, ve_wait=ve_hq)
        se_hqT = ctx.cnt["se"]

        for oc in range(2):
            @block.tensor
            def _(tensor, oc=oc, se_hqT=se_hqT):
                if oc == 0:
                    ctx.wait(tensor, "se", se_hqT)
                ctx.wait(tensor, "ve", ctx.at("d1_res"))
                for m in range(NIT):
                    ii = tensor.matmul(p2[:], hqT[:, m * 128:(m + 1) * 128],
                                       wdqT[:, m * D + oc * 512:m * D + oc * 512 + 512],
                                       start=(m == 0), stop=(m == NIT - 1))
                ctx.inc(ii, "pe", 1)

            pe_now = ctx.cnt["pe"]

            @block.vector
            def _(vector, oc=oc, pe_now=pe_now):
                ctx.wait(vector, "pe", pe_now)
                ctx.wait(vector, "st", ctx.at("d1_store"))
                ctx.inc(vector.tensor_add(obuf[:, oc * 512:(oc + 1) * 512], p2[:],
                                          xt[:, oc * 512:(oc + 1) * 512]), "ve", 1)
                ctx.rec("d1_res", "ve")

            ve_now = ctx.cnt["ve"]

        @block.sync
        def _(sync, j=j):
            ctx.wait(sync, "ve")
            ctx.inc(sync.dma_start(out=x_out[j * 128:(j + 1) * 128, :],
                                   in_=obuf[:, 0:1024]), "st", 16)
            ctx.rec("d1_store", "st")


def alloc_D(ctx):
    ctx.sbuf("wgqT", [128, 8 * IC], BF16)
    ctx.sbuf("wuqT", [128, 8 * IC], BF16)
    ctx.sbuf("wdqT", [128, NIT * D], BF16)
    ctx.sbuf("xt", [128, 1024], F32)
    ctx.sbuf("stat", [128, 8], F32)
    ctx.sbuf("idb", [128, 128], BF16)
    ctx.sbuf("qa", [128, IC], F32)
    ctx.sbuf("qb", [128, IC], BF16)
    ctx.sbuf("qc", [128, IC], BF16)
    ctx.sbuf("sgn", [128, IC], BF16)
    ctx.sbuf("scb", [128, IC], BF16)
    ctx.sbuf("rcb", [128, IC], F32)
    ctx.sbuf("blk", [128, 2 * (IC // 16)], F32)
    ctx.sbuf("hbuf", [128, IC], F32)
    ctx.sbuf("hq", [128, IC], BF16)
    ctx.sbuf("hqT", [128, NIT * 128], BF16)
    ctx.sbuf("sg", [128, 512], F32)


def build_D():
    nc = bass.Bass("TRN2", target_bir_lowering=False)
    x_in = nc.dram_tensor("x_in", [S, D], F32, kind="ExternalInput").ap()
    wg = nc.dram_tensor("wg", [IC, D], F32, kind="ExternalInput").ap()
    wu = nc.dram_tensor("wu", [IC, D], F32, kind="ExternalInput").ap()
    wd = nc.dram_tensor("wd", [D, IC], F32, kind="ExternalInput").ap()
    ident_b = nc.dram_tensor("ident_b", [128, 128], BF16, kind="ExternalInput").ap()
    fconsts = nc.dram_tensor("fconsts", [1, 64], F32, kind="ExternalInput").ap()
    x_out = nc.dram_tensor("x_out", [S, D], F32, kind="ExternalOutput").ap()

    ctx = Ctx(nc)
    with ctx.es:
        alloc_shared(ctx)
        alloc_D(ctx)
        cp = ctx.sb["cpool"]
        for i, v in enumerate(FCONSTS):
            nc.const_aps.register(F32, float(v), cp[:, i:i + 1])
        with nc.Block() as block:
            @block.sync
            def _(sync):
                fc_rep = bass.AP(fconsts.tensor, 0, [[0, 128], [1, len(FCONSTS)]])
                ctx.inc(sync.dma_start(out=cp[:, 0:len(FCONSTS)], in_=fc_rep), "ld", 16)

            emit_block_D(ctx, block, x_in, x_out,
                         {"wg": wg, "wu": wu, "wd": wd},
                         {"ident_b": ident_b})

            @block.gpsimd
            def _(gpsimd):
                ctx.wait(gpsimd, "st")
    return nc




def emit_block_C(ctx, block, x_in, x_out, w_aps, b_aps, consts, scratch):
    sb, ps = ctx.sb, ctx.ps
    xt = sb["xt"]; stat = sb["stat"]; idb = sb["idb"]; sq = sb["sq"]
    lnout = sb["lnout"]; c0st = sb["c0st"]; wtile = sb["wtile"]; wtb = sb["wtb"]
    wbT = sb["wbT"]; v_sb = sb["v_sb"]; xch = sb["xch"]; qkblk = sb["qkblk"]
    attT = sb["attT"]; stg = sb["stg"]; stgb = sb["stgb"]
    pmq = sb["pmq"]; ccol = sb["ccol"]; acolS = sb["acolS"]; alph = sb["alph"]
    bvb = sb["bvb"]; bpb = sb["bpb"]; bcol = sb["bcol"]
    onef = sb["onef"]; onefr = sb["onefr"]
    dn = stg_dn = None
    ystg = sb["ystg"]
    p0, p1, p2, p3 = ps["p0"], ps["p1"], ps["p2"], ps["p3"]
    pb0, pb1 = ps["pb0"], ps["pb1"]
    xlnT_d, qT_d, kT_d, yT_d, cscr = scratch
    dn = stg[0:1, 512:1024]
    avs = stg[0:64, 0:512]
    H = 16

    @block.sync
    def _(sync):
        ctx.inc(sync.dma_start(out=idb[:], in_=consts["ident_b"]), "ld", 16)
        for w, off in (("bq", 0), ("bk", 8)):
            for k in range(8):
                ctx.inc(sync.dma_start(out=bcol[:, off + k:off + k + 1],
                                       in_=b_aps[w][k * 128:(k + 1) * 128]), "ld", 16)
        ctx.inc(sync.dma_start(out=stg[0:1, 0:1024], in_=b_aps["bv"]), "ld", 16)

    ld_i = ctx.cnt["ld"]

    @block.vector
    def _(vector):
        ctx.wait(vector, "ld", ld_i)
        vector.memset(onef[:], 1.0)
        vector.memset(onefr[:], 1.0)
        ctx.inc(vector.memset(v_sb[:], 1.0), "ve", 1)

    ve_ones = ctx.cnt["ve"]

    @block.tensor
    def _(tensor):
        ctx.wait(tensor, "ve", ve_ones)
        tensor.matmul(p0[:], onefr[0:1, :], stg[0:1, 0:512], start=True, stop=True)
        ctx.inc(tensor.matmul(p1[:], onefr[0:1, :], stg[0:1, 512:1024],
                              start=True, stop=True), "pe", 1)

    pe_b = ctx.cnt["pe"]

    @block.scalar
    def _(scalar):
        ctx.wait(scalar, "pe", pe_b)
        scalar.copy(bvb[:, 0:512], p0[:])
        ctx.inc(scalar.copy(bvb[:, 512:1024], p1[:]), "se", 1)

    se_bv = ctx.cnt["se"]

    @block.sync
    def _(sync):
        ctx.wait(sync, "se", se_bv)
        ctx.inc(sync.dma_start(out=stg[0:1, 0:1024], in_=b_aps["bp"]), "ld", 16)

    ld_bp = ctx.cnt["ld"]

    @block.tensor
    def _(tensor):
        ctx.wait(tensor, "ld", ld_bp)
        tensor.matmul(p2[:], onefr[0:1, :], stg[0:1, 0:512], start=True, stop=True)
        ctx.inc(tensor.matmul(p3[:], onefr[0:1, :], stg[0:1, 512:1024],
                              start=True, stop=True), "pe", 1)

    pe_b2 = ctx.cnt["pe"]

    @block.scalar
    def _(scalar):
        ctx.wait(scalar, "pe", pe_b2)
        scalar.copy(bpb[:, 0:512], p2[:])
        ctx.inc(scalar.copy(bpb[:, 512:1024], p3[:]), "se", 1)

    ctx.rec("c_binit", "se")

    for j in range(NT):
        bi = j % 2
        sc = bi * 4
        xv = xt[:, bi * 1024:(bi + 1) * 1024]
        lv = lnout[:, bi * 1024:(bi + 1) * 1024]
        cv = c0st[:, bi * 1024:(bi + 1) * 1024]

        @block.sync
        def _(sync, j=j, xv=xv):
            ctx.wait(sync, "ve", ctx.at("c0_apply_%d" % (j - 2)))
            ctx.inc(sync.dma_start(out=xv, in_=x_in[j * 128:(j + 1) * 128, :]), "ld", 16)

        ld_now = ctx.cnt["ld"]
        emit_ln_stats(ctx, block, xv, sq[:, 0:1024], stat, sc, D, ld_wait=ld_now)
        se_rs = ctx.cnt["se"]

        @block.vector
        def _(vector, xv=xv, lv=lv, se_rs=se_rs, j=j, sc=sc):
            ctx.wait(vector, "se", se_rs)
            ctx.wait(vector, "pe", ctx.at("c0_tr_%d" % (j - 2)))
            ctx.inc(vector.tensor_scalar(lv, xv, stat[:, sc:sc + 1], stat[:, sc + 2:sc + 3],
                                         ALU.subtract, ALU.mult), "ve", 1)
            ctx.rec("c0_apply_%d" % j, "ve")

        ve_ln = ctx.cnt["ve"]

        @block.tensor
        def _(tensor, lv=lv, j=j, ve_ln=ve_ln):
            ctx.wait(tensor, "ve", ve_ln)
            ctx.wait(tensor, "se", ctx.at("c0_ev_%d" % (j - 2)))
            pbank = pb0 if (j % 2 == 0) else pb1
            for k in range(8):
                i = tensor.transpose(pbank[:, k * 128:(k + 1) * 128],
                                     lv[:, k * 128:(k + 1) * 128], idb[:])
            ctx.inc(i, "pe", 1)
            ctx.rec("c0_tr_%d" % j, "pe")

        pe_tr = ctx.cnt["pe"]

        @block.scalar
        def _(scalar, j=j, pe_tr=pe_tr, cv=cv):
            ctx.wait(scalar, "pe", pe_tr)
            ctx.wait(scalar, "st", ctx.at("c0_st_%d" % (j - 2)))
            pbank = pb0 if (j % 2 == 0) else pb1
            ctx.inc(scalar.copy(cv, pbank[:]), "se", 1)
            ctx.rec("c0_ev_%d" % j, "se")

        se_ev = ctx.cnt["se"]

        @block.sync
        def _(sync, j=j, se_ev=se_ev, cv=cv):
            ctx.wait(sync, "se", se_ev)
            for k in range(8):
                ctx.inc(sync.dma_start(
                    out=xlnT_d[k * 128:(k + 1) * 128, j * 128:(j + 1) * 128],
                    in_=cv[:, k * 128:(k + 1) * 128]), "st", 16)
            ctx.rec("c0_st_%d" % j, "st")

    ctx.rec("c0_allst", "st")

    for wi, wname in enumerate(("Wq", "Wk", "Wv", "Wp")):
        for n in range(64):
            i, k = n // 8, n % 8
            bi = n % 2
            wv = wtile[:, bi * 128:(bi + 1) * 128]
            wb = wtb[:, bi * 128:(bi + 1) * 128]

            @block.sync
            def _(sync, wname=wname, i=i, k=k, n=n, wv=wv):
                ctx.wait(sync, "se", ctx.at("c1_abs_%d" % (n - 2)))
                ctx.wait(sync, "ve", ctx.at("c1_cv_%d" % (n - 2)))
                ctx.inc(sync.dma_start(
                    out=wv, in_=w_aps[wname][i * 128:(i + 1) * 128,
                                             k * 128:(k + 1) * 128]), "ld", 16)

            ld_now = ctx.cnt["ld"]

            @block.scalar
            def _(scalar, wv=wv, n=n, wi=wi, ld_now=ld_now):
                ctx.wait(scalar, "ld", ld_now)
                ctx.inc(scalar.activation(sq[:, 0:128], wv, AF.Abs,
                                          bias=0.0, scale=1.0,
                                          accum_out=acolS[:, wi * 64 + n:wi * 64 + n + 1]),
                        "se", 1)
                ctx.rec("c1_abs_%d" % n, "se")

            se_abs = ctx.cnt["se"]

            @block.vector
            def _(vector, wv=wv, wb=wb, n=n, se_abs=se_abs):
                ctx.wait(vector, "se", se_abs)
                ctx.wait(vector, "pe", ctx.at("c1_tr_%d" % (n - 2)))
                ctx.inc(vector.tensor_copy(wb, wv), "ve", 1)
                ctx.rec("c1_cv_%d" % n, "ve")

            ve_cv = ctx.cnt["ve"]

            @block.tensor
            def _(tensor, wb=wb, n=n, ve_cv=ve_cv):
                ctx.wait(tensor, "ve", ve_cv)
                ctx.wait(tensor, "se", ctx.at("c1_sg_%d" % (n - 2)))
                pbank = pb0 if (n % 2 == 0) else pb1
                ctx.inc(tensor.transpose(pbank[:, 0:128], wb, idb[:]), "pe", 1)
                ctx.rec("c1_tr_%d" % n, "pe")

            pe_now = ctx.cnt["pe"]

            @block.scalar
            def _(scalar, wi=wi, i=i, k=k, n=n, pe_now=pe_now):
                ctx.wait(scalar, "pe", pe_now)
                pbank = pb0 if (n % 2 == 0) else pb1
                ctx.inc(scalar.activation(
                    wbT[:, wi * 8192 + k * 1024 + i * 128:
                        wi * 8192 + k * 1024 + (i + 1) * 128],
                    pbank[:, 0:128], AF.Sign, bias=0.0, scale=1.0), "se", 1)
                ctx.rec("c1_sg_%d" % n, "se")

    se_allabs = ctx.cnt["se"]

    @block.vector
    def _(vector, se_allabs=se_allabs):
        ctx.wait(vector, "se", se_allabs)
        ctx.inc(vector.tensor_reduce(pmq[:, 0:4],
                                     acolS[:, 0:256].rearrange("p (a b) -> p a b", b=64),
                                     AX.X, ALU.add), "ve", 1)

    ve_rs = ctx.cnt["ve"]

    @block.tensor
    def _(tensor, ve_rs=ve_rs):
        ctx.wait(tensor, "ve", ve_rs)
        ctx.inc(tensor.matmul(p0[0:1, 0:4], onef[:, 0:1], pmq[:, 0:4],
                              start=True, stop=True), "pe", 1)

    pe_s = ctx.cnt["pe"]

    @block.vector
    def _(vector, pe_s=pe_s):
        ctx.wait(vector, "pe", pe_s)
        ctx.inc(vector.tensor_scalar_mul(stg[0:1, 32:36], p0[0:1, 0:4],
                                         1.0 / (1024.0 * 1024.0)), "ve", 1)
        ctx.wait(vector, "ve")

    ve_al = ctx.cnt["ve"]

    @block.tensor
    def _(tensor, ve_al=ve_al):
        ctx.wait(tensor, "ve", ve_al)
        ctx.inc(tensor.matmul(p0[:, 8:12], onefr[0:1, :], stg[0:1, 32:36],
                              start=True, stop=True), "pe", 1)

    pe_al = ctx.cnt["pe"]

    @block.scalar
    def _(scalar, pe_al=pe_al):
        ctx.wait(scalar, "pe", pe_al)
        ctx.inc(scalar.copy(alph[:, 0:4], p0[:, 8:12]), "se", 1)

    ctx.rec("c1_done_se", "se")

    for tch in range(8):
        bi = tch % 2
        xv = xch[:, bi * 4096:(bi + 1) * 4096]

        @block.sync
        def _(sync, tch=tch, xv=xv):
            ctx.wait(sync, "st", ctx.at("c0_allst"))
            ctx.wait(sync, "pe", ctx.at("c2_done_%d" % (tch - 2)))
            for k in range(8):
                ctx.inc(sync.dma_start(
                    out=xv[:, k * 512:(k + 1) * 512],
                    in_=xlnT_d[k * 128:(k + 1) * 128, tch * 512:(tch + 1) * 512]),
                    "ld", 16)

        ld_now = ctx.cnt["ld"]

        for wi, dst in ((0, qT_d), (1, kT_d)):
            for ot in range(8):
                @block.tensor
                def _(tensor, wi=wi, ot=ot, xv=xv, ld_now=ld_now, tch=tch):
                    if wi == 0 and ot == 0:
                        ctx.wait(tensor, "ld", ld_now)
                        if tch == 0:
                            ctx.wait(tensor, "se", ctx.at("c1_done_se"))
                    ctx.wait(tensor, "ve", ctx.at("c2_ep"))
                    bank = p0 if (ot % 2 == 0) else p1
                    for k in range(8):
                        ii = tensor.matmul(bank[:],
                                           wbT[:, wi * 8192 + k * 1024 + ot * 128:
                                               wi * 8192 + k * 1024 + (ot + 1) * 128],
                                           xv[:, k * 512:(k + 1) * 512],
                                           start=(k == 0), stop=(k == 7))
                    ctx.inc(ii, "pe", 1)

                pe_now = ctx.cnt["pe"]

                @block.vector
                def _(vector, wi=wi, ot=ot, tch=tch, pe_now=pe_now):
                    ctx.wait(vector, "pe", pe_now)
                    ctx.wait(vector, "st", ctx.at("c2_st"))
                    bank = p0 if (ot % 2 == 0) else p1
                    ctx.inc(vector.tensor_scalar(stgb[:, 0:512], bank[:],
                                                 alph[:, wi:wi + 1],
                                                 bcol[:, wi * 8 + ot:wi * 8 + ot + 1],
                                                 ALU.mult, ALU.add), "ve", 1)
                    ctx.wait(vector, "ve")
                    ctx.inc(vector.tensor_reduce(
                        pmq[:, 8 + wi * 64 + ot * 8 + tch:9 + wi * 64 + ot * 8 + tch],
                        stgb[:, 0:512], AX.X, ALU.max,
                        apply_absolute_value=True), "ve", 1)
                    ctx.rec("c2_ep", "ve")

                ve_ep = ctx.cnt["ve"]

                @block.sync
                def _(sync, dst=dst, ot=ot, tch=tch, ve_ep=ve_ep):
                    ctx.wait(sync, "ve", ve_ep)
                    ctx.inc(sync.dma_start(
                        out=dst[ot * 128:(ot + 1) * 128, tch * 512:(tch + 1) * 512],
                        in_=stgb[:, 0:512]), "st", 16)
                    ctx.rec("c2_st", "st")

        for r in range(4):
            tt = tch * 4 + r
            for oc in range(2):
                @block.tensor
                def _(tensor, tt=tt, oc=oc, xv=xv, r=r):
                    ctx.wait(tensor, "ve", ctx.at("c2_vep"))
                    bank = p2 if (oc == 0) else p3
                    for k in range(8):
                        ii = tensor.matmul(bank[:],
                                           xv[:, k * 512 + r * 128:k * 512 + (r + 1) * 128],
                                           wbT[:, 2 * 8192 + k * 1024 + oc * 512:
                                               2 * 8192 + k * 1024 + (oc + 1) * 512],
                                           start=(k == 0), stop=(k == 7))
                    ctx.inc(ii, "pe", 1)

                pe_now = ctx.cnt["pe"]

                @block.vector
                def _(vector, tt=tt, oc=oc, pe_now=pe_now):
                    ctx.wait(vector, "pe", pe_now)
                    bank = p2 if (oc == 0) else p3
                    for hh in range(8):
                        h = oc * 8 + hh
                        ii = vector.scalar_tensor_tensor(
                            v_sb[:, tt * 1040 + 65 * h:tt * 1040 + 65 * h + 64],
                            bank[:, hh * 64:(hh + 1) * 64], alph[:, 2:3],
                            bvb[:, h * 64:(h + 1) * 64], ALU.mult, ALU.add)
                    ctx.inc(ii, "ve", 1)
                    ctx.rec("c2_vep", "ve")

        @block.tensor
        def _(tensor, tch=tch):
            ctx.wait(tensor, "ve", ctx.at("c2_vep"))
            tensor.nop()
            ctx.rec("c2_done_%d" % tch, "pe")

    ctx.rec("c2_allst", "st")

    ve_pm = ctx.cnt["ve"]

    @block.vector
    def _(vector, ve_pm=ve_pm):
        ctx.wait(vector, "ve", ve_pm)
        ctx.inc(vector.tensor_reduce(pmq[:, 4:6],
                                     pmq[:, 8:136].rearrange("p (a b) -> p a b", b=64),
                                     AX.X, ALU.max), "ve", 1)

    ve_red = ctx.cnt["ve"]

    @block.sync
    def _(sync, ve_red=ve_red):
        ctx.wait(sync, "ve", ve_red)
        ctx.inc(sync.dma_start(out=cscr[0:128, 0:2], in_=pmq[:, 4:6]), "st", 16)

    st_c = ctx.cnt["st"]

    @block.sync
    def _(sync, st_c=st_c):
        ctx.wait(sync, "st", st_c)
        cs = bass.AP(cscr.tensor, 0, [[0, 1], [1, 256]])
        ctx.inc(sync.dma_start(out=stg[0:1, 64:320], in_=cs), "ld", 16)

    ld_c = ctx.cnt["ld"]

    @block.vector
    def _(vector, ld_c=ld_c):
        ctx.wait(vector, "ld", ld_c)
        base = stg[0:1, 64:320]
        v2 = bass.AP(base.tensor, base.offset, [base.ap[0], [1, 2], [2, 128]])
        ctx.inc(vector.tensor_reduce(stg[0:1, 0:2], v2, AX.X, ALU.max), "ve", 1)
        ctx.wait(vector, "ve")
        ctx.inc(vector.scalar_tensor_tensor(stg[0:1, 2:3], stg[0:1, 0:1], 8.0,
                                            stg[0:1, 1:2], ALU.mult, ALU.mult), "ve", 1)
        ctx.wait(vector, "ve")
        ctx.inc(vector.tensor_scalar(stg[0:1, 3:4], stg[0:1, 2:3], -1.0, -2.0,
                                     ALU.mult, ALU.add), "ve", 1)

    ve_c = ctx.cnt["ve"]

    @block.tensor
    def _(tensor, ve_c=ve_c):
        ctx.wait(tensor, "ve", ve_c)
        ctx.inc(tensor.matmul(p0[:, 16:17], onefr[0:1, :], stg[0:1, 3:4],
                              start=True, stop=True), "pe", 1)

    pe_c = ctx.cnt["pe"]

    @block.scalar
    def _(scalar, pe_c=pe_c):
        ctx.wait(scalar, "pe", pe_c)
        ctx.inc(scalar.copy(ccol[:, 0:1], p0[:, 16:17]), "se", 1)

    ctx.rec("c2b_done", "se")

    for h in range(H):
        @block.sync
        def _(sync, h=h):
            ctx.wait(sync, "st", ctx.at("c2_allst"))
            ctx.wait(sync, "pe", ctx.at("c3_hdone_%d" % (h - 1)))
            ctx.inc(sync.dma_start(out=qkblk[0:64, 0:4096],
                                   in_=kT_d[h * 64:(h + 1) * 64, :]), "ld", 16)
            ctx.inc(sync.dma_start(out=qkblk[0:64, 4096:8192],
                                   in_=qT_d[h * 64:(h + 1) * 64, :]), "ld", 16)

        ld_h = ctx.cnt["ld"]

        for qc in range(8):
            pe_base = ctx.cnt["pe"]
            se_base = ctx.cnt["se"]
            ve_base = ctx.cnt["ve"]
            st_base = ctx.cnt["st"]

            def pe_of_S(kt, pe_base=pe_base):
                return pe_base + (1 if kt == 0 else 2 * kt)

            @block.tensor
            def _(tensor, h=h, qc=qc, ld_h=ld_h, pe_base=pe_base, se_base=se_base,
                  ve_base=ve_base):
                if qc == 0:
                    tensor.wait_ge(ctx.sems["ld"], ld_h)
                for kt in range(32):
                    bank = p0 if (kt % 2 == 0) else p1
                    if kt >= 2:
                        tensor.wait_ge(ctx.sems["se"], se_base + kt - 1)
                    ii = tensor.matmul(bank[:], qkblk[0:64, kt * 128:(kt + 1) * 128],
                                       qkblk[0:64, 4096 + qc * 512:4096 + (qc + 1) * 512],
                                       start=True, stop=True)
                    ctx.inc(ii, "pe", 1)
                    if kt > 0:
                        if kt == 1:
                            ctx.wait(tensor, "ve", ctx.at("c3_dn_ve"))
                        tensor.wait_ge(ctx.sems["se"], se_base + kt)
                        ii = tensor.matmul(p2[0:65, :],
                                           v_sb[:, (kt - 1) * 1040 + 65 * h:
                                                (kt - 1) * 1040 + 65 * h + 65],
                                           attT[:, ((kt - 1) % 2) * 512:
                                                ((kt - 1) % 2) * 512 + 512],
                                           start=(kt == 1), stop=False)
                        ctx.inc(ii, "pe", 1)
                tensor.wait_ge(ctx.sems["se"], se_base + 32)
                ii = tensor.matmul(p2[0:65, :],
                                   v_sb[:, 31 * 1040 + 65 * h:31 * 1040 + 65 * h + 65],
                                   attT[:, 512:1024],
                                   start=False, stop=True)
                ctx.inc(ii, "pe", 1)
                tensor.wait_ge(ctx.sems["ve"], ve_base + 2)
                ctx.inc(tensor.matmul(p3[0:64, :], onefr[0:1, 0:64], dn[0:1, :],
                                      start=True, stop=True), "pe", 1)

            pe_av = pe_base + 64
            pe_dn = ctx.cnt["pe"]

            @block.scalar
            def _(scalar, h=h, pe_base=pe_base, se_base=se_base, pe_av=pe_av):
                if h == 0:
                    ctx.wait(scalar, "se", ctx.at("c2b_done"))
                for kt in range(32):
                    bank = p0 if (kt % 2 == 0) else p1
                    scalar.wait_ge(ctx.sems["pe"], pe_of_S(kt))
                    ctx.inc(scalar.activation(
                        attT[:, (kt % 2) * 512:(kt % 2) * 512 + 512],
                        bank[:], AF.Exp, bias=ccol[:, 0:1], scale=1.0), "se", 1)
                scalar.wait_ge(ctx.sems["pe"], pe_av)
                ctx.wait(scalar, "ve", ctx.at("c3_ymul_ve"))
                ctx.inc(scalar.copy(avs[0:64, :], p2[0:64, :]), "se", 1)
                ctx.rec("c3_avs_se", "se")

            se_avs = ctx.cnt["se"]

            @block.vector
            def _(vector, pe_av=pe_av, se_avs=se_avs, pe_dn=pe_dn):
                vector.wait_ge(ctx.sems["pe"], pe_av)
                ctx.inc(vector.tensor_copy(dn[0:1, :], p2[64:65, :]), "ve", 1)
                ctx.wait(vector, "ve")
                ctx.inc(vector.reciprocal(dn[0:1, :], dn[0:1, :]), "ve", 1)
                ctx.rec("c3_dn_ve", "ve")
                vector.wait_ge(ctx.sems["se"], se_avs)
                vector.wait_ge(ctx.sems["pe"], pe_dn)
                ctx.wait(vector, "st", ctx.at("c3_yst"))
                ctx.inc(vector.tensor_mul(ystg[0:64, :], avs[0:64, :], p3[0:64, :]),
                        "ve", 1)
                ctx.rec("c3_ymul_ve", "ve")

            ve_y = ctx.cnt["ve"]

            @block.sync
            def _(sync, h=h, qc=qc, ve_y=ve_y):
                ctx.wait(sync, "ve", ve_y)
                ctx.inc(sync.dma_start(
                    out=yT_d[h * 64:(h + 1) * 64, qc * 512:(qc + 1) * 512],
                    in_=ystg[0:64, :]), "st", 16)
                ctx.rec("c3_yst", "st")

        ctx.rec("c3_hdone_%d" % h, "pe")

    ctx.rec("c3_allst", "st")

    for tt in range(NT):
        bi = tt % 2
        yv = xch[:, bi * 4096:bi * 4096 + 1024]
        xv = xt[:, bi * 1024:(bi + 1) * 1024]

        @block.sync
        def _(sync, tt=tt, yv=yv, xv=xv):
            ctx.wait(sync, "st", ctx.at("c3_allst"))
            ctx.wait(sync, "pe", ctx.at("c4_mm_%d" % (tt - 2)))
            ctx.wait(sync, "ve", ctx.at("c4_add_%d" % (tt - 2)))
            for k in range(8):
                ctx.inc(sync.dma_start(
                    out=yv[:, k * 128:(k + 1) * 128],
                    in_=yT_d[k * 128:(k + 1) * 128, tt * 128:(tt + 1) * 128]), "ld", 16)
            ctx.inc(sync.dma_start(out=xv, in_=x_in[tt * 128:(tt + 1) * 128, :]), "ld", 16)

        ld_now = ctx.cnt["ld"]

        for oc in range(2):
            @block.tensor
            def _(tensor, tt=tt, oc=oc, yv=yv, ld_now=ld_now):
                if oc == 0:
                    ctx.wait(tensor, "ld", ld_now)
                ctx.wait(tensor, "ve", ctx.at("c4_add_%d" % (tt - 1)))
                bank = p0 if oc == 0 else p1
                for k in range(8):
                    ii = tensor.matmul(bank[:],
                                       yv[:, k * 128:(k + 1) * 128],
                                       wbT[:, 3 * 8192 + k * 1024 + oc * 512:
                                           3 * 8192 + k * 1024 + (oc + 1) * 512],
                                       start=(k == 0), stop=(k == 7))
                ctx.inc(ii, "pe", 1)
                if oc == 1:
                    ctx.rec("c4_mm_%d" % tt, "pe")

        pe_now = ctx.cnt["pe"]

        @block.vector
        def _(vector, tt=tt, pe_now=pe_now, xv=xv):
            ctx.wait(vector, "pe", pe_now)
            ctx.wait(vector, "st", ctx.at("c4_store_%d" % (tt - 1)))
            bank = p0
            vector.scalar_tensor_tensor(stg[:, 0:512], p0[:],
                                        alph[:, 3:4], bpb[:, 0:512],
                                        ALU.mult, ALU.add)
            ctx.inc(vector.scalar_tensor_tensor(stg[:, 512:1024], p1[:],
                                                alph[:, 3:4], bpb[:, 512:1024],
                                                ALU.mult, ALU.add), "ve", 1)
            ctx.wait(vector, "ve")
            ctx.inc(vector.tensor_add(stg[:, 0:1024], stg[:, 0:1024], xv), "ve", 1)
            ctx.rec("c4_add_%d" % tt, "ve")

        ve_o = ctx.cnt["ve"]

        @block.sync
        def _(sync, tt=tt, ve_o=ve_o):
            ctx.wait(sync, "ve", ve_o)
            ctx.inc(sync.dma_start(out=x_out[tt * 128:(tt + 1) * 128, :],
                                   in_=stg[:, 0:1024]), "st", 16)
            ctx.rec("c4_store_%d" % tt, "st")


def alloc_C(ctx):
    ctx.sbuf("xt", [128, 2048], F32)
    ctx.sbuf("sq", [128, 1024], F32)
    ctx.sbuf("lnout", [128, 2048], BF16)
    ctx.sbuf("c0st", [128, 2048], BF16)
    ctx.sbuf("stat", [128, 8], F32)
    ctx.sbuf("idb", [128, 128], BF16)
    ctx.sbuf("wtile", [128, 256], F32)
    ctx.sbuf("wtb", [128, 256], BF16)
    ctx.sbuf("wbT", [128, 4 * 8192], BF16)
    ctx.sbuf("v_sb", [128, 32 * 1040], BF16)
    ctx.sbuf("xch", [128, 8192], BF16)
    ctx.sbuf("qkblk", [64, 8192], BF16)
    ctx.sbuf("attT", [128, 1024], BF16)
    ctx.sbuf("stg", [128, 1024], F32)
    ctx.sbuf("stgb", [128, 1024], BF16)
    ctx.sbuf("pmq", [128, 144], F32)
    ctx.sbuf("ccol", [128, 4], F32)
    ctx.sbuf("acolS", [128, 256], F32)
    ctx.sbuf("alph", [128, 8], F32)
    ctx.sbuf("bvb", [128, 1024], F32)
    ctx.sbuf("bpb", [128, 1024], F32)
    ctx.sbuf("bcol", [128, 24], F32)
    ctx.sbuf("onef", [128, 2], F32)
    ctx.sbuf("onefr", [1, 128], F32)
    ctx.sbuf("ystg", [64, 512], BF16)


def build_C():
    nc = bass.Bass("TRN2", target_bir_lowering=False)
    x_in = nc.dram_tensor("x_in", [S, D], F32, kind="ExternalInput").ap()
    aps = {}
    for w in ("Wq", "Wk", "Wv", "Wp"):
        aps[w] = nc.dram_tensor(w, [D, D], F32, kind="ExternalInput").ap()
    bps = {}
    for b in ("bq", "bk", "bv", "bp"):
        bps[b] = nc.dram_tensor(b, [D], F32, kind="ExternalInput").ap()
    ident_b = nc.dram_tensor("ident_b", [128, 128], BF16, kind="ExternalInput").ap()
    fconsts = nc.dram_tensor("fconsts", [1, 64], F32, kind="ExternalInput").ap()
    x_out = nc.dram_tensor("x_out", [S, D], F32, kind="ExternalOutput").ap()
    xlnT_d = nc.dram_tensor("xlnT_d", [D, S], BF16, kind="Internal").ap()
    qT_d = nc.dram_tensor("qT_d", [D, S], BF16, kind="Internal").ap()
    kT_d = nc.dram_tensor("kT_d", [D, S], BF16, kind="Internal").ap()
    yT_d = nc.dram_tensor("yT_d", [D, S], BF16, kind="Internal").ap()
    cscr = nc.dram_tensor("cscr", [128, 2], F32, kind="Internal").ap()

    ctx = Ctx(nc)
    with ctx.es:
        alloc_shared(ctx)
        alloc_C(ctx)
        cp = ctx.sb["cpool"]
        for i, v in enumerate(FCONSTS):
            nc.const_aps.register(F32, float(v), cp[:, i:i + 1])
        with nc.Block() as block:
            @block.sync
            def _(sync):
                fc_rep = bass.AP(fconsts.tensor, 0, [[0, 128], [1, len(FCONSTS)]])
                ctx.inc(sync.dma_start(out=cp[:, 0:len(FCONSTS)], in_=fc_rep), "ld", 16)

            emit_block_C(ctx, block, x_in, x_out, aps, bps,
                         {"ident_b": ident_b},
                         (xlnT_d, qT_d, kT_d, yT_d, cscr))

            @block.gpsimd
            def _(gpsimd):
                ctx.wait(gpsimd, "st")
    return nc



INNER = 512
HID2 = 5460
HID = 2730
UCH = [(0, 512), (512, 512), (1024, 512), (1536, 512), (2048, 512),
       (2560, 170), (2730, 342), (3072, 512), (3584, 512), (4096, 512),
       (4608, 512), (5120, 340)]


def emit_battn(ctx, block, sub, x_in, xi_in, xi_out, w_aps, consts, do_ln, yT_d):
    sb, ps = ctx.sb, ctx.ps
    xt = sb["xt"]; stat = sb["stat"]; idb = sb["idb"]; sq = sb["sq"]
    htile = sb["htile"]
    hbt = sb["hbt"]
    hT = sb["hT"]
    wqkv_sb = sb["wqkv_sb"]
    wo_sb = sb["wo_sb"]
    mcur = sb["mcur"]
    ropes = sb["ropes"]
    qsb = sb["qsb"]
    sqf = sb["sqf"]
    ssb = sb["ssb"]
    qn = sb["qn"]
    kn = sb["kn"]
    rot = sb["rot"]
    qrotT = sb["qrotT"]
    krotT = sb["krotT"]
    v_sb = sb["v_sb"]
    ystg = sb["ystg"]; ybuf = sb["ybuf"]
    satt = sb["satt"]
    attb = sb["attb"]
    attT = sb["attT"]
    den = sb["den"]
    obuf = sb["obuf"]
    onefr = sb["onefr"]
    qsrow = sb["qsrow"]
    p0, p1, p2, p3 = ps["p0"], ps["p1"], ps["p2"], ps["p3"]
    pb0, pb1 = ps["pb0"], ps["pb1"]

    @block.sync
    def _(sync):
        ctx.inc(sync.dma_start(out=idb[:], in_=consts["ident_b"]), "ld", 16)

    @block.gpsimd
    def _(gpsimd):
        for k in range(8):
            ctx.inc(gpsimd.dma_start(out=wqkv_sb[:, k * 1536:(k + 1) * 1536],
                                     in_=w_aps["wqkv"][k * 128:(k + 1) * 128, :]),
                    "gld", 16)
        for r in range(4):
            ctx.inc(gpsimd.dma_start(out=wo_sb[:, r * 1024:(r + 1) * 1024],
                                     in_=w_aps["wo"][r * 128:(r + 1) * 128, :]),
                    "gld", 16)
        qs_rep = bass.AP(w_aps["qs"].tensor, w_aps["qs"].offset,
                         [[0, 1], [0, 8], [1, 64]])
        ks_rep = bass.AP(w_aps["ks"].tensor, w_aps["ks"].offset,
                         [[0, 1], [0, 8], [1, 64]])
        ctx.inc(gpsimd.dma_start(out=qsrow[0:1, 0:512], in_=qs_rep), "gld", 16)
        ctx.inc(gpsimd.dma_start(out=qsrow[0:1, 512:1024], in_=ks_rep), "gld", 16)

    st_w = ctx.cnt["gld"]
    ld_w = ctx.cnt["ld"]

    @block.vector
    def _(vector):
        ctx.wait(vector, "ld", ld_w)
        ctx.inc(vector.memset(onefr[:], 1.0), "ve", 1)

    ve_one = ctx.cnt["ve"]

    @block.tensor
    def _(tensor):
        ctx.wait(tensor, "ve", ve_one)
        ctx.wait(tensor, "gld", st_w)
        tensor.matmul(p0[:], onefr[0:1, :], qsrow[0:1, 0:512], start=True, stop=True)
        ctx.inc(tensor.matmul(p1[:], onefr[0:1, :], qsrow[0:1, 512:1024],
                              start=True, stop=True), "pe", 1)

    pe_qs = ctx.cnt["pe"]

    @block.scalar
    def _(scalar):
        ctx.wait(scalar, "pe", pe_qs)
        scalar.copy(qsb[:, 0:512], p0[:])
        ctx.inc(scalar.copy(qsb[:, 512:1024], p1[:]), "se", 1)

    ctx.rec("ba_init", "se")

    for j in range(NT):
        @block.sync
        def _(sync, j=j):
            ctx.wait(sync, "ve", ctx.at("ba_xt_free_%d" % (j - 1)))
            if do_ln:
                ctx.inc(sync.dma_start(out=xt[:, 0:1024],
                                       in_=x_in[j * 128:(j + 1) * 128, :]), "ld", 16)
            else:
                ctx.inc(sync.dma_start(out=xt[:, 0:1024],
                                       in_=xi_in[j * 128:(j + 1) * 128, :]), "ld", 16)

        ld_now = ctx.cnt["ld"]

        if do_ln:
            emit_ln_stats(ctx, block, xt[:, 0:1024], sq[:, 0:1024], stat, 0, D,
                          ld_wait=ld_now)
            se_rs = ctx.cnt["se"]

            @block.vector
            def _(vector, se_rs=se_rs, j=j):
                ctx.wait(vector, "se", se_rs)
                ctx.wait(vector, "st", ctx.at("ba_xist_%d" % (j - 1)))
                ctx.inc(vector.tensor_scalar(xt[:, 1024:2048], xt[:, 0:1024],
                                             stat[:, 0:1], stat[:, 2:3],
                                             ALU.subtract, ALU.mult), "ve", 1)

            ve_xi = ctx.cnt["ve"]

            @block.sync
            def _(sync, j=j, ve_xi=ve_xi):
                ctx.wait(sync, "ve", ve_xi)
                ctx.inc(sync.dma_start(out=xi_in[j * 128:(j + 1) * 128, :],
                                       in_=xt[:, 1024:2048]), "st", 16)
                ctx.rec("ba_xist_%d" % j, "st")
            xiview = xt[:, 1024:2048]
        else:
            xiview = xt[:, 0:1024]

        @block.vector
        def _(vector, j=j, xiview=xiview):
            if do_ln:
                ctx.wait(vector, "ve")
            else:
                ctx.wait(vector, "ld", ctx.cnt["ld"])
            ctx.wait(vector, "pe", ctx.at("ba_htr_%d" % (j - 1)))
            if j == 0:
                vector.memset(sb["zcol"][:], 0.0)
            ctx.inc(vector.tensor_copy(hbt[:], xiview[:, 0:1024]), "ve", 1)
            ctx.rec("ba_xt_free_%d" % j, "ve")

        ve_h = ctx.cnt["ve"]

        @block.tensor
        def _(tensor, ve_h=ve_h, j=j):
            ctx.wait(tensor, "ve", ve_h)
            ctx.wait(tensor, "se", ctx.at("ba_pbev"))
            for k in range(8):
                ii = tensor.transpose(pb0[:, (k % 4) * 128:(k % 4) * 128 + 128] if k < 4
                                      else pb1[:, (k % 4) * 128:(k % 4) * 128 + 128],
                                      hbt[:, k * 128:(k + 1) * 128], idb[:])
            ctx.inc(ii, "pe", 1)
            ctx.rec("ba_htr_%d" % j, "pe")

        pe_tr = ctx.cnt["pe"]

        @block.scalar
        def _(scalar, pe_tr=pe_tr, j=j):
            ctx.wait(scalar, "pe", pe_tr)
            pc_r = sb["prevcol"][:, ((j + 1) % 2) * 4:((j + 1) % 2) * 4 + 4]
            pc_w = sb["prevcol"][:, (j % 2) * 4:(j % 2) * 4 + 4]
            for k in range(8):
                pbank = pb0 if k < 4 else pb1
                pslc = pbank[:, (k % 4) * 128:(k % 4) * 128 + 128]
                if k < 4:
                    ii = scalar.copy(hT[:, k * 128:(k + 1) * 128], pslc)
                else:
                    scalar.copy(hT[:, k * 128 + 1:(k + 1) * 128],
                                pbank[:, (k % 4) * 128:(k % 4) * 128 + 127])
                    if j == 0:
                        scalar.copy(hT[:, k * 128:k * 128 + 1], sb["zcol"][:])
                    else:
                        scalar.copy(hT[:, k * 128:k * 128 + 1],
                                    pc_r[:, k - 4:k - 3])
                    ii = scalar.copy(pc_w[:, k - 4:k - 3],
                                     pbank[:, (k % 4) * 128 + 127:(k % 4) * 128 + 128])
            ctx.inc(ii, "se", 1)
            ctx.rec("ba_pbev", "se")

        se_hT = ctx.cnt["se"]

        @block.sync
        def _(sync, j=j):
            ctx.wait(sync, "ve", ctx.at("ba_rope_used_%d" % (j - 1)))
            for ri, nm in enumerate(("rope_cq", "rope_sq", "rope_ck", "rope_sk")):
                ap = consts[nm]
                rep = bass.AP(ap.tensor, ap.offset + j * 128 * 64,
                              [[64, 128], [0, 8], [1, 64]])
                ctx.inc(sync.dma_start(out=ropes[:, ri * 512:(ri + 1) * 512], in_=rep),
                        "ld", 16)

        ld_rope = ctx.cnt["ld"]

        @block.tensor
        def _(tensor, j=j, se_hT=se_hT):
            if j == 0:
                ctx.wait(tensor, "se", ctx.at("ba_init"))
            ctx.wait(tensor, "se", se_hT)
            ctx.wait(tensor, "ve", ctx.at("ba_qk_done_%d" % (j - 1)))
            ctx.wait(tensor, "ve", ctx.at("ba_v_done_%d" % (j - 1)))
            for wi, bank in ((0, p0), (1, p1), (2, p2)):
                for k in range(8):
                    ii = tensor.matmul(bank[:], hT[:, k * 128:(k + 1) * 128],
                                       wqkv_sb[:, k * 1536 + wi * 512:
                                               k * 1536 + (wi + 1) * 512],
                                       start=(k == 0), stop=(k == 7))
            ctx.inc(ii, "pe", 1)

        pe_qkv = ctx.cnt["pe"]

        @block.vector
        def _(vector, j=j, pe_qkv=pe_qkv):
            ctx.wait(vector, "pe", pe_qkv)
            ctx.inc(vector.tensor_copy(v_sb[:, j * 512:(j + 1) * 512], p2[:]), "ve", 1)
            ctx.rec("ba_v_done_%d" % j, "ve")

        for qk in range(2):
            bank = p0 if qk == 0 else p1
            nbuf = qn if qk == 0 else kn
            dstT = qrotT if qk == 0 else krotT

            se_base_qk = ctx.cnt["se"]
            ve_base_qk = ctx.cnt["ve"]

            @block.scalar
            def _(scalar, bank=bank, pe_qkv=pe_qkv, qk=qk, j=j, ve_base_qk=ve_base_qk):
                ctx.wait(scalar, "pe", pe_qkv)
                if qk == 0:
                    ctx.wait(scalar, "ve", ctx.at("ba_sqf_used_%d_%d" % (j - 1, 1)))
                else:
                    ctx.wait(scalar, "ve", ctx.at("ba_sqf_used_%d_%d" % (j, 0)))
                ctx.inc(scalar.activation(sqf[:], bank[:], AF.Square,
                                          bias=0.0, scale=1.0), "se", 1)
                sc = qk * 8
                scalar.wait_ge(ctx.sems["ve"], ve_base_qk + 2)
                ctx.inc(scalar.activation(ssb[:, 16 + sc:24 + sc], ssb[:, sc:sc + 8],
                                          AF.Ln, bias=0.0, scale=1.0), "se", 1)
                ctx.wait(scalar, "se")
                ctx.inc(scalar.activation(ssb[:, 16 + sc:24 + sc], ssb[:, 16 + sc:24 + sc],
                                          AF.Exp, bias=0.0, scale=-0.5), "se", 1)

            se_nr = ctx.cnt["se"]

            @block.vector
            def _(vector, qk=qk, se_nr=se_nr, bank=bank, nbuf=nbuf, ld_rope=ld_rope,
                  se_base_qk=se_base_qk, j=j):
                vector.wait_ge(ctx.sems["se"], se_base_qk + 1)
                sc = qk * 8
                ctx.inc(vector.tensor_reduce(ssb[:, sc:sc + 8],
                                             sqf[:].rearrange("p (a b) -> p a b", b=64),
                                             AX.X, ALU.add), "ve", 1)
                ctx.rec("ba_sqf_used_%d_%d" % (j, qk), "ve")
                ctx.wait(vector, "ve")
                ctx.inc(vector.tensor_scalar_max(ssb[:, sc:sc + 8],
                                                 ssb[:, sc:sc + 8], 1e-24), "ve", 1)
                vector.wait_ge(ctx.sems["se"], se_nr)
                for h in range(8):
                    ii = vector.tensor_scalar_mul(nbuf[:, h * 64:(h + 1) * 64],
                                                  bank[:, h * 64:(h + 1) * 64],
                                                  ssb[:, 16 + sc + h:17 + sc + h])
                ctx.inc(ii, "ve", 1)
                ctx.rec("ba_qk_done_%d" % 0, "ve")
                ctx.wait(vector, "ve")
                ctx.inc(vector.tensor_mul(nbuf[:], nbuf[:], qsb[:, qk * 512:(qk + 1) * 512]),
                        "ve", 1)
                ctx.wait(vector, "ve")
                ctx.wait(vector, "ld", ld_rope)
                for h in range(8):
                    vector.tensor_scalar_mul(rot[:, h * 64:h * 64 + 32],
                                             nbuf[:, h * 64 + 32:h * 64 + 64], -1.0)
                    ii = vector.tensor_copy(rot[:, h * 64 + 32:h * 64 + 64],
                                            nbuf[:, h * 64:h * 64 + 32])
                ctx.inc(ii, "ve", 1)
                ctx.wait(vector, "ve")
                co = ropes[:, (qk * 2) * 512:(qk * 2 + 1) * 512]
                si = ropes[:, (qk * 2 + 1) * 512:(qk * 2 + 2) * 512]
                vector.tensor_mul(satt[:, 0:512], nbuf[:], co)
                ctx.inc(vector.tensor_mul(rot[:], rot[:], si), "ve", 1)
                ctx.wait(vector, "ve")
                ctx.wait(vector, "pe", ctx.at("ba_attbtr"))
                ctx.inc(vector.tensor_add(attb[:, 0:512], satt[:, 0:512], rot[:]),
                        "ve", 1)

            ve_rope = ctx.cnt["ve"]

            @block.tensor
            def _(tensor, qk=qk, j=j, ve_rope=ve_rope):
                ctx.wait(tensor, "ve", ve_rope)
                ctx.wait(tensor, "se", ctx.at("ba_pbev"))
                for r in range(4):
                    ii = tensor.transpose(pb0[:, r * 128:(r + 1) * 128],
                                          attb[:, r * 128:(r + 1) * 128], idb[:])
                ctx.inc(ii, "pe", 1)
                ctx.rec("ba_attbtr", "pe")

            pe_qtr = ctx.cnt["pe"]

            @block.scalar
            def _(scalar, qk=qk, j=j, pe_qtr=pe_qtr, dstT=dstT):
                ctx.wait(scalar, "pe", pe_qtr)
                for r in range(4):
                    ii = scalar.copy(dstT[:, r * 4096 + j * 128:r * 4096 + (j + 1) * 128],
                                     pb0[:, r * 128:(r + 1) * 128])
                ctx.inc(ii, "se", 1)
                ctx.rec("ba_pbev", "se")

        ctx.rec("ba_qk_done_%d" % j, "ve")
        ctx.rec("ba_rope_used_%d" % j, "ve")
        ctx.rec("ba_attb_pe", "pe")

    ctx.rec("ba_prep_done_se", "se")
    ctx.rec("ba_prep_done_ve", "ve")

    for tt in range(NT):
        w = tt // 4
        qp = tt % 4
        w0 = (w == 0)
        mb = tt % 2

        @block.sync
        def _(sync, tt=tt, qp=qp, w0=w0, mb=mb):
            ctx.wait(sync, "ve", ctx.at("ba_mask_used_%d" % (tt - 2)))
            mi = (4 + qp) if w0 else qp
            ctx.inc(sync.dma_start(out=mcur[:, mb * 1024:(mb + 1) * 1024],
                                   in_=consts["lmask_b"][mi * 128:(mi + 1) * 128, :]),
                    "ld", 16)

        ld_mask = ctx.cnt["ld"]

        for h in range(8):
            r, po = h // 2, (h % 2) * 64
            B = ctx.cnt["pe"]
            V = ctx.cnt["ve"]
            SS = ctx.cnt["se"]
            first_iter = (tt == 0 and h == 0)

            @block.tensor
            def _(tensor, tt=tt, h=h, r=r, po=po, w=w, w0=w0, B=B, V=V, SS=SS,
                  first_iter=first_iter):
                if first_iter:
                    ctx.wait(tensor, "se", ctx.at("ba_prep_done_se"))
                    ctx.wait(tensor, "ve", ctx.at("ba_prep_done_ve"))
                if not first_iter:
                    tensor.wait_ge(ctx.sems["ve"], V - 3)
                if not w0:
                    tensor.matmul(p0[:], qrotT[po:po + 64, r * 4096 + tt * 128:
                                               r * 4096 + (tt + 1) * 128],
                                  krotT[po:po + 64, r * 4096 + (w - 1) * 512:
                                        r * 4096 + w * 512],
                                  start=True, stop=True)
                ii = tensor.matmul(p1[:], qrotT[po:po + 64, r * 4096 + tt * 128:
                                                r * 4096 + (tt + 1) * 128],
                                   krotT[po:po + 64, r * 4096 + w * 512:
                                         r * 4096 + (w + 1) * 512],
                                   start=True, stop=True)
                ctx.inc(ii, "pe", 1)
                tensor.wait_ge(ctx.sems["ve"], V + 4)
                tensor.wait_ge(ctx.sems["se"], max(SS - 1, ctx.at("ba_pbev") if first_iter else 0, 0))
                for s_ in range(8):
                    ii = tensor.transpose(pb0[:, (s_ % 4) * 128:(s_ % 4) * 128 + 128] if s_ < 4
                                          else pb1[:, (s_ % 4) * 128:(s_ % 4) * 128 + 128],
                                          attb[:, s_ * 128:(s_ + 1) * 128], idb[:])
                ctx.inc(ii, "pe", 1)
                tensor.wait_ge(ctx.sems["se"], SS + 2)
                srange = range(4, 8) if w0 else range(8)
                first = 4 if w0 else 0
                for s_ in srange:
                    kt = (w - 1) * 4 + s_
                    ii = tensor.matmul(p3[0:64, 0:128],
                                       v_sb[:, kt * 512 + h * 64:kt * 512 + (h + 1) * 64],
                                       attT[:, s_ * 128:(s_ + 1) * 128],
                                       start=(s_ == first), stop=(s_ == 7))
                ctx.inc(ii, "pe", 1)

            @block.vector
            def _(vector, qp=qp, w0=w0, mb=mb, ld_mask=ld_mask, tt=tt, h=h,
                  B=B, V=V, SS=SS, first_iter=first_iter):
                vector.wait_ge(ctx.sems["pe"], B + 1)
                ctx.wait(vector, "ld", ld_mask)
                if not first_iter:
                    vector.wait_ge(ctx.sems["se"], max(SS - 2, 0))
                if not w0:
                    vector.tensor_add(satt[:, 0:512], p0[:],
                                      mcur[:, mb * 1024:mb * 1024 + 512])
                ii = vector.tensor_add(satt[:, 512:1024], p1[:],
                                       mcur[:, mb * 1024 + 512:(mb + 1) * 1024])
                ctx.inc(ii, "ve", 1)
                ctx.rec("ba_mask_used_%d" % tt, "ve")
                ctx.wait(vector, "ve")
                lo = 512 if w0 else 0
                ctx.inc(vector.tensor_reduce(den[:, 0:1], satt[:, lo:1024], AX.X,
                                             ALU.max, negate=True), "ve", 1)
                vector.wait_ge(ctx.sems["se"], SS + 1)
                ctx.inc(vector.reciprocal(den[:, 2:3], den[:, 1:2]), "ve", 1)
                ctx.wait(vector, "ve")
                if not first_iter:
                    vector.wait_ge(ctx.sems["pe"], max(B - 1, 0))
                if w0:
                    vector.memset(attb[:, 0:512], 0.0)
                ctx.inc(vector.tensor_scalar_mul(attb[:, lo:1024], satt[:, lo:1024],
                                                 den[:, 2:3]), "ve", 1)

            @block.scalar
            def _(scalar, tt=tt, h=h, w0=w0, B=B, V=V, SS=SS):
                lo = 512 if w0 else 0
                scalar.wait_ge(ctx.sems["ve"], V + 2)
                ctx.inc(scalar.activation(satt[:, lo:1024], satt[:, lo:1024], AF.Exp,
                                          bias=den[:, 0:1], scale=1.0,
                                          accum_out=den[:, 1:2]), "se", 1)
                scalar.wait_ge(ctx.sems["pe"], B + 2)
                for s_ in range(8):
                    pbank = pb0 if s_ < 4 else pb1
                    ii = scalar.copy(attT[:, s_ * 128:(s_ + 1) * 128],
                                     pbank[:, (s_ % 4) * 128:(s_ % 4) * 128 + 128])
                ctx.inc(ii, "se", 1)
                scalar.wait_ge(ctx.sems["pe"], B + 3)
                ctx.wait(scalar, "st", ctx.at("ba_yst_st"))
                ctx.inc(scalar.copy(ystg[0:64, 0:128], p3[0:64, 0:128]), "se", 1)

            se_y = ctx.cnt["se"]

            @block.sync
            def _(sync, tt=tt, h=h, se_y=se_y):
                ctx.wait(sync, "se", se_y)
                ctx.inc(sync.dma_start(
                    out=yT_d[h * 64:(h + 1) * 64, tt * 128:(tt + 1) * 128],
                    in_=ystg[0:64, 0:128]), "st", 16)
                ctx.rec("ba_yst_st", "st")

    ctx.rec("ba_attn_done_st", "st")

    for tt in range(NT):
        bb = tt % 2

        @block.sync
        def _(sync, tt=tt, bb=bb):
            ctx.wait(sync, "st", ctx.at("ba_attn_done_st"))
            ctx.wait(sync, "ve", ctx.at("ba_ores_%d" % (tt - 1)))
            ctx.wait(sync, "pe", ctx.at("ba_womm_%d" % (tt - 2)))
            for r in range(4):
                ctx.inc(sync.dma_start(
                    out=ybuf[:, bb * 512 + r * 128:bb * 512 + (r + 1) * 128],
                    in_=yT_d[r * 128:(r + 1) * 128, tt * 128:(tt + 1) * 128]), "ld", 16)
            ctx.inc(sync.dma_start(out=xt[:, bb * 1024:(bb + 1) * 1024],
                                   in_=xi_in[tt * 128:(tt + 1) * 128, :]), "ld", 16)

        ld_now = ctx.cnt["ld"]

        for oc in range(2):
            @block.tensor
            def _(tensor, tt=tt, oc=oc, ld_now=ld_now, bb=bb):
                if oc == 0:
                    ctx.wait(tensor, "ld", ld_now)
                ctx.wait(tensor, "ve", ctx.at("ba_ores_%d" % (tt - 1)))
                bank = p0 if oc == 0 else p1
                for r in range(4):
                    ii = tensor.matmul(bank[:],
                                       ybuf[:, bb * 512 + r * 128:bb * 512 + (r + 1) * 128],
                                       wo_sb[:, r * 1024 + oc * 512:r * 1024 + (oc + 1) * 512],
                                       start=(r == 0), stop=(r == 3))
                ctx.inc(ii, "pe", 1)
                if oc == 1:
                    ctx.rec("ba_womm_%d" % tt, "pe")

        pe_wo = ctx.cnt["pe"]

        @block.vector
        def _(vector, tt=tt, pe_wo=pe_wo, ld_now=ld_now, bb=bb):
            ctx.wait(vector, "pe", pe_wo)
            ctx.wait(vector, "ld", ld_now)
            ctx.wait(vector, "st", ctx.at("ba_ost_%d" % (tt - 1)))
            vector.tensor_add(obuf[:, 0:512], p0[:], xt[:, bb * 1024:bb * 1024 + 512])
            ctx.inc(vector.tensor_add(obuf[:, 512:1024], p1[:],
                                      xt[:, bb * 1024 + 512:(bb + 1) * 1024]),
                    "ve", 1)
            ctx.rec("ba_ores_%d" % tt, "ve")

        ve_o = ctx.cnt["ve"]

        @block.sync
        def _(sync, tt=tt, ve_o=ve_o):
            ctx.wait(sync, "ve", ve_o)
            ctx.inc(sync.dma_start(out=xi_out[tt * 128:(tt + 1) * 128, :],
                                   in_=obuf[:, 0:1024]), "st", 16)
            ctx.rec("ba_ost_%d" % tt, "st")


def alloc_battn(ctx):
    ctx.sbuf("xt", [128, 2048], F32)
    ctx.sbuf("sq", [128, 1024], F32)
    ctx.sbuf("stat", [128, 8], F32)
    ctx.sbuf("idb", [128, 128], BF16)
    ctx.sbuf("htile", [128, 1024], F32)
    ctx.sbuf("hbt", [128, 1024], BF16)
    ctx.sbuf("hT", [128, 1024], BF16)
    ctx.sbuf("wqkv_sb", [128, 8 * 1536], BF16)
    ctx.sbuf("wo_sb", [128, 4 * 1024], BF16)
    ctx.sbuf("mcur", [128, 2048], BF16)
    ctx.sbuf("ropes", [128, 4 * 512], F32)
    ctx.sbuf("qsb", [128, 1024], F32)
    ctx.sbuf("sqf", [128, 512], F32)
    ctx.sbuf("ssb", [128, 32], F32)
    ctx.sbuf("qn", [128, 512], F32)
    ctx.sbuf("kn", [128, 512], F32)
    ctx.sbuf("rot", [128, 512], F32)
    ctx.sbuf("qrotT", [128, 4 * 4096], BF16)
    ctx.sbuf("krotT", [128, 4 * 4096], BF16)
    ctx.sbuf("v_sb", [128, 32 * 512], BF16)
    ctx.sbuf("ystg", [64, 128], BF16)
    ctx.sbuf("ybuf", [128, 1024], BF16)
    ctx.sbuf("satt", [128, 1024], F32)
    ctx.sbuf("attb", [128, 1024], BF16)
    ctx.sbuf("attT", [128, 1024], BF16)
    ctx.sbuf("den", [128, 4], F32)
    ctx.sbuf("obuf", [128, 1024], F32)
    ctx.sbuf("onefr", [1, 128], F32)
    ctx.sbuf("qsrow", [1, 1024], F32)
    ctx.sbuf("prevcol", [128, 8], BF16)
    ctx.sbuf("zcol", [128, 1], BF16)


DEBUG_BA = False

def build_battn(sub, do_ln):
    nc = bass.Bass("TRN2", target_bir_lowering=False)
    x_in = nc.dram_tensor("x_in", [S, D], F32, kind="ExternalInput").ap()
    xi_in = nc.dram_tensor("xi_in", [S, D], F32,
                           kind="ExternalInput" if not do_ln else "ExternalOutput").ap()
    wqkv = nc.dram_tensor("wqkv", [D, 1536], F32, kind="ExternalInput").ap()
    qs = nc.dram_tensor("qs", [64], F32, kind="ExternalInput").ap()
    ks = nc.dram_tensor("ks", [64], F32, kind="ExternalInput").ap()
    wo = nc.dram_tensor("wo", [INNER, D], F32, kind="ExternalInput").ap()
    ident_b = nc.dram_tensor("ident_b", [128, 128], BF16, kind="ExternalInput").ap()
    lmask_b = nc.dram_tensor("lmask_b", [1024, 1024], BF16, kind="ExternalInput").ap()
    rope_cq = nc.dram_tensor("rope_cq", [S, 64], F32, kind="ExternalInput").ap()
    rope_sq = nc.dram_tensor("rope_sq", [S, 64], F32, kind="ExternalInput").ap()
    rope_ck = nc.dram_tensor("rope_ck", [S, 64], F32, kind="ExternalInput").ap()
    rope_sk = nc.dram_tensor("rope_sk", [S, 64], F32, kind="ExternalInput").ap()
    fconsts = nc.dram_tensor("fconsts", [1, 64], F32, kind="ExternalInput").ap()
    xi_out = nc.dram_tensor("xi_out", [S, D], F32, kind="ExternalOutput").ap()
    yT_d = nc.dram_tensor("yT_d", [INNER, S], BF16,
                          kind="ExternalOutput" if DEBUG_BA else "Internal").ap()
    if DEBUG_BA:
        dbg_qT = nc.dram_tensor("dbg_qT", [128, 4 * 4096], BF16, kind="ExternalOutput").ap()
        dbg_kT = nc.dram_tensor("dbg_kT", [128, 4 * 4096], BF16, kind="ExternalOutput").ap()
        dbg_v = nc.dram_tensor("dbg_v", [128, 32 * 512], BF16, kind="ExternalOutput").ap()
        dbg_htile = nc.dram_tensor("dbg_htile", [128, 1024], F32, kind="ExternalOutput").ap()
        dbg_hbt = nc.dram_tensor("dbg_hbt", [128, 1024], BF16, kind="ExternalOutput").ap()

    ctx = Ctx(nc)
    with ctx.es:
        alloc_shared(ctx)
        alloc_battn(ctx)
        cp = ctx.sb["cpool"]
        for i, v in enumerate(FCONSTS):
            nc.const_aps.register(F32, float(v), cp[:, i:i + 1])
        with nc.Block() as block:
            @block.sync
            def _(sync):
                fc_rep = bass.AP(fconsts.tensor, 0, [[0, 128], [1, len(FCONSTS)]])
                ctx.inc(sync.dma_start(out=cp[:, 0:len(FCONSTS)], in_=fc_rep), "ld", 16)

            emit_battn(ctx, block, sub, x_in, xi_in, xi_out,
                       {"wqkv": wqkv, "qs": qs, "ks": ks, "wo": wo},
                       {"ident_b": ident_b, "lmask_b": lmask_b,
                        "rope_cq": rope_cq, "rope_sq": rope_sq,
                        "rope_ck": rope_ck, "rope_sk": rope_sk,
                        "dbg_htile": dbg_htile if DEBUG_BA else None,
                        "dbg_hbt": dbg_hbt if DEBUG_BA else None}, do_ln, yT_d)

            @block.gpsimd
            def _(gpsimd):
                ctx.wait(gpsimd, "st")
                ctx.wait(gpsimd, "ve")
                ctx.wait(gpsimd, "se")
                if DEBUG_BA:
                    ctx.inc(gpsimd.dma_start(out=dbg_qT, in_=ctx.sb["qrotT"][:]), "st", 16)
                    ctx.inc(gpsimd.dma_start(out=dbg_kT, in_=ctx.sb["krotT"][:]), "st", 16)
                    ctx.inc(gpsimd.dma_start(out=dbg_v, in_=ctx.sb["v_sb"][:]), "st", 16)
                    ctx.wait(gpsimd, "st")
    return nc



def emit_bffn(ctx, block, sub, xi_in, xi_out, x1_in, w_aps, consts, final):
    sb, ps = ctx.sb, ctx.ps
    xt = sb["xt"]; stat = sb["stat"]; idb = sb["idb"]; sq = sb["sq"]
    htile = sb["htile"]; hbt = sb["hbt"]; hT = sb["hT"]
    w1_sb = sb["w1_sb"]; w2_sb = sb["w2_sb"]
    b1b = sb["b1b"]; lgb = sb["lgb"]; lbb = sb["lbb"]; b2b = sb["b2b"]
    u = sb["u"]; hmb = sb["hmb"]; hmT = sb["hmT"]
    obuf = sb["obuf"]; onefr = sb["onefr"]; brow = sb["brow"]
    p0, p1, p2, p3 = ps["p0"], ps["p1"], ps["p2"], ps["p3"]
    pb0, pb1 = ps["pb0"], ps["pb1"]

    @block.sync
    def _(sync):
        ctx.inc(sync.dma_start(out=idb[:], in_=consts["ident_b"]), "ld", 16)

    @block.gpsimd
    def _(gpsimd):
        for k in range(8):
            ctx.inc(gpsimd.dma_start(out=w1_sb[:, k * HID2:(k + 1) * HID2],
                                     in_=w_aps["w1"][k * 128:(k + 1) * 128, :]),
                    "gld", 16)
        for k in range(21):
            ctx.inc(gpsimd.dma_start(out=w2_sb[:, k * 1024:(k + 1) * 1024],
                                     in_=w_aps["w2"][k * 128:(k + 1) * 128, :]),
                    "gld", 16)
        ctx.inc(gpsimd.dma_start(out=w2_sb[0:42, 21 * 1024:22 * 1024],
                                 in_=w_aps["w2"][21 * 128:2730, :]), "gld", 16)

    st_w = ctx.cnt["gld"]

    @block.vector
    def _(vector):
        ctx.inc(vector.memset(onefr[:], 1.0), "ve", 1)

    ve_one = ctx.cnt["ve"]

    bjobs = []
    for (u0, ulen) in UCH:
        bjobs.append((w_aps["b1"], u0, ulen, b1b, u0))
    for oc in range(2):
        bjobs.append((w_aps["lg"], oc * 512, 512, lgb, oc * 512))
        bjobs.append((w_aps["lb"], oc * 512, 512, lbb, oc * 512))
        bjobs.append((w_aps["b2"], oc * 512, 512, b2b, oc * 512))
    for bi, (ap, off, ln_, dstb, doff) in enumerate(bjobs):
        @block.sync
        def _(sync, ap=ap, off=off, ln_=ln_, bi=bi):
            ctx.wait(sync, "pe", ctx.at("bf_bmm_%d" % (bi - 1)))
            ctx.inc(sync.dma_start(out=brow[0:1, 0:ln_], in_=ap[off:off + ln_]),
                    "ld", 16)

        ld_b = ctx.cnt["ld"]

        @block.tensor
        def _(tensor, bi=bi, ln_=ln_, ld_b=ld_b):
            ctx.wait(tensor, "ld", ld_b)
            if bi == 0:
                ctx.wait(tensor, "ve", ve_one)
            ctx.wait(tensor, "se", ctx.at("bf_bev_%d" % (bi - 1)))
            ctx.inc(tensor.matmul(p0[:, 0:ln_], onefr[0:1, :], brow[0:1, 0:ln_],
                                  start=True, stop=True), "pe", 1)
            ctx.rec("bf_bmm_%d" % bi, "pe")

        pe_b = ctx.cnt["pe"]

        @block.scalar
        def _(scalar, bi=bi, dstb=dstb, doff=doff, ln_=ln_, pe_b=pe_b):
            ctx.wait(scalar, "pe", pe_b)
            ctx.inc(scalar.copy(dstb[:, doff:doff + ln_], p0[:, 0:ln_]), "se", 1)
            ctx.rec("bf_bev_%d" % bi, "se")

    ctx.rec("bf_init_se", "se")

    for j in range(NT):
        @block.sync
        def _(sync, j=j):
            ctx.wait(sync, "ve", ctx.at("bf_xt_used_%d" % (j - 1)))
            ctx.inc(sync.dma_start(out=xt[:, 0:1024],
                                   in_=xi_in[j * 128:(j + 1) * 128, :]), "ld", 16)
            if j == 0:
                ctx.inc(sync.dma_start(out=htile[1:128, 512:1024],
                                       in_=xi_in[0:127, 512:1024]), "ld", 16)
            else:
                ctx.inc(sync.dma_start(out=htile[0:128, 512:1024],
                                       in_=xi_in[j * 128 - 1:j * 128 + 127, 512:1024]),
                        "ld", 16)
            if final:
                ctx.inc(sync.dma_start(out=xt[:, 1024:2048],
                                       in_=x1_in[j * 128:(j + 1) * 128, :]), "ld", 16)

        ld_now = ctx.cnt["ld"]

        @block.vector
        def _(vector, j=j, ld_now=ld_now):
            ctx.wait(vector, "ld", ld_now)
            ctx.wait(vector, "pe", ctx.at("bf_htr_%d" % (j - 1)))
            if j == 0:
                vector.memset(htile[0:1, 512:1024], 0.0)
            ctx.inc(vector.tensor_copy(htile[:, 0:512], xt[:, 0:512]), "ve", 1)

        ve_ht = ctx.cnt["ve"]
        emit_ln_stats(ctx, block, htile[:], sq[:, 0:1024], stat, 0, D, ve_wait=ve_ht)
        se_rs = ctx.cnt["se"]

        @block.vector
        def _(vector, j=j, se_rs=se_rs):
            ctx.wait(vector, "se", se_rs)
            ctx.inc(vector.tensor_scalar(sq[:, 0:1024], htile[:], stat[:, 0:1],
                                         stat[:, 2:3], ALU.subtract, ALU.mult),
                    "ve", 1)
            ctx.wait(vector, "ve")
            ctx.inc(vector.tensor_mul(sq[:, 0:1024], sq[:, 0:1024], lgb[:]), "ve", 1)
            ctx.wait(vector, "ve")
            ctx.inc(vector.tensor_add(hbt[:], sq[:, 0:1024], lbb[:]), "ve", 1)

        ve_h = ctx.cnt["ve"]

        @block.tensor
        def _(tensor, ve_h=ve_h, j=j):
            ctx.wait(tensor, "ve", ve_h)
            ctx.wait(tensor, "se", ctx.at("bf_pbev"))
            for k in range(8):
                ii = tensor.transpose(pb0[:, (k % 4) * 128:(k % 4) * 128 + 128] if k < 4
                                      else pb1[:, (k % 4) * 128:(k % 4) * 128 + 128],
                                      hbt[:, k * 128:(k + 1) * 128], idb[:])
            ctx.inc(ii, "pe", 1)
            ctx.rec("bf_htr_%d" % j, "pe")

        pe_tr = ctx.cnt["pe"]

        @block.scalar
        def _(scalar, pe_tr=pe_tr):
            ctx.wait(scalar, "pe", pe_tr)
            for k in range(8):
                pbank = pb0 if k < 4 else pb1
                ii = scalar.copy(hT[:, k * 128:(k + 1) * 128],
                                 pbank[:, (k % 4) * 128:(k % 4) * 128 + 128])
            ctx.inc(ii, "se", 1)
            ctx.rec("bf_pbev", "se")

        se_hT = ctx.cnt["se"]

        for ci, (u0, ulen) in enumerate(UCH):
            @block.tensor
            def _(tensor, ci=ci, u0=u0, ulen=ulen, se_hT=se_hT, j=j):
                if ci == 0:
                    ctx.wait(tensor, "se", se_hT)
                    if j == 0:
                        ctx.wait(tensor, "se", ctx.at("bf_init_se"))
                bank = p0 if (ci % 2 == 0) else p1
                ctx.wait(tensor, "ve", ctx.at("bf_uev_%s" % ("e" if ci % 2 == 0 else "o")))
                if ci == 0 and j == 0:
                    ctx.wait(tensor, "gld", st_w)
                for k in range(8):
                    ii = tensor.matmul(bank[:, 0:ulen],
                                       hT[:, k * 128:(k + 1) * 128],
                                       w1_sb[:, k * HID2 + u0:k * HID2 + u0 + ulen],
                                       start=(k == 0), stop=(k == 7))
                ctx.inc(ii, "pe", 1)

            pe_u = ctx.cnt["pe"]

            @block.vector
            def _(vector, ci=ci, u0=u0, ulen=ulen, pe_u=pe_u):
                ctx.wait(vector, "pe", pe_u)
                bank = p0 if (ci % 2 == 0) else p1
                ctx.inc(vector.scalar_tensor_tensor(u[:, u0:u0 + ulen], bank[:, 0:ulen],
                                                    1.0, b1b[:, u0:u0 + ulen],
                                                    ALU.bypass, ALU.add), "ve", 1)
                ctx.rec("bf_uev_%s" % ("e" if ci % 2 == 0 else "o"), "ve")

        ve_u = ctx.cnt["ve"]

        @block.scalar
        def _(scalar, ve_u=ve_u):
            ctx.wait(scalar, "ve", ve_u)
            ctx.inc(scalar.activation(u[:, HID:HID2], u[:, HID:HID2], AF.Gelu,
                                      bias=0.0, scale=1.0), "se", 1)

        se_g = ctx.cnt["se"]

        @block.vector
        def _(vector, se_g=se_g):
            ctx.wait(vector, "se", se_g)
            ctx.wait(vector, "pe", ctx.at("bf_hmtr_0"))
            ctx.inc(vector.tensor_mul(hmb[:], u[:, 0:HID], u[:, HID:HID2]), "ve", 1)

        ve_hm = ctx.cnt["ve"]

        for g0 in range(0, 22, 8):
            gt = list(range(g0, min(g0 + 8, 22)))

            @block.tensor
            def _(tensor, gt=gt, g0=g0, ve_hm=ve_hm, j=j):
                if g0 == 0:
                    ctx.wait(tensor, "ve", ve_hm)
                ctx.wait(tensor, "se", ctx.at("bf_pbev"))
                for k in gt:
                    w_ = min(128, HID - k * 128)
                    slot = (k % 8) // 2
                    pbank = pb0 if k % 2 == 0 else pb1
                    ii = tensor.transpose(pbank[0:w_, slot * 128:slot * 128 + 128],
                                          hmb[:, k * 128:k * 128 + w_], idb[:])
                ctx.inc(ii, "pe", 1)
                if g0 == 16:
                    ctx.rec("bf_hmtr_0", "pe")

            pe_g = ctx.cnt["pe"]

            @block.scalar
            def _(scalar, gt=gt, pe_g=pe_g):
                ctx.wait(scalar, "pe", pe_g)
                for k in gt:
                    w_ = min(128, HID - k * 128)
                    slot = (k % 8) // 2
                    pbank = pb0 if k % 2 == 0 else pb1
                    ii = scalar.copy(hmT[0:w_, k * 128:(k + 1) * 128],
                                     pbank[0:w_, slot * 128:slot * 128 + 128])
                ctx.inc(ii, "se", 1)
                ctx.rec("bf_pbev", "se")

        se_hmT = ctx.cnt["se"]

        for oc in range(2):
            @block.tensor
            def _(tensor, oc=oc, se_hmT=se_hmT):
                if oc == 0:
                    ctx.wait(tensor, "se", se_hmT)
                ctx.wait(tensor, "ve", ctx.at("bf_res_ve"))
                bank = p2 if oc == 0 else p3
                for k in range(22):
                    kw = min(128, HID - k * 128)
                    ii = tensor.matmul(bank[:],
                                       hmT[0:kw, k * 128:(k + 1) * 128],
                                       w2_sb[0:kw, k * 1024 + oc * 512:
                                             k * 1024 + (oc + 1) * 512],
                                       start=(k == 0), stop=(k == 21))
                ctx.inc(ii, "pe", 1)

        pe_w2 = ctx.cnt["pe"]

        @block.vector
        def _(vector, j=j, pe_w2=pe_w2, ld_now=ld_now):
            ctx.wait(vector, "pe", pe_w2)
            ctx.wait(vector, "st", ctx.at("bf_ost_%d" % (j - 1)))
            vector.scalar_tensor_tensor(obuf[:, 0:512], p2[:], 1.0, b2b[:, 0:512],
                                        ALU.bypass, ALU.add)
            ctx.inc(vector.scalar_tensor_tensor(obuf[:, 512:1024], p3[:], 1.0,
                                                b2b[:, 512:1024], ALU.bypass, ALU.add),
                    "ve", 1)
            ctx.wait(vector, "ve")
            ctx.inc(vector.tensor_add(obuf[:, 0:1024], obuf[:, 0:1024], xt[:, 0:1024]),
                    "ve", 1)
            if final:
                ctx.wait(vector, "ve")
                ctx.inc(vector.tensor_add(obuf[:, 0:1024], obuf[:, 0:1024],
                                          xt[:, 1024:2048]), "ve", 1)
            ctx.rec("bf_res_ve", "ve")
            ctx.rec("bf_xt_used_%d" % j, "ve")

        ve_o = ctx.cnt["ve"]

        @block.sync
        def _(sync, j=j, ve_o=ve_o):
            ctx.wait(sync, "ve", ve_o)
            ctx.inc(sync.dma_start(out=xi_out[j * 128:(j + 1) * 128, :],
                                   in_=obuf[:, 0:1024]), "st", 16)
            ctx.rec("bf_ost_%d" % j, "st")


def alloc_bffn(ctx):
    ctx.sbuf("xt", [128, 2048], F32)
    ctx.sbuf("sq", [128, 1024], F32)
    ctx.sbuf("stat", [128, 8], F32)
    ctx.sbuf("idb", [128, 128], BF16)
    ctx.sbuf("htile", [128, 1024], F32)
    ctx.sbuf("hbt", [128, 1024], BF16)
    ctx.sbuf("hT", [128, 1024], BF16)
    ctx.sbuf("w1_sb", [128, 8 * HID2], BF16)
    ctx.sbuf("w2_sb", [128, 22 * 1024], BF16)
    ctx.sbuf("b1b", [128, HID2], BF16)
    ctx.sbuf("lgb", [128, 1024], BF16)
    ctx.sbuf("lbb", [128, 1024], BF16)
    ctx.sbuf("b2b", [128, 1024], BF16)
    ctx.sbuf("u", [128, HID2], F32)
    ctx.sbuf("hmb", [128, HID], BF16)
    ctx.sbuf("hmT", [128, 22 * 128], BF16)
    ctx.sbuf("obuf", [128, 1024], F32)
    ctx.sbuf("onefr", [1, 128], F32)
    ctx.sbuf("brow", [1, 512], F32)


def build_bffn(sub, final):
    nc = bass.Bass("TRN2", target_bir_lowering=False)
    xi_in = nc.dram_tensor("xi_in", [S, D], F32, kind="ExternalInput").ap()
    x1_in = nc.dram_tensor("x1_in", [S, D], F32, kind="ExternalInput").ap()
    w1 = nc.dram_tensor("w1", [D, HID2], F32, kind="ExternalInput").ap()
    b1 = nc.dram_tensor("b1", [HID2], F32, kind="ExternalInput").ap()
    w2 = nc.dram_tensor("w2", [HID, D], F32, kind="ExternalInput").ap()
    b2 = nc.dram_tensor("b2", [D], F32, kind="ExternalInput").ap()
    lg = nc.dram_tensor("lg", [D], F32, kind="ExternalInput").ap()
    lb = nc.dram_tensor("lb", [D], F32, kind="ExternalInput").ap()
    ident_b = nc.dram_tensor("ident_b", [128, 128], BF16, kind="ExternalInput").ap()
    fconsts = nc.dram_tensor("fconsts", [1, 64], F32, kind="ExternalInput").ap()
    xi_out = nc.dram_tensor("xi_out", [S, D], F32, kind="ExternalOutput").ap()

    ctx = Ctx(nc)
    with ctx.es:
        alloc_shared(ctx)
        alloc_bffn(ctx)
        cp = ctx.sb["cpool"]
        for i, v in enumerate(FCONSTS):
            nc.const_aps.register(F32, float(v), cp[:, i:i + 1])
        with nc.Block() as block:
            @block.sync
            def _(sync):
                fc_rep = bass.AP(fconsts.tensor, 0, [[0, 128], [1, len(FCONSTS)]])
                ctx.inc(sync.dma_start(out=cp[:, 0:len(FCONSTS)], in_=fc_rep), "ld", 16)

            emit_bffn(ctx, block, sub, xi_in, xi_out, x1_in,
                      {"w1": w1, "b1": b1, "w2": w2, "b2": b2, "lg": lg, "lb": lb},
                      {"ident_b": ident_b}, final)

            @block.gpsimd
            def _(gpsimd):
                ctx.wait(gpsimd, "st")
    return nc


def emit_boundary(ctx, block, idx):
    tot = {s: ctx.cnt[s] for s in ("ld", "st", "pe", "ve", "se")}

    @block.sync
    def _(sync):
        for s in ("ld", "st", "pe", "ve", "se"):
            if tot[s] > 0:
                sync.wait_ge(ctx.sems[s], tot[s])
        if ctx.cnt["gld"] > 0:
            sync.wait_ge(ctx.sems["gld"], ctx.cnt["gld"])
        for s in ("ld", "st", "pe", "ve", "se"):
            sync.sem_clear(ctx.sems[s])
        sync.sem_inc(ctx.sems["gp"], 1)

    ctx.cnt["gp"] += 1
    gval = ctx.cnt["gp"]

    for engname in ("tensor", "vector", "scalar", "gpsimd"):
        def mk(eng, gval=gval):
            eng.wait_ge(ctx.sems["gp"], gval)
        getattr(block, engname)(mk)
    ctx.reset_counts()


def build_fused(start=0, upto=7, nofinal=False):
    nc = bass.Bass("TRN2", target_bir_lowering=False)
    g = {}
    def ein(name, shape, dtype=F32):
        g[name] = nc.dram_tensor(name, shape, dtype, kind="ExternalInput").ap()
        return g[name]

    def state(name, produced_in, consumed_in):
        prod = [s for s in produced_in if start <= s < upto]
        cons = [s for s in consumed_in if start <= s < upto]
        if not prod and not cons:
            kind = None
        elif prod and (max(consumed_in, default=-1) >= upto or not cons):
            kind = "ExternalOutput"
        elif not prod and cons:
            kind = "ExternalInput"
        else:
            kind = "Internal"
        if kind:
            g[name] = nc.dram_tensor(name, [S, D], F32, kind=kind).ap()

    if start <= 0 < upto or start <= 1 < upto:
        pass
    ein("x_in", [S, D])
    if start <= 0 < upto:
        for w in ("Wg", "Wv", "Wd"):
            ein(w, [D, D])
        for b in ("bg", "bv", "bd"):
            ein(b, [D])
    for sub in range(2):
        if start <= 1 + 2 * sub < upto:
            ein("wqkv%d" % sub, [D, 1536]); ein("qs%d" % sub, [64]); ein("ks%d" % sub, [64])
            ein("wo%d" % sub, [INNER, D])
        if start <= 2 + 2 * sub < upto:
            ein("w1_%d" % sub, [D, HID2]); ein("b1_%d" % sub, [HID2])
            ein("w2_%d" % sub, [HID, D]); ein("b2_%d" % sub, [D])
            ein("lg%d" % sub, [D]); ein("lb%d" % sub, [D])
    if start <= 5 < upto:
        for w in ("CWq", "CWk", "CWv", "CWp"):
            ein(w, [D, D])
        for b in ("cbq", "cbk", "cbv", "cbp"):
            ein(b, [D])
    if start <= 6 < upto:
        ein("wg", [IC, D]); ein("wu", [IC, D]); ein("wd", [D, IC])
    ein("ident_b", [128, 128], BF16); ein("ident_f", [128, 128])
    ein("fconsts", [1, 64]); ein("lmask_b", [1024, 1024], BF16)
    for r in ("rope_cq", "rope_sq", "rope_ck", "rope_sk"):
        ein(r, [S, 64])

    state("xs0", [0], [1, 4])
    state("xi_a", [1, 2], [1, 3])
    state("xi_b", [1, 3], [2, 4])
    state("x2_d", [4], [5])
    state("x3_d", [5], [6])
    if upto >= 7:
        x_out = nc.dram_tensor("x_out", [S, D], F32, kind="ExternalOutput").ap()
    else:
        x_out = None

    def intl(name, shape, dtype=F32):
        g[name] = nc.dram_tensor(name, shape, dtype, kind="Internal").ap()
    if start <= 5 < upto:
        intl("xlnT_d", [D, S], BF16); intl("qT_d", [D, S], BF16)
        intl("kT_d", [D, S], BF16); intl("yT_dC", [D, S], BF16)
        intl("cscr", [128, 2])
    if any(start <= s < upto for s in (1, 3)):
        intl("yT_dB", [INNER, S], BF16)

    ctx = Ctx(nc)
    with ctx.es:
        alloc_shared(ctx)
        cp = ctx.sb["cpool"]
        for i, v in enumerate(FCONSTS):
            nc.const_aps.register(F32, float(v), cp[:, i:i + 1])
        with nc.Block() as block:
            @block.sync
            def _(sync):
                fc_rep = bass.AP(g["fconsts"].tensor, 0, [[0, 128], [1, len(FCONSTS)]])
                ctx.inc(sync.dma_start(out=cp[:, 0:len(FCONSTS)], in_=fc_rep), "ld", 16)

            first = True
            bno = 0

            def boundary():
                nonlocal first, bno
                if not first:
                    pass
                bno += 1
                emit_boundary(ctx, block, bno)

            if start <= 0 < upto:
                ctx.push_scope()
                alloc_common(ctx)
                emit_block_A(ctx, block, g["x_in"], g["xs0"],
                             {"Wg": g["Wg"], "Wv": g["Wv"], "Wd": g["Wd"]},
                             {"bg": g["bg"], "bv": g["bv"], "bd": g["bd"]},
                             {"ident_b": g["ident_b"], "ident_f": g["ident_f"]})
                ctx.pop_scope()
                if upto > 1:
                    boundary()

            for sub in range(2):
                if start <= 1 + 2 * sub < upto:
                    bcst = {"ident_b": g["ident_b"], "lmask_b": g["lmask_b"],
                            "rope_cq": g["rope_cq"], "rope_sq": g["rope_sq"],
                            "rope_ck": g["rope_ck"], "rope_sk": g["rope_sk"],
                            "dbg_htile": None, "dbg_hbt": None}
                    ctx.push_scope()
                    alloc_battn(ctx)
                    emit_battn(ctx, block, sub,
                               g.get("xs0"), g["xi_a"], g["xi_b"],
                               {"wqkv": g["wqkv%d" % sub], "qs": g["qs%d" % sub],
                                "ks": g["ks%d" % sub], "wo": g["wo%d" % sub]},
                               bcst, sub == 0, g["yT_dB"])
                    ctx.pop_scope()
                    if upto > 2 + 2 * sub:
                        boundary()
                if start <= 2 + 2 * sub < upto:
                    ctx.push_scope()
                    alloc_bffn(ctx)
                    emit_bffn(ctx, block, sub, g["xi_b"],
                              g["xi_a"] if sub == 0 else g["x2_d"],
                              g.get("xs0"),
                              {"w1": g["w1_%d" % sub], "b1": g["b1_%d" % sub],
                               "w2": g["w2_%d" % sub], "b2": g["b2_%d" % sub],
                               "lg": g["lg%d" % sub], "lb": g["lb%d" % sub]},
                              {"ident_b": g["ident_b"]},
                              (sub == 1) and not nofinal)
                    ctx.pop_scope()
                    if upto > 3 + 2 * sub:
                        boundary()

            if start <= 5 < upto:
                ctx.push_scope()
                alloc_C(ctx)
                emit_block_C(ctx, block, g["x2_d"], g["x3_d"],
                             {"Wq": g["CWq"], "Wk": g["CWk"], "Wv": g["CWv"], "Wp": g["CWp"]},
                             {"bq": g["cbq"], "bk": g["cbk"], "bv": g["cbv"], "bp": g["cbp"]},
                             {"ident_b": g["ident_b"]},
                             (g["xlnT_d"], g["qT_d"], g["kT_d"], g["yT_dC"], g["cscr"]))
                ctx.pop_scope()
                if upto > 6:
                    boundary()

            if start <= 6 < upto:
                ctx.push_scope()
                alloc_D(ctx)
                emit_block_D(ctx, block, g["x3_d"], x_out,
                             {"wg": g["wg"], "wu": g["wu"], "wd": g["wd"]},
                             {"ident_b": g["ident_b"]})
                ctx.pop_scope()

            @block.sync
            def _(sync):
                ctx.wait(sync, "st")
                ctx.wait(sync, "ld")
    return nc


def kernel(**inputs):
    consts = host_constants()
    fc64 = np.zeros((1, 64), dtype=np.float32)
    fc64[0, :len(FCONSTS)] = np.array(FCONSTS, dtype=np.float32)
    lmask_b = consts["lmask"].astype(np.float32).astype(BF)
    x = np.ascontiguousarray(inputs["L0_attn_x"]).astype(np.float32)
    cores = list(range(8))
    base = {
        "Wg": inputs["L0_attn_Wg"], "Wv": inputs["L0_attn_Wv"], "Wd": inputs["L0_attn_Wd"],
        "bg": inputs["L0_attn_bg"], "bv": inputs["L0_attn_bv"], "bd": inputs["L0_attn_bd"],
        "CWq": inputs["L1_attn_Wq"], "CWk": inputs["L1_attn_Wk"],
        "CWv": inputs["L1_attn_Wv"], "CWp": inputs["L1_attn_Wp"],
        "cbq": inputs["L1_attn_bq"], "cbk": inputs["L1_attn_bk"],
        "cbv": inputs["L1_attn_bv"], "cbp": inputs["L1_attn_bp"],
        "wg": inputs["L1_ffn_w_gate"], "wu": inputs["L1_ffn_w_up"],
        "wd": inputs["L1_ffn_w_down"],
        "ident_b": consts["ident_b"], "ident_f": consts["ident_f"],
        "fconsts": fc64, "lmask_b": lmask_b,
        "rope_cq": consts["rope_cq"], "rope_sq": consts["rope_sq"],
        "rope_ck": consts["rope_ck"], "rope_sk": consts["rope_sk"],
    }
    for sub in range(2):
        base["wqkv%d" % sub] = np.ascontiguousarray(inputs["L0_ffn_wqkv"][sub])
        base["qs%d" % sub] = np.ascontiguousarray(inputs["L0_ffn_q_scale"][sub])
        base["ks%d" % sub] = np.ascontiguousarray(inputs["L0_ffn_k_scale"][sub])
        base["wo%d" % sub] = np.ascontiguousarray(inputs["L0_ffn_wo"][sub])
        base["w1_%d" % sub] = np.ascontiguousarray(inputs["L0_ffn_w1"][sub])
        base["b1_%d" % sub] = np.ascontiguousarray(inputs["L0_ffn_b1"][sub])
        base["w2_%d" % sub] = np.ascontiguousarray(inputs["L0_ffn_w2"][sub])
        base["b2_%d" % sub] = np.ascontiguousarray(inputs["L0_ffn_b2"][sub])
        base["lg%d" % sub] = np.ascontiguousarray(inputs["L0_ffn_ln_g"][sub])
        base["lb%d" % sub] = np.ascontiguousarray(inputs["L0_ffn_ln_b"][sub])
    nc = build_fused(0, 7)
    km = [dict(base, x_in=x[c]) for c in cores]
    res = run_bass_kernel_spmd(nc, km, core_ids=cores)
    return np.stack([r["x_out"] for r in res.results], axis=0)


# version 09
# speedup vs baseline: 1.0438x; 1.0001x over previous
import numpy as np
import ml_dtypes
from contextlib import ExitStack

import concourse.bass as bass
import concourse.mybir as mybir
from concourse.bass_utils import run_bass_kernel_spmd

F32 = mybir.dt.float32
BF16 = mybir.dt.bfloat16
AF = mybir.ActivationFunctionType
ALU = mybir.AluOpType
AX = mybir.AxisListType

S = 4096
D = 1024
NT = S // 128
NCD = D // 128
BF = ml_dtypes.bfloat16
FCONSTS = [1e-5, -0.5, 0.5, 2.0, -1.0, -2.0, 1.0 / 1024, 8.0, 0.001, 0.998,
           1e-12, 1.0 / 6.0, 6.0, 0.125, 1.0 / 512, 1.0 / 2816, 1.0 / 64,
           -1e9, 0.25, 0.75, 1.25, 1.75, 2.5, 3.5, 5.0, 3.0, 4.0, 1.5]


def host_constants():
    c = {}
    c["ident_b"] = np.eye(128, dtype=np.float32).astype(BF)
    c["ident_f"] = np.eye(128, dtype=np.float32)
    c["fconsts"] = np.array(FCONSTS, dtype=np.float32).reshape(1, -1)
    DIM_HEAD, WINDOW, n = 64, 512, S
    half = np.arange(0, DIM_HEAD, 2, dtype=np.float32)
    inv_freq = 1.0 / 10000.0 ** (half / DIM_HEAD)
    t = np.arange(n, dtype=np.float32)
    freqs = t[:, None] * inv_freq[None, :]
    freqs = np.concatenate([freqs, freqs], axis=-1)
    base = (half + 0.4 * DIM_HEAD) / (1.4 * DIM_HEAD)
    power = (t - n // 2) / float(WINDOW // 2)
    scale = base[None, :] ** power[:, None]
    scale = np.concatenate([scale, scale], axis=-1)
    cos, sin = np.cos(freqs), np.sin(freqs)
    QK_SCALE = 8.0
    c["rope_cq"] = (cos * scale * QK_SCALE).astype(np.float32)
    c["rope_sq"] = (sin * scale * QK_SCALE).astype(np.float32)
    c["rope_ck"] = (cos / scale).astype(np.float32)
    c["rope_sk"] = (sin / scale).astype(np.float32)
    masks = np.zeros((8, 128, 1024), dtype=np.float32)
    for qp in range(4):
        qi = (np.arange(128) + 128 * qp)[:, None]
        kj = np.arange(1024)[None, :]
        valid = (qi <= kj) & (kj <= qi + 512)
        valid0 = (kj >= 512) & (kj <= qi + 512)
        masks[qp] = np.where(valid, 0.0, -1e9)
        masks[4 + qp] = np.where(valid0, 0.0, -1e9)
    c["lmask"] = masks.reshape(8 * 128, 1024)
    return c


class Ctx:
    def __init__(self, nc):
        self.nc = nc
        self.es = ExitStack()
        self.cnt = {}
        self.sems = {}
        self.sb = {}
        self.ps = {}
        self.mark = {}
        self.scope_id = 0

    def sem(self, name):
        s = self.es.enter_context(self.nc.semaphore(name))
        self.sems[name] = s
        self.cnt[name] = 0
        return s

    def inc(self, instr, name, n=1):
        self.cnt[name] += n
        instr.then_inc(self.sems[name], n)
        return self.cnt[name]

    def wait(self, eng, name, val=None):
        v = self.cnt[name] if val is None else val
        if v > 0:
            eng.wait_ge(self.sems[name], v)

    def rec(self, key, sem):
        self.mark[key] = self.cnt[sem]

    def at(self, key, default=0):
        return self.mark.get(key, default)

    def sbuf(self, name, shape, dtype):
        mangled = name if self.scope_id == 0 else "%s_s%d" % (name, self.scope_id)
        h = self.es.enter_context(self.nc.sbuf_tensor(mangled, shape, dtype))
        self.sb[name] = h
        return h

    def psum(self, name, shape, dtype):
        h = self.es.enter_context(self.nc.psum_tensor(name, shape, dtype))
        self.ps[name] = h
        return h

    def push_scope(self):
        self._scope = ExitStack()
        self._saved_es = self.es
        self.es = self._scope
        self.scope_id += 1

    def pop_scope(self):
        self._scope.close()
        self.es = self._saved_es

    def reset_counts(self):
        for s in ("ld", "st", "pe", "ve", "se"):
            self.cnt[s] = 0
        self.mark = {}


def emit_block_A(ctx, block, x_in, x_out, W_aps, b_aps, consts):
    sb, ps = ctx.sb, ctx.ps
    xlnT = sb["big0"]
    wT = sb["wT"]
    xt = sb["xt"]; sq = sb["sq"]; lnout = sb["lnout"]; stat = sb["stat"]
    bcol = sb["bcol"]; idb = sb["idb"]; idf = sb["idf"]
    wtile = sb["wtile"]; wtb = sb["wtb"]; gbuf = sb["gbuf"]
    afull = sb["afull"]; xsfull = sb["xsfull"]; hfull = sb["hfull"]
    rbuf = sb["rbuf"]; obuf = sb["obuf"]
    p0, p1, p2, p3 = ps["p0"], ps["p1"], ps["p2"], ps["p3"]
    pb0, pb1 = ps["pb0"], ps["pb1"]

    @block.sync
    def _(sync):
        ctx.inc(sync.dma_start(out=idb[:], in_=consts["ident_b"]), "ld", 16)
        ctx.inc(sync.dma_start(out=idf[:], in_=consts["ident_f"]), "ld", 16)
        for w, off in (("bg", 0), ("bv", 8), ("bd", 16)):
            for k in range(8):
                ctx.inc(sync.dma_start(out=bcol[:, off + k:off + k + 1],
                                       in_=b_aps[w][k * 128:(k + 1) * 128]), "ld", 16)

    for j in range(NT):
        bi = j % 2
        sc = bi * 4
        xv = xt[:, bi * 1024:(bi + 1) * 1024]
        lv = lnout[:, bi * 1024:(bi + 1) * 1024]

        @block.sync
        def _(sync, j=j, xv=xv):
            ctx.wait(sync, "ve", ctx.at("a1_apply_%d" % (j - 2)))
            ctx.inc(sync.dma_start(out=xv, in_=x_in[j * 128:(j + 1) * 128, :]), "ld", 16)

        ld_now = ctx.cnt["ld"]

        @block.vector
        def _(vector, xv=xv, ld_now=ld_now, sc=sc):
            ctx.wait(vector, "ld", ld_now)
            ctx.inc(vector.reduce_sum(stat[:, sc:sc + 1], xv, AX.X), "ve", 1)

        @block.scalar
        def _(scalar, xv=xv, ld_now=ld_now, sc=sc, j=j):
            ctx.wait(scalar, "ld", ld_now)
            ctx.inc(scalar.activation(sq[:], xv, AF.Square, bias=0.0, scale=1.0,
                                      accum_out=stat[:, sc + 1:sc + 2]), "se", 1)

        se_sq = ctx.cnt["se"]

        @block.vector
        def _(vector, se_sq=se_sq, sc=sc):
            ctx.wait(vector, "se", se_sq)
            ctx.wait(vector, "ve")
            ctx.inc(vector.tensor_scalar_mul(stat[:, sc:sc + 1], stat[:, sc:sc + 1],
                                             1.0 / D), "ve", 1)
            ctx.wait(vector, "ve")
            ctx.inc(vector.scalar_tensor_tensor(stat[:, sc + 3:sc + 4], stat[:, sc:sc + 1],
                                                1.0, stat[:, sc:sc + 1],
                                                ALU.bypass, ALU.mult), "ve", 1)
            ctx.wait(vector, "ve")
            ctx.inc(vector.scalar_tensor_tensor(stat[:, sc + 2:sc + 3], stat[:, sc + 1:sc + 2],
                                                1.0 / D, stat[:, sc + 3:sc + 4],
                                                ALU.mult, ALU.subtract), "ve", 1)

        ve_var = ctx.cnt["ve"]

        @block.scalar
        def _(scalar, ve_var=ve_var, sc=sc):
            ctx.wait(scalar, "ve", ve_var)
            ctx.inc(scalar.activation(stat[:, sc + 2:sc + 3], stat[:, sc + 2:sc + 3],
                                      AF.Ln, bias=1e-5, scale=1.0), "se", 1)
            ctx.wait(scalar, "se")
            ctx.inc(scalar.activation(stat[:, sc + 2:sc + 3], stat[:, sc + 2:sc + 3],
                                      AF.Exp, bias=0.0, scale=-0.5), "se", 1)

        se_rs = ctx.cnt["se"]

        @block.vector
        def _(vector, xv=xv, lv=lv, se_rs=se_rs, j=j, sc=sc):
            ctx.wait(vector, "se", se_rs)
            ctx.wait(vector, "pe", ctx.at("a1_tr_%d" % (j - 2)))
            i = vector.tensor_scalar(lv, xv, stat[:, sc:sc + 1], stat[:, sc + 2:sc + 3],
                                     ALU.subtract, ALU.mult)
            ctx.inc(i, "ve", 1)
            ctx.rec("a1_apply_%d" % j, "ve")

        ve_ln = ctx.cnt["ve"]

        @block.tensor
        def _(tensor, lv=lv, j=j, ve_ln=ve_ln):
            ctx.wait(tensor, "ve", ve_ln)
            ctx.wait(tensor, "se", ctx.at("a1_evac_%d" % (j - 2)))
            pbank = pb0 if (j % 2 == 0) else pb1
            for k in range(8):
                i = tensor.transpose(pbank[:, k * 128:(k + 1) * 128],
                                     lv[:, k * 128:(k + 1) * 128], idb[:])
            ctx.inc(i, "pe", 1)
            ctx.rec("a1_tr_%d" % j, "pe")

        pe_tr = ctx.cnt["pe"]

        @block.scalar
        def _(scalar, j=j, pe_tr=pe_tr):
            ctx.wait(scalar, "pe", pe_tr)
            pbank = pb0 if (j % 2 == 0) else pb1
            for k in range(8):
                i = scalar.copy(xlnT[:, k * 4096 + j * 128: k * 4096 + (j + 1) * 128],
                                pbank[:, k * 128:(k + 1) * 128])
            ctx.inc(i, "se", 1)
            ctx.rec("a1_evac_%d" % j, "se")

    ctx.rec("a2a_sebase", "se")
    ctx.rec("a2a_pebase", "pe")
    for wi, wname in enumerate(("Wg", "Wv", "Wd")):
        for i in range(8):
            for k in range(8):
                n = wi * 64 + i * 8 + k
                bi = n % 2
                wv = wtile[:, bi * 128:(bi + 1) * 128]
                wb = wtb[:, bi * 128:(bi + 1) * 128]

                @block.sync
                def _(sync, wname=wname, i=i, k=k, n=n, wv=wv):
                    ctx.wait(sync, "ve", ctx.at("a2a_cv_%d" % (n - 2)))
                    ctx.inc(sync.dma_start(
                        out=wv, in_=W_aps[wname][i * 128:(i + 1) * 128,
                                                 k * 128:(k + 1) * 128]), "ld", 16)

                ld_now = ctx.cnt["ld"]

                @block.vector
                def _(vector, wv=wv, wb=wb, ld_now=ld_now, n=n):
                    ctx.wait(vector, "ld", ld_now)
                    ctx.wait(vector, "pe", ctx.at("a2a_tr_%d" % (n - 2),
                                                  ctx.at("a2a_pebase")))
                    ctx.inc(vector.tensor_copy(wb, wv), "ve", 1)
                    ctx.rec("a2a_cv_%d" % n, "ve")

                ve_now = ctx.cnt["ve"]

                @block.tensor
                def _(tensor, wb=wb, ve_now=ve_now, n=n):
                    ctx.wait(tensor, "ve", ve_now)
                    ctx.wait(tensor, "se", ctx.at("a2a_ev_%d" % (n - 2),
                                                  ctx.at("a2a_sebase")))
                    pbank = pb0 if (n % 2 == 0) else pb1
                    ctx.inc(tensor.transpose(pbank[:, 0:128], wb, idb[:]), "pe", 1)
                    ctx.rec("a2a_tr_%d" % n, "pe")

                pe_now = ctx.cnt["pe"]

                @block.scalar
                def _(scalar, wi=wi, i=i, k=k, n=n, pe_now=pe_now):
                    ctx.wait(scalar, "pe", pe_now)
                    pbank = pb0 if (n % 2 == 0) else pb1
                    i2 = scalar.copy(wT[:, wi * 8192 + k * 1024 + i * 128:
                                        wi * 8192 + k * 1024 + (i + 1) * 128],
                                     pbank[:, 0:128])
                    ctx.inc(i2, "se", 1)
                    ctx.rec("a2a_ev_%d" % n, "se")

    ve_all = ctx.cnt["ve"]
    se_all = ctx.cnt["se"]
    for i in range(8):
        for cch in range(8):
            t0 = cch * 512

            @block.tensor
            def _(tensor, i=i, cch=cch, t0=t0):
                if i == 0 and cch == 0:
                    ctx.wait(tensor, "ve", ve_all)
                    ctx.wait(tensor, "se", se_all)
                ctx.wait(tensor, "se", ctx.at("a2b_gate_se"))
                for wi, bank in ((0, p0), (1, p1), (2, p2)):
                    for k in range(8):
                        ii = tensor.matmul(bank[:],
                                           wT[:, wi * 8192 + k * 1024 + i * 128:
                                              wi * 8192 + k * 1024 + (i + 1) * 128],
                                           xlnT[:, k * 4096 + t0:k * 4096 + t0 + 512],
                                           start=(k == 0), stop=(k == 7))
                ctx.inc(ii, "pe", 1)

            pe_now = ctx.cnt["pe"]

            @block.scalar
            def _(scalar, i=i, t0=t0, pe_now=pe_now):
                ctx.wait(scalar, "pe", pe_now)
                ctx.wait(scalar, "ve", ctx.at("a2b_gate_ve"))
                scalar.activation(gbuf[:, 0:512], p0[:], AF.Sigmoid,
                                  bias=bcol[:, 0 + i:1 + i], scale=1.0)
                scalar.activation(gbuf[:, 512:1024], p1[:], AF.Tanh,
                                  bias=bcol[:, 8 + i:9 + i], scale=1.0)
                ctx.inc(scalar.activation(gbuf[:, 1024:1536], p2[:], AF.Sigmoid,
                                          bias=bcol[:, 16 + i:17 + i], scale=1.0), "se", 1)
                ctx.rec("a2b_gate_se", "se")

            se_now = ctx.cnt["se"]

            @block.vector
            def _(vector, t0=t0, se_now=se_now):
                ctx.wait(vector, "se", se_now)
                vector.tensor_tensor(xsfull[:, t0:t0 + 512], gbuf[:, 0:512],
                                     gbuf[:, 512:1024], ALU.mult)
                i2 = vector.tensor_scalar(afull[:, t0:t0 + 512], gbuf[:, 1024:1536],
                                          0.998, 0.001, ALU.mult, ALU.add)
                ctx.inc(i2, "ve", 1)
                ctx.rec("a2b_gate_ve", "ve")

        @block.vector
        def _(vector, i=i):
            ctx.wait(vector, "ve")
            ctx.inc(vector.tensor_tensor_scan(hfull[:], afull[:], xsfull[:], 0.0,
                                              ALU.mult, ALU.add), "ve", 1)

        ve_scan = ctx.cnt["ve"]

        for q in range(8):
            @block.tensor
            def _(tensor, i=i, q=q, ve_scan=ve_scan):
                ctx.wait(tensor, "ve", ve_scan)
                ctx.wait(tensor, "ve", ctx.at("a2b_res_ve"))
                for r in range(4):
                    jt = q * 4 + r
                    ii = tensor.transpose(p3[:, r * 128:(r + 1) * 128],
                                          hfull[:, jt * 128:(jt + 1) * 128], idf[:])
                ctx.inc(ii, "pe", 1)

            pe_now = ctx.cnt["pe"]

            @block.sync
            def _(sync, i=i, q=q):
                ctx.wait(sync, "ve", ctx.at("a2b_res_ve"))
                for r in range(4):
                    jt = q * 4 + r
                    ctx.inc(sync.dma_start(
                        out=rbuf[:, r * 128:(r + 1) * 128],
                        in_=x_in[jt * 128:(jt + 1) * 128, i * 128:(i + 1) * 128]), "ld", 16)

            ld_now = ctx.cnt["ld"]

            @block.vector
            def _(vector, pe_now=pe_now, ld_now=ld_now):
                ctx.wait(vector, "pe", pe_now)
                ctx.wait(vector, "ld", ld_now)
                ctx.wait(vector, "st", ctx.at("a2b_res_st"))
                ctx.inc(vector.tensor_tensor(obuf[:, 0:512], p3[:], rbuf[:, 0:512],
                                             ALU.add), "ve", 1)
                ctx.rec("a2b_res_ve", "ve")

            ve_now = ctx.cnt["ve"]

            @block.sync
            def _(sync, i=i, q=q, ve_now=ve_now):
                ctx.wait(sync, "ve", ve_now)
                for r in range(4):
                    jt = q * 4 + r
                    ctx.inc(sync.dma_start(
                        out=x_out[jt * 128:(jt + 1) * 128, i * 128:(i + 1) * 128],
                        in_=obuf[:, r * 128:(r + 1) * 128]), "st", 16)
                ctx.rec("a2b_res_st", "st")


def alloc_shared(ctx):
    ctx.psum("p0", [128, 512], F32)
    ctx.psum("p1", [128, 512], F32)
    ctx.psum("p2", [128, 512], F32)
    ctx.psum("p3", [128, 512], F32)
    ctx.psum("pb0", [128, 1024], BF16)
    ctx.psum("pb1", [128, 1024], BF16)
    ctx.psum("p4", [128, 512], F32)
    ctx.psum("p5", [128, 512], F32)
    ctx.sbuf("cpool", [128, 64], F32)
    for s in ("ld", "st", "pe", "ve", "se", "gp", "gld"):
        ctx.sem(s)


def alloc_common(ctx):
    ctx.sbuf("big0", [128, 32768], BF16)
    ctx.sbuf("wT", [128, 24576], BF16)
    ctx.sbuf("xt", [128, 2048], F32)
    ctx.sbuf("sq", [128, 1024], F32)
    ctx.sbuf("lnout", [128, 2048], BF16)
    ctx.sbuf("stat", [128, 8], F32)
    ctx.sbuf("bcol", [128, 24], F32)
    ctx.sbuf("idb", [128, 128], BF16)
    ctx.sbuf("idf", [128, 128], F32)
    ctx.sbuf("wtile", [128, 256], F32)
    ctx.sbuf("wtb", [128, 256], BF16)
    ctx.sbuf("gbuf", [128, 1536], F32)
    ctx.sbuf("afull", [128, 4096], F32)
    ctx.sbuf("xsfull", [128, 4096], F32)
    ctx.sbuf("hfull", [128, 4096], F32)
    ctx.sbuf("rbuf", [128, 1024], F32)
    ctx.sbuf("obuf", [128, 1024], F32)


DEBUG_A = False

def build_A():
    nc = bass.Bass("TRN2", target_bir_lowering=False)
    x_in = nc.dram_tensor("x_in", [S, D], F32, kind="ExternalInput").ap()
    Wg = nc.dram_tensor("Wg", [D, D], F32, kind="ExternalInput").ap()
    Wv = nc.dram_tensor("Wv", [D, D], F32, kind="ExternalInput").ap()
    Wd = nc.dram_tensor("Wd", [D, D], F32, kind="ExternalInput").ap()
    bg = nc.dram_tensor("bg", [D], F32, kind="ExternalInput").ap()
    bv = nc.dram_tensor("bv", [D], F32, kind="ExternalInput").ap()
    bd = nc.dram_tensor("bd", [D], F32, kind="ExternalInput").ap()
    ident_b = nc.dram_tensor("ident_b", [128, 128], BF16, kind="ExternalInput").ap()
    ident_f = nc.dram_tensor("ident_f", [128, 128], F32, kind="ExternalInput").ap()
    x_out = nc.dram_tensor("x_out", [S, D], F32, kind="ExternalOutput").ap()
    if DEBUG_A:
        dbg_xlnT = nc.dram_tensor("dbg_xlnT", [128, 32768], BF16, kind="ExternalOutput").ap()
        dbg_bcol = nc.dram_tensor("dbg_bcol", [128, 24], F32, kind="ExternalOutput").ap()
        dbg_a = nc.dram_tensor("dbg_a", [128, 4096], F32, kind="ExternalOutput").ap()
        dbg_xs = nc.dram_tensor("dbg_xs", [128, 4096], F32, kind="ExternalOutput").ap()
        dbg_h = nc.dram_tensor("dbg_h", [128, 4096], F32, kind="ExternalOutput").ap()
        dbg_wT = nc.dram_tensor("dbg_wT", [128, 24576], BF16, kind="ExternalOutput").ap()

    fconsts = nc.dram_tensor("fconsts", [1, 64], F32, kind="ExternalInput").ap()
    ctx = Ctx(nc)
    with ctx.es:
        alloc_shared(ctx)
        alloc_common(ctx)
        cp = ctx.sb["cpool"]
        for i, v in enumerate(FCONSTS):
            nc.const_aps.register(F32, float(v), cp[:, i:i + 1])
        with nc.Block() as block:
            @block.sync
            def _(sync):
                fc_rep = bass.AP(fconsts.tensor, 0, [[0, 128], [1, len(FCONSTS)]])
                ctx.inc(sync.dma_start(out=cp[:, 0:len(FCONSTS)], in_=fc_rep), "ld", 16)

            emit_block_A(ctx, block, x_in, x_out,
                         {"Wg": Wg, "Wv": Wv, "Wd": Wd},
                         {"bg": bg, "bv": bv, "bd": bd},
                         {"ident_b": ident_b, "ident_f": ident_f})

            @block.gpsimd
            def _(gpsimd):
                ctx.wait(gpsimd, "st")
                ctx.wait(gpsimd, "ve")
                ctx.wait(gpsimd, "se")
                if DEBUG_A:
                    ctx.inc(gpsimd.dma_start(out=dbg_xlnT, in_=ctx.sb["big0"][:]), "st", 16)
                    ctx.inc(gpsimd.dma_start(out=dbg_bcol, in_=ctx.sb["bcol"][:]), "st", 16)
                    ctx.inc(gpsimd.dma_start(out=dbg_a, in_=ctx.sb["afull"][:]), "st", 16)
                    ctx.inc(gpsimd.dma_start(out=dbg_xs, in_=ctx.sb["xsfull"][:]), "st", 16)
                    ctx.inc(gpsimd.dma_start(out=dbg_h, in_=ctx.sb["hfull"][:]), "st", 16)
                    ctx.inc(gpsimd.dma_start(out=dbg_wT, in_=ctx.sb["wT"][:]), "st", 16)
                    ctx.wait(gpsimd, "st")
    return nc



def emit_ln_stats(ctx, block, xv, sqbuf, stat, sc, width, ld_wait=None, extra=None,
                  ve_wait=None):
    D_ = width
    ve_base = ctx.cnt["ve"]
    se_base = ctx.cnt["se"]

    @block.scalar
    def _(scalar):
        if ld_wait is not None:
            ctx.wait(scalar, "ld", ld_wait)
        if ve_wait is not None:
            ctx.wait(scalar, "ve", ve_wait)
        if extra is not None:
            extra(scalar)
        ctx.inc(scalar.activation(sqbuf, xv, AF.Square, bias=0.0, scale=1.0,
                                  accum_out=stat[:, sc + 1:sc + 2]), "se", 1)
        scalar.wait_ge(ctx.sems["ve"], ve_base + 4)
        ctx.inc(scalar.activation(stat[:, sc + 2:sc + 3], stat[:, sc + 2:sc + 3],
                                  AF.Ln, bias=1e-5, scale=1.0), "se", 1)
        ctx.wait(scalar, "se")
        ctx.inc(scalar.activation(stat[:, sc + 2:sc + 3], stat[:, sc + 2:sc + 3],
                                  AF.Exp, bias=0.0, scale=-0.5), "se", 1)

    @block.vector
    def _(vector):
        if ld_wait is not None:
            ctx.wait(vector, "ld", ld_wait)
        ctx.inc(vector.reduce_sum(stat[:, sc:sc + 1], xv, AX.X), "ve", 1)
        vector.wait_ge(ctx.sems["se"], se_base + 1)
        ctx.wait(vector, "ve")
        ctx.inc(vector.tensor_scalar_mul(stat[:, sc:sc + 1], stat[:, sc:sc + 1],
                                         1.0 / D_), "ve", 1)
        ctx.wait(vector, "ve")
        ctx.inc(vector.scalar_tensor_tensor(stat[:, sc + 3:sc + 4], stat[:, sc:sc + 1],
                                            1.0, stat[:, sc:sc + 1],
                                            ALU.bypass, ALU.mult), "ve", 1)
        ctx.wait(vector, "ve")
        ctx.inc(vector.scalar_tensor_tensor(stat[:, sc + 2:sc + 3], stat[:, sc + 1:sc + 2],
                                            1.0 / D_, stat[:, sc + 3:sc + 4],
                                            ALU.mult, ALU.subtract), "ve", 1)
    return ctx.cnt["se"]


def emit_fp4_quant(ctx, block, xv, out_b, W, bufs):
    G = W // 16
    qa, qb, qc, sgn, scb, rcb, blk = bufs

    @block.scalar
    def _(scalar):
        ctx.inc(scalar.activation(qa[:, 0:W], xv, AF.Abs, bias=0.0, scale=1.0), "se", 1)
        ctx.wait(scalar, "se")
        ctx.inc(scalar.activation(sgn[:, 0:W], xv, AF.Sign, bias=0.0, scale=1.0), "se", 1)

    se_abs = ctx.cnt["se"]

    @block.vector
    def _(vector):
        ctx.wait(vector, "se", se_abs)
        ctx.inc(vector.tensor_reduce(
            blk[:, 0:G], qa[:, 0:W].rearrange("p (g b) -> p g b", b=16),
            AX.X, ALU.max), "ve", 1)
        ctx.wait(vector, "ve")
        ctx.inc(vector.tensor_scalar(blk[:, G:2 * G], blk[:, 0:G], 1.0 / 6.0, 1e-12,
                                     ALU.mult, ALU.max), "ve", 1)
        ctx.wait(vector, "ve")
        ctx.inc(vector.reciprocal(blk[:, 0:G], blk[:, G:2 * G]), "ve", 1)
        ctx.wait(vector, "ve")
        scv = scb[:, 0:W].rearrange("p (g b) -> p g b", b=16)
        rcv = rcb[:, 0:W].rearrange("p (g b) -> p g b", b=16)
        s3 = blk[:, G:2 * G].rearrange("p (g b) -> p g b", b=1)
        r3 = blk[:, 0:G].rearrange("p (g b) -> p g b", b=1)
        for r in range(16):
            vector.tensor_copy(scv[:, :, r:r + 1], s3)
            i = vector.tensor_copy(rcv[:, :, r:r + 1], r3)
        ctx.inc(i, "ve", 1)
        ctx.wait(vector, "ve")
        ctx.inc(vector.tensor_mul(qa[:, 0:W], qa[:, 0:W], rcb[:, 0:W]), "ve", 1)
        ctx.wait(vector, "ve")
        vector.tensor_single_scalar(qb[:, 0:W], qa[:, 0:W], 0.25, ALU.is_gt)
        vector.tensor_single_scalar(qc[:, 0:W], qa[:, 0:W], 2.5, ALU.is_gt)
        vector.scalar_tensor_tensor(qb[:, 0:W], qa[:, 0:W], 0.75, qb[:, 0:W],
                                    ALU.is_gt, ALU.add)
        vector.scalar_tensor_tensor(qc[:, 0:W], qa[:, 0:W], 3.5, qc[:, 0:W],
                                    ALU.is_gt, ALU.add)
        vector.scalar_tensor_tensor(qb[:, 0:W], qa[:, 0:W], 1.25, qb[:, 0:W],
                                    ALU.is_gt, ALU.add)
        vector.scalar_tensor_tensor(qc[:, 0:W], qa[:, 0:W], 5.0, qc[:, 0:W],
                                    ALU.is_gt, ALU.add)
        vector.scalar_tensor_tensor(qb[:, 0:W], qa[:, 0:W], 1.75, qb[:, 0:W],
                                    ALU.is_gt, ALU.add)
        ctx.inc(vector.scalar_tensor_tensor(qc[:, 0:W], qa[:, 0:W], 5.0, qc[:, 0:W],
                                            ALU.is_gt, ALU.add), "ve", 1)
        ctx.wait(vector, "ve")
        ctx.inc(vector.scalar_tensor_tensor(qa[:, 0:W], qb[:, 0:W], 0.5, qc[:, 0:W],
                                            ALU.mult, ALU.add), "ve", 1)
        ctx.wait(vector, "ve")
        ctx.inc(vector.tensor_mul(qa[:, 0:W], qa[:, 0:W], sgn[:, 0:W]), "ve", 1)
        ctx.wait(vector, "ve")
        ctx.inc(vector.tensor_mul(out_b, qa[:, 0:W], scb[:, 0:W]), "ve", 1)
    return ctx.cnt["ve"]



def emit_transpose_evac(ctx, block, srcbuf, ntr, dst_fn, idb, pb0, pb1, ve_wait=None):
    for g0 in range(0, ntr, 8):
        gtiles = list(range(g0, min(g0 + 8, ntr)))

        @block.tensor
        def _(tensor, gtiles=gtiles, g0=g0, ve_wait=ve_wait):
            if g0 == 0 and ve_wait is not None:
                ctx.wait(tensor, "ve", ve_wait)
            ctx.wait(tensor, "se", ctx.at("trev_ev"))
            for k in gtiles:
                slot = (k % 8) // 2
                pbank = pb0 if (k % 2 == 0) else pb1
                ii = tensor.transpose(pbank[:, slot * 128:slot * 128 + 128],
                                      srcbuf[:, k * 128:(k + 1) * 128], idb[:])
            ctx.inc(ii, "pe", 1)
            ctx.rec("trev_tr", "pe")

        pe_now = ctx.cnt["pe"]

        @block.scalar
        def _(scalar, gtiles=gtiles, pe_now=pe_now):
            ctx.wait(scalar, "pe", pe_now)
            for k in gtiles:
                slot = (k % 8) // 2
                pbank = pb0 if (k % 2 == 0) else pb1
                ii = dst_fn(scalar, k, pbank[:, slot * 128:slot * 128 + 128])
            ctx.inc(ii, "se", 1)
            ctx.rec("trev_ev", "se")
    return ctx.cnt["se"]


IC = 2816
NIT = IC // 128

def emit_block_D(ctx, block, x_in, x_out, w_aps, consts):
    sb, ps = ctx.sb, ctx.ps
    wgqT, wuqT, wdqT = sb["wgqT"], sb["wuqT"], sb["wdqT"]
    xt = sb["xt"]; stat = sb["stat"]; idb = sb["idb"]
    qa, qb, qc = sb["qa"], sb["qb"], sb["qc"]
    sgn, scb, rcb, blk = sb["sgn"], sb["scb"], sb["rcb"], sb["blk"]
    hbuf = sb["hbuf"]; hq = sb["hq"]; hqT = sb["hqT"]
    xq = sb["hq"]
    xqT = sb["hqT"]
    sg = sb["sg"]
    obuf = sb["hbuf"]
    p0, p1, p2, p3 = ps["p0"], ps["p1"], ps["p2"], ps["p3"]
    pb0, pb1 = ps["pb0"], ps["pb1"]

    @block.sync
    def _(sync):
        ctx.inc(sync.dma_start(out=idb[:], in_=consts["ident_b"]), "ld", 16)

    jobs = [("wg", w_aps["wg"], NIT, D, wgqT, IC),
            ("wu", w_aps["wu"], NIT, D, wuqT, IC),
            ("wd", w_aps["wd"], NCD, IC, wdqT, D)]
    for name, wap, nrt, W, dstT, dcols in jobs:
        ntr = W // 128
        for r in range(nrt):
            @block.sync
            def _(sync, wap=wap, r=r, W=W):
                ctx.wait(sync, "ve", ctx.at("d0_q"))
                ctx.inc(sync.dma_start(out=hbuf[:, 0:W],
                                       in_=wap[r * 128:(r + 1) * 128, 0:W]), "ld", 16)

            ld_now = ctx.cnt["ld"]

            @block.vector
            def _(vector, ld_now=ld_now):
                ctx.wait(vector, "ld", ld_now)
                ctx.wait(vector, "pe", ctx.at("trev_tr"))
                vector.engine_nop()

            @block.scalar
            def _(scalar, ld_now=ld_now):
                ctx.wait(scalar, "ld", ld_now)

            emit_fp4_quant(ctx, block, hbuf[:, 0:W], hq[:, 0:W], W,
                           (qa, qb, qc, sgn, scb, rcb, blk))
            ve_q = ctx.cnt["ve"]
            ctx.rec("d0_q", "ve")

            def dst_fn(eng, k, psrc, dstT=dstT, dcols=dcols, r=r):
                return eng.copy(dstT[:, k * dcols + r * 128:
                                     k * dcols + (r + 1) * 128], psrc)
            emit_transpose_evac(ctx, block, hq, ntr, dst_fn, idb, pb0, pb1,
                                ve_wait=ve_q)
            ctx.rec("d0_tr", "pe")

    ICH = [(0, 512), (512, 512), (1024, 512), (1536, 512), (2048, 512), (2560, 256)]
    for j in range(NT):
        @block.sync
        def _(sync, j=j):
            ctx.wait(sync, "st", ctx.at("d1_store"))
            ctx.inc(sync.dma_start(out=xt[:, 0:1024],
                                   in_=x_in[j * 128:(j + 1) * 128, :]), "ld", 16)

        ld_now = ctx.cnt["ld"]
        emit_ln_stats(ctx, block, xt[:, 0:1024], qa[:, 0:1024], stat, 0, D,
                      ld_wait=ld_now)
        se_rs = ctx.cnt["se"]

        @block.vector
        def _(vector, se_rs=se_rs):
            ctx.wait(vector, "se", se_rs)
            ctx.inc(vector.tensor_scalar(hbuf[:, 0:1024], xt[:, 0:1024],
                                         stat[:, 0:1], stat[:, 2:3],
                                         ALU.subtract, ALU.mult), "ve", 1)
            ctx.wait(vector, "ve")

        emit_fp4_quant(ctx, block, hbuf[:, 0:1024], xq[:, 0:1024], D,
                       (qa, qb, qc, sgn, scb, rcb, blk))
        ve_q = ctx.cnt["ve"]

        def xq_dst(eng, k, psrc):
            return eng.copy(xqT[:, k * 128:(k + 1) * 128], psrc)
        emit_transpose_evac(ctx, block, xq, 8, xq_dst, idb, pb0, pb1, ve_wait=ve_q)
        se_xqT = ctx.cnt["se"]

        pe_gu = ctx.cnt["pe"]
        se_gu = ctx.cnt["se"]
        ve_gu = ctx.cnt["ve"]

        @block.tensor
        def _(tensor, se_xqT=se_xqT, ve_gu=ve_gu, se_gu=se_gu):
            ctx.wait(tensor, "se", se_xqT)
            ctx.wait(tensor, "ve", ctx.at("d1_gu"))
            for ci, (i0, ilen) in enumerate(ICH):
                if ci >= 1:
                    tensor.wait_ge(ctx.sems["se"], se_gu + ci)
                    tensor.wait_ge(ctx.sems["ve"], ve_gu + ci)
                for k in range(8):
                    tensor.matmul(p0[:, 0:ilen], xqT[:, k * 128:(k + 1) * 128],
                                  wgqT[:, k * IC + i0:k * IC + i0 + ilen],
                                  start=(k == 0), stop=(k == 7))
                for k in range(8):
                    ii = tensor.matmul(p1[:, 0:ilen], xqT[:, k * 128:(k + 1) * 128],
                                       wuqT[:, k * IC + i0:k * IC + i0 + ilen],
                                       start=(k == 0), stop=(k == 7))
                ctx.inc(ii, "pe", 1)

        @block.scalar
        def _(scalar, pe_gu=pe_gu, ve_gu=ve_gu):
            for ci, (i0, ilen) in enumerate(ICH):
                scalar.wait_ge(ctx.sems["pe"], pe_gu + ci + 1)
                if ci >= 1:
                    scalar.wait_ge(ctx.sems["ve"], ve_gu + ci)
                ctx.inc(scalar.activation(sg[:, 0:ilen], p0[:, 0:ilen], AF.Silu,
                                          bias=0.0, scale=1.0), "se", 1)

        @block.vector
        def _(vector, se_gu=se_gu):
            for ci, (i0, ilen) in enumerate(ICH):
                vector.wait_ge(ctx.sems["se"], se_gu + ci + 1)
                ctx.inc(vector.tensor_mul(hbuf[:, i0:i0 + ilen], sg[:, 0:ilen],
                                          p1[:, 0:ilen]), "ve", 1)
            ctx.rec("d1_gu", "ve")

        @block.vector
        def _(vector):
            ctx.wait(vector, "ve")

        emit_fp4_quant(ctx, block, hbuf[:, 0:IC], hq[:, 0:IC], IC,
                       (qa, qb, qc, sgn, scb, rcb, blk))
        ve_hq = ctx.cnt["ve"]

        def hq_dst(eng, k, psrc):
            return eng.copy(hqT[:, k * 128:(k + 1) * 128], psrc)
        emit_transpose_evac(ctx, block, hq, NIT, hq_dst, idb, pb0, pb1, ve_wait=ve_hq)
        se_hqT = ctx.cnt["se"]

        for oc in range(2):
            @block.tensor
            def _(tensor, oc=oc, se_hqT=se_hqT):
                if oc == 0:
                    ctx.wait(tensor, "se", se_hqT)
                ctx.wait(tensor, "ve", ctx.at("d1_res"))
                for m in range(NIT):
                    ii = tensor.matmul(p2[:], hqT[:, m * 128:(m + 1) * 128],
                                       wdqT[:, m * D + oc * 512:m * D + oc * 512 + 512],
                                       start=(m == 0), stop=(m == NIT - 1))
                ctx.inc(ii, "pe", 1)

            pe_now = ctx.cnt["pe"]

            @block.vector
            def _(vector, oc=oc, pe_now=pe_now):
                ctx.wait(vector, "pe", pe_now)
                ctx.wait(vector, "st", ctx.at("d1_store"))
                ctx.inc(vector.tensor_add(obuf[:, oc * 512:(oc + 1) * 512], p2[:],
                                          xt[:, oc * 512:(oc + 1) * 512]), "ve", 1)
                ctx.rec("d1_res", "ve")

            ve_now = ctx.cnt["ve"]

        @block.sync
        def _(sync, j=j):
            ctx.wait(sync, "ve")
            ctx.inc(sync.dma_start(out=x_out[j * 128:(j + 1) * 128, :],
                                   in_=obuf[:, 0:1024]), "st", 16)
            ctx.rec("d1_store", "st")


def alloc_D(ctx):
    ctx.sbuf("wgqT", [128, 8 * IC], BF16)
    ctx.sbuf("wuqT", [128, 8 * IC], BF16)
    ctx.sbuf("wdqT", [128, NIT * D], BF16)
    ctx.sbuf("xt", [128, 1024], F32)
    ctx.sbuf("stat", [128, 8], F32)
    ctx.sbuf("idb", [128, 128], BF16)
    ctx.sbuf("qa", [128, IC], F32)
    ctx.sbuf("qb", [128, IC], BF16)
    ctx.sbuf("qc", [128, IC], BF16)
    ctx.sbuf("sgn", [128, IC], BF16)
    ctx.sbuf("scb", [128, IC], BF16)
    ctx.sbuf("rcb", [128, IC], F32)
    ctx.sbuf("blk", [128, 2 * (IC // 16)], F32)
    ctx.sbuf("hbuf", [128, IC], F32)
    ctx.sbuf("hq", [128, IC], BF16)
    ctx.sbuf("hqT", [128, NIT * 128], BF16)
    ctx.sbuf("sg", [128, 512], F32)


def build_D():
    nc = bass.Bass("TRN2", target_bir_lowering=False)
    x_in = nc.dram_tensor("x_in", [S, D], F32, kind="ExternalInput").ap()
    wg = nc.dram_tensor("wg", [IC, D], F32, kind="ExternalInput").ap()
    wu = nc.dram_tensor("wu", [IC, D], F32, kind="ExternalInput").ap()
    wd = nc.dram_tensor("wd", [D, IC], F32, kind="ExternalInput").ap()
    ident_b = nc.dram_tensor("ident_b", [128, 128], BF16, kind="ExternalInput").ap()
    fconsts = nc.dram_tensor("fconsts", [1, 64], F32, kind="ExternalInput").ap()
    x_out = nc.dram_tensor("x_out", [S, D], F32, kind="ExternalOutput").ap()

    ctx = Ctx(nc)
    with ctx.es:
        alloc_shared(ctx)
        alloc_D(ctx)
        cp = ctx.sb["cpool"]
        for i, v in enumerate(FCONSTS):
            nc.const_aps.register(F32, float(v), cp[:, i:i + 1])
        with nc.Block() as block:
            @block.sync
            def _(sync):
                fc_rep = bass.AP(fconsts.tensor, 0, [[0, 128], [1, len(FCONSTS)]])
                ctx.inc(sync.dma_start(out=cp[:, 0:len(FCONSTS)], in_=fc_rep), "ld", 16)

            emit_block_D(ctx, block, x_in, x_out,
                         {"wg": wg, "wu": wu, "wd": wd},
                         {"ident_b": ident_b})

            @block.gpsimd
            def _(gpsimd):
                ctx.wait(gpsimd, "st")
    return nc




def emit_block_C(ctx, block, x_in, x_out, w_aps, b_aps, consts, scratch):
    sb, ps = ctx.sb, ctx.ps
    xt = sb["xt"]; stat = sb["stat"]; idb = sb["idb"]; sq = sb["sq"]
    lnout = sb["lnout"]; c0st = sb["c0st"]; wtile = sb["wtile"]; wtb = sb["wtb"]
    wbT = sb["wbT"]; v_sb = sb["v_sb"]; xch = sb["xch"]; qkblk = sb["qkblk"]
    attT = sb["attT"]; stg = sb["stg"]; stgb = sb["stgb"]
    pmq = sb["pmq"]; ccol = sb["ccol"]; acolS = sb["acolS"]; alph = sb["alph"]
    bvb = sb["bvb"]; bpb = sb["bpb"]; bcol = sb["bcol"]
    onef = sb["onef"]; onefr = sb["onefr"]
    dn = stg_dn = None
    ystg = sb["ystg"]
    p0, p1, p2, p3 = ps["p0"], ps["p1"], ps["p2"], ps["p3"]
    pb0, pb1 = ps["pb0"], ps["pb1"]
    xlnT_d, qT_d, kT_d, yT_d, cscr = scratch
    dn = stg[0:1, 512:1024]
    avs = stg[0:64, 0:512]
    H = 16

    @block.sync
    def _(sync):
        ctx.inc(sync.dma_start(out=idb[:], in_=consts["ident_b"]), "ld", 16)
        for w, off in (("bq", 0), ("bk", 8)):
            for k in range(8):
                ctx.inc(sync.dma_start(out=bcol[:, off + k:off + k + 1],
                                       in_=b_aps[w][k * 128:(k + 1) * 128]), "ld", 16)
        ctx.inc(sync.dma_start(out=stg[0:1, 0:1024], in_=b_aps["bv"]), "ld", 16)

    ld_i = ctx.cnt["ld"]

    @block.vector
    def _(vector):
        ctx.wait(vector, "ld", ld_i)
        vector.memset(onef[:], 1.0)
        vector.memset(onefr[:], 1.0)
        ctx.inc(vector.memset(v_sb[:], 1.0), "ve", 1)

    ve_ones = ctx.cnt["ve"]

    @block.tensor
    def _(tensor):
        ctx.wait(tensor, "ve", ve_ones)
        tensor.matmul(p0[:], onefr[0:1, :], stg[0:1, 0:512], start=True, stop=True)
        ctx.inc(tensor.matmul(p1[:], onefr[0:1, :], stg[0:1, 512:1024],
                              start=True, stop=True), "pe", 1)

    pe_b = ctx.cnt["pe"]

    @block.scalar
    def _(scalar):
        ctx.wait(scalar, "pe", pe_b)
        scalar.copy(bvb[:, 0:512], p0[:])
        ctx.inc(scalar.copy(bvb[:, 512:1024], p1[:]), "se", 1)

    se_bv = ctx.cnt["se"]

    @block.sync
    def _(sync):
        ctx.wait(sync, "se", se_bv)
        ctx.inc(sync.dma_start(out=stg[0:1, 0:1024], in_=b_aps["bp"]), "ld", 16)

    ld_bp = ctx.cnt["ld"]

    @block.tensor
    def _(tensor):
        ctx.wait(tensor, "ld", ld_bp)
        tensor.matmul(p2[:], onefr[0:1, :], stg[0:1, 0:512], start=True, stop=True)
        ctx.inc(tensor.matmul(p3[:], onefr[0:1, :], stg[0:1, 512:1024],
                              start=True, stop=True), "pe", 1)

    pe_b2 = ctx.cnt["pe"]

    @block.scalar
    def _(scalar):
        ctx.wait(scalar, "pe", pe_b2)
        scalar.copy(bpb[:, 0:512], p2[:])
        ctx.inc(scalar.copy(bpb[:, 512:1024], p3[:]), "se", 1)

    ctx.rec("c_binit", "se")

    for j in range(NT):
        bi = j % 2
        sc = bi * 4
        xv = xt[:, bi * 1024:(bi + 1) * 1024]
        lv = lnout[:, bi * 1024:(bi + 1) * 1024]
        cv = c0st[:, bi * 1024:(bi + 1) * 1024]

        @block.sync
        def _(sync, j=j, xv=xv):
            ctx.wait(sync, "ve", ctx.at("c0_apply_%d" % (j - 2)))
            ctx.inc(sync.dma_start(out=xv, in_=x_in[j * 128:(j + 1) * 128, :]), "ld", 16)

        ld_now = ctx.cnt["ld"]
        emit_ln_stats(ctx, block, xv, sq[:, 0:1024], stat, sc, D, ld_wait=ld_now)
        se_rs = ctx.cnt["se"]

        @block.vector
        def _(vector, xv=xv, lv=lv, se_rs=se_rs, j=j, sc=sc):
            ctx.wait(vector, "se", se_rs)
            ctx.wait(vector, "pe", ctx.at("c0_tr_%d" % (j - 2)))
            ctx.inc(vector.tensor_scalar(lv, xv, stat[:, sc:sc + 1], stat[:, sc + 2:sc + 3],
                                         ALU.subtract, ALU.mult), "ve", 1)
            ctx.rec("c0_apply_%d" % j, "ve")

        ve_ln = ctx.cnt["ve"]

        @block.tensor
        def _(tensor, lv=lv, j=j, ve_ln=ve_ln):
            ctx.wait(tensor, "ve", ve_ln)
            ctx.wait(tensor, "se", ctx.at("c0_ev_%d" % (j - 2)))
            pbank = pb0 if (j % 2 == 0) else pb1
            for k in range(8):
                i = tensor.transpose(pbank[:, k * 128:(k + 1) * 128],
                                     lv[:, k * 128:(k + 1) * 128], idb[:])
            ctx.inc(i, "pe", 1)
            ctx.rec("c0_tr_%d" % j, "pe")

        pe_tr = ctx.cnt["pe"]

        @block.scalar
        def _(scalar, j=j, pe_tr=pe_tr, cv=cv):
            ctx.wait(scalar, "pe", pe_tr)
            ctx.wait(scalar, "st", ctx.at("c0_st_%d" % (j - 2)))
            pbank = pb0 if (j % 2 == 0) else pb1
            ctx.inc(scalar.copy(cv, pbank[:]), "se", 1)
            ctx.rec("c0_ev_%d" % j, "se")

        se_ev = ctx.cnt["se"]

        @block.sync
        def _(sync, j=j, se_ev=se_ev, cv=cv):
            ctx.wait(sync, "se", se_ev)
            for k in range(8):
                ctx.inc(sync.dma_start(
                    out=xlnT_d[k * 128:(k + 1) * 128, j * 128:(j + 1) * 128],
                    in_=cv[:, k * 128:(k + 1) * 128]), "st", 16)
            ctx.rec("c0_st_%d" % j, "st")

    ctx.rec("c0_allst", "st")

    for wi, wname in enumerate(("Wq", "Wk", "Wv", "Wp")):
        for n in range(64):
            i, k = n // 8, n % 8
            bi = n % 2
            wv = wtile[:, bi * 128:(bi + 1) * 128]
            wb = wtb[:, bi * 128:(bi + 1) * 128]

            @block.sync
            def _(sync, wname=wname, i=i, k=k, n=n, wv=wv):
                ctx.wait(sync, "se", ctx.at("c1_abs_%d" % (n - 2)))
                ctx.wait(sync, "ve", ctx.at("c1_cv_%d" % (n - 2)))
                ctx.inc(sync.dma_start(
                    out=wv, in_=w_aps[wname][i * 128:(i + 1) * 128,
                                             k * 128:(k + 1) * 128]), "ld", 16)

            ld_now = ctx.cnt["ld"]

            @block.scalar
            def _(scalar, wv=wv, n=n, wi=wi, ld_now=ld_now):
                ctx.wait(scalar, "ld", ld_now)
                ctx.inc(scalar.activation(sq[:, 0:128], wv, AF.Abs,
                                          bias=0.0, scale=1.0,
                                          accum_out=acolS[:, wi * 64 + n:wi * 64 + n + 1]),
                        "se", 1)
                ctx.rec("c1_abs_%d" % n, "se")

            se_abs = ctx.cnt["se"]

            @block.vector
            def _(vector, wv=wv, wb=wb, n=n, se_abs=se_abs):
                ctx.wait(vector, "se", se_abs)
                ctx.wait(vector, "pe", ctx.at("c1_tr_%d" % (n - 2)))
                ctx.inc(vector.tensor_copy(wb, wv), "ve", 1)
                ctx.rec("c1_cv_%d" % n, "ve")

            ve_cv = ctx.cnt["ve"]

            @block.tensor
            def _(tensor, wb=wb, n=n, ve_cv=ve_cv):
                ctx.wait(tensor, "ve", ve_cv)
                ctx.wait(tensor, "se", ctx.at("c1_sg_%d" % (n - 2)))
                pbank = pb0 if (n % 2 == 0) else pb1
                ctx.inc(tensor.transpose(pbank[:, 0:128], wb, idb[:]), "pe", 1)
                ctx.rec("c1_tr_%d" % n, "pe")

            pe_now = ctx.cnt["pe"]

            @block.scalar
            def _(scalar, wi=wi, i=i, k=k, n=n, pe_now=pe_now):
                ctx.wait(scalar, "pe", pe_now)
                pbank = pb0 if (n % 2 == 0) else pb1
                ctx.inc(scalar.activation(
                    wbT[:, wi * 8192 + k * 1024 + i * 128:
                        wi * 8192 + k * 1024 + (i + 1) * 128],
                    pbank[:, 0:128], AF.Sign, bias=0.0, scale=1.0), "se", 1)
                ctx.rec("c1_sg_%d" % n, "se")

    se_allabs = ctx.cnt["se"]

    @block.vector
    def _(vector, se_allabs=se_allabs):
        ctx.wait(vector, "se", se_allabs)
        ctx.inc(vector.tensor_reduce(pmq[:, 0:4],
                                     acolS[:, 0:256].rearrange("p (a b) -> p a b", b=64),
                                     AX.X, ALU.add), "ve", 1)

    ve_rs = ctx.cnt["ve"]

    @block.tensor
    def _(tensor, ve_rs=ve_rs):
        ctx.wait(tensor, "ve", ve_rs)
        ctx.inc(tensor.matmul(p0[0:1, 0:4], onef[:, 0:1], pmq[:, 0:4],
                              start=True, stop=True), "pe", 1)

    pe_s = ctx.cnt["pe"]

    @block.vector
    def _(vector, pe_s=pe_s):
        ctx.wait(vector, "pe", pe_s)
        ctx.inc(vector.tensor_scalar_mul(stg[0:1, 32:36], p0[0:1, 0:4],
                                         1.0 / (1024.0 * 1024.0)), "ve", 1)
        ctx.wait(vector, "ve")

    ve_al = ctx.cnt["ve"]

    @block.tensor
    def _(tensor, ve_al=ve_al):
        ctx.wait(tensor, "ve", ve_al)
        ctx.inc(tensor.matmul(p0[:, 8:12], onefr[0:1, :], stg[0:1, 32:36],
                              start=True, stop=True), "pe", 1)

    pe_al = ctx.cnt["pe"]

    @block.scalar
    def _(scalar, pe_al=pe_al):
        ctx.wait(scalar, "pe", pe_al)
        ctx.inc(scalar.copy(alph[:, 0:4], p0[:, 8:12]), "se", 1)

    ctx.rec("c1_done_se", "se")

    for tch in range(8):
        bi = tch % 2
        xv = xch[:, bi * 4096:(bi + 1) * 4096]

        @block.sync
        def _(sync, tch=tch, xv=xv):
            ctx.wait(sync, "st", ctx.at("c0_allst"))
            ctx.wait(sync, "pe", ctx.at("c2_done_%d" % (tch - 2)))
            for k in range(8):
                ctx.inc(sync.dma_start(
                    out=xv[:, k * 512:(k + 1) * 512],
                    in_=xlnT_d[k * 128:(k + 1) * 128, tch * 512:(tch + 1) * 512]),
                    "ld", 16)

        ld_now = ctx.cnt["ld"]

        for wi, dst in ((0, qT_d), (1, kT_d)):
            for ot in range(8):
                @block.tensor
                def _(tensor, wi=wi, ot=ot, xv=xv, ld_now=ld_now, tch=tch):
                    if wi == 0 and ot == 0:
                        ctx.wait(tensor, "ld", ld_now)
                        if tch == 0:
                            ctx.wait(tensor, "se", ctx.at("c1_done_se"))
                    ctx.wait(tensor, "ve", ctx.at("c2_ep"))
                    bank = p0 if (ot % 2 == 0) else p1
                    for k in range(8):
                        ii = tensor.matmul(bank[:],
                                           wbT[:, wi * 8192 + k * 1024 + ot * 128:
                                               wi * 8192 + k * 1024 + (ot + 1) * 128],
                                           xv[:, k * 512:(k + 1) * 512],
                                           start=(k == 0), stop=(k == 7))
                    ctx.inc(ii, "pe", 1)

                pe_now = ctx.cnt["pe"]

                @block.vector
                def _(vector, wi=wi, ot=ot, tch=tch, pe_now=pe_now):
                    ctx.wait(vector, "pe", pe_now)
                    ctx.wait(vector, "st", ctx.at("c2_st"))
                    bank = p0 if (ot % 2 == 0) else p1
                    ctx.inc(vector.tensor_scalar(stgb[:, 0:512], bank[:],
                                                 alph[:, wi:wi + 1],
                                                 bcol[:, wi * 8 + ot:wi * 8 + ot + 1],
                                                 ALU.mult, ALU.add), "ve", 1)
                    ctx.wait(vector, "ve")
                    ctx.inc(vector.tensor_reduce(
                        pmq[:, 8 + wi * 64 + ot * 8 + tch:9 + wi * 64 + ot * 8 + tch],
                        stgb[:, 0:512], AX.X, ALU.max,
                        apply_absolute_value=True), "ve", 1)
                    ctx.rec("c2_ep", "ve")

                ve_ep = ctx.cnt["ve"]

                @block.sync
                def _(sync, dst=dst, ot=ot, tch=tch, ve_ep=ve_ep):
                    ctx.wait(sync, "ve", ve_ep)
                    ctx.inc(sync.dma_start(
                        out=dst[ot * 128:(ot + 1) * 128, tch * 512:(tch + 1) * 512],
                        in_=stgb[:, 0:512]), "st", 16)
                    ctx.rec("c2_st", "st")

        for r in range(4):
            tt = tch * 4 + r
            for oc in range(2):
                @block.tensor
                def _(tensor, tt=tt, oc=oc, xv=xv, r=r):
                    ctx.wait(tensor, "ve", ctx.at("c2_vep"))
                    bank = p2 if (oc == 0) else p3
                    for k in range(8):
                        ii = tensor.matmul(bank[:],
                                           xv[:, k * 512 + r * 128:k * 512 + (r + 1) * 128],
                                           wbT[:, 2 * 8192 + k * 1024 + oc * 512:
                                               2 * 8192 + k * 1024 + (oc + 1) * 512],
                                           start=(k == 0), stop=(k == 7))
                    ctx.inc(ii, "pe", 1)

                pe_now = ctx.cnt["pe"]

                @block.vector
                def _(vector, tt=tt, oc=oc, pe_now=pe_now):
                    ctx.wait(vector, "pe", pe_now)
                    bank = p2 if (oc == 0) else p3
                    for hh in range(8):
                        h = oc * 8 + hh
                        ii = vector.scalar_tensor_tensor(
                            v_sb[:, tt * 1040 + 65 * h:tt * 1040 + 65 * h + 64],
                            bank[:, hh * 64:(hh + 1) * 64], alph[:, 2:3],
                            bvb[:, h * 64:(h + 1) * 64], ALU.mult, ALU.add)
                    ctx.inc(ii, "ve", 1)
                    ctx.rec("c2_vep", "ve")

        @block.tensor
        def _(tensor, tch=tch):
            ctx.wait(tensor, "ve", ctx.at("c2_vep"))
            tensor.nop()
            ctx.rec("c2_done_%d" % tch, "pe")

    ctx.rec("c2_allst", "st")

    ve_pm = ctx.cnt["ve"]

    @block.vector
    def _(vector, ve_pm=ve_pm):
        ctx.wait(vector, "ve", ve_pm)
        ctx.inc(vector.tensor_reduce(pmq[:, 4:6],
                                     pmq[:, 8:136].rearrange("p (a b) -> p a b", b=64),
                                     AX.X, ALU.max), "ve", 1)

    ve_red = ctx.cnt["ve"]

    @block.sync
    def _(sync, ve_red=ve_red):
        ctx.wait(sync, "ve", ve_red)
        ctx.inc(sync.dma_start(out=cscr[0:128, 0:2], in_=pmq[:, 4:6]), "st", 16)

    st_c = ctx.cnt["st"]

    @block.sync
    def _(sync, st_c=st_c):
        ctx.wait(sync, "st", st_c)
        cs = bass.AP(cscr.tensor, 0, [[0, 1], [1, 256]])
        ctx.inc(sync.dma_start(out=stg[0:1, 64:320], in_=cs), "ld", 16)

    ld_c = ctx.cnt["ld"]

    @block.vector
    def _(vector, ld_c=ld_c):
        ctx.wait(vector, "ld", ld_c)
        base = stg[0:1, 64:320]
        v2 = bass.AP(base.tensor, base.offset, [base.ap[0], [1, 2], [2, 128]])
        ctx.inc(vector.tensor_reduce(stg[0:1, 0:2], v2, AX.X, ALU.max), "ve", 1)
        ctx.wait(vector, "ve")
        ctx.inc(vector.scalar_tensor_tensor(stg[0:1, 2:3], stg[0:1, 0:1], 8.0,
                                            stg[0:1, 1:2], ALU.mult, ALU.mult), "ve", 1)
        ctx.wait(vector, "ve")
        ctx.inc(vector.tensor_scalar(stg[0:1, 3:4], stg[0:1, 2:3], -1.0, -2.0,
                                     ALU.mult, ALU.add), "ve", 1)

    ve_c = ctx.cnt["ve"]

    @block.tensor
    def _(tensor, ve_c=ve_c):
        ctx.wait(tensor, "ve", ve_c)
        ctx.inc(tensor.matmul(p0[:, 16:17], onefr[0:1, :], stg[0:1, 3:4],
                              start=True, stop=True), "pe", 1)

    pe_c = ctx.cnt["pe"]

    @block.scalar
    def _(scalar, pe_c=pe_c):
        ctx.wait(scalar, "pe", pe_c)
        ctx.inc(scalar.copy(ccol[:, 0:1], p0[:, 16:17]), "se", 1)

    ctx.rec("c2b_done", "se")

    for h in range(H):
        @block.sync
        def _(sync, h=h):
            ctx.wait(sync, "st", ctx.at("c2_allst"))
            ctx.wait(sync, "pe", ctx.at("c3_hdone_%d" % (h - 1)))
            ctx.inc(sync.dma_start(out=qkblk[0:64, 0:4096],
                                   in_=kT_d[h * 64:(h + 1) * 64, :]), "ld", 16)
            ctx.inc(sync.dma_start(out=qkblk[0:64, 4096:8192],
                                   in_=qT_d[h * 64:(h + 1) * 64, :]), "ld", 16)

        ld_h = ctx.cnt["ld"]

        for qc in range(8):
            pe_base = ctx.cnt["pe"]
            se_base = ctx.cnt["se"]
            ve_base = ctx.cnt["ve"]
            st_base = ctx.cnt["st"]

            def pe_of_S(kt, pe_base=pe_base):
                return pe_base + (1 if kt == 0 else 2 * kt)

            @block.tensor
            def _(tensor, h=h, qc=qc, ld_h=ld_h, pe_base=pe_base, se_base=se_base,
                  ve_base=ve_base):
                if qc == 0:
                    tensor.wait_ge(ctx.sems["ld"], ld_h)
                for kt in range(32):
                    bank = p0 if (kt % 2 == 0) else p1
                    if kt >= 2:
                        tensor.wait_ge(ctx.sems["se"], se_base + kt - 1)
                    ii = tensor.matmul(bank[:], qkblk[0:64, kt * 128:(kt + 1) * 128],
                                       qkblk[0:64, 4096 + qc * 512:4096 + (qc + 1) * 512],
                                       start=True, stop=True)
                    ctx.inc(ii, "pe", 1)
                    if kt > 0:
                        if kt == 1:
                            ctx.wait(tensor, "ve", ctx.at("c3_dn_ve"))
                        tensor.wait_ge(ctx.sems["se"], se_base + kt)
                        ii = tensor.matmul(p2[0:65, :],
                                           v_sb[:, (kt - 1) * 1040 + 65 * h:
                                                (kt - 1) * 1040 + 65 * h + 65],
                                           attT[:, ((kt - 1) % 2) * 512:
                                                ((kt - 1) % 2) * 512 + 512],
                                           start=(kt == 1), stop=False)
                        ctx.inc(ii, "pe", 1)
                tensor.wait_ge(ctx.sems["se"], se_base + 32)
                ii = tensor.matmul(p2[0:65, :],
                                   v_sb[:, 31 * 1040 + 65 * h:31 * 1040 + 65 * h + 65],
                                   attT[:, 512:1024],
                                   start=False, stop=True)
                ctx.inc(ii, "pe", 1)
                tensor.wait_ge(ctx.sems["ve"], ve_base + 2)
                ctx.inc(tensor.matmul(p3[0:64, :], onefr[0:1, 0:64], dn[0:1, :],
                                      start=True, stop=True), "pe", 1)

            pe_av = pe_base + 64
            pe_dn = ctx.cnt["pe"]

            @block.scalar
            def _(scalar, h=h, pe_base=pe_base, se_base=se_base, pe_av=pe_av):
                if h == 0:
                    ctx.wait(scalar, "se", ctx.at("c2b_done"))
                for kt in range(32):
                    bank = p0 if (kt % 2 == 0) else p1
                    scalar.wait_ge(ctx.sems["pe"], pe_of_S(kt))
                    ctx.inc(scalar.activation(
                        attT[:, (kt % 2) * 512:(kt % 2) * 512 + 512],
                        bank[:], AF.Exp, bias=ccol[:, 0:1], scale=1.0), "se", 1)
                scalar.wait_ge(ctx.sems["pe"], pe_av)
                ctx.wait(scalar, "ve", ctx.at("c3_ymul_ve"))
                ctx.inc(scalar.copy(avs[0:64, :], p2[0:64, :]), "se", 1)
                ctx.rec("c3_avs_se", "se")

            se_avs = ctx.cnt["se"]

            @block.vector
            def _(vector, pe_av=pe_av, se_avs=se_avs, pe_dn=pe_dn):
                vector.wait_ge(ctx.sems["pe"], pe_av)
                ctx.inc(vector.tensor_copy(dn[0:1, :], p2[64:65, :]), "ve", 1)
                ctx.wait(vector, "ve")
                ctx.inc(vector.reciprocal(dn[0:1, :], dn[0:1, :]), "ve", 1)
                ctx.rec("c3_dn_ve", "ve")
                vector.wait_ge(ctx.sems["se"], se_avs)
                vector.wait_ge(ctx.sems["pe"], pe_dn)
                ctx.wait(vector, "st", ctx.at("c3_yst"))
                ctx.inc(vector.tensor_mul(ystg[0:64, :], avs[0:64, :], p3[0:64, :]),
                        "ve", 1)
                ctx.rec("c3_ymul_ve", "ve")

            ve_y = ctx.cnt["ve"]

            @block.sync
            def _(sync, h=h, qc=qc, ve_y=ve_y):
                ctx.wait(sync, "ve", ve_y)
                ctx.inc(sync.dma_start(
                    out=yT_d[h * 64:(h + 1) * 64, qc * 512:(qc + 1) * 512],
                    in_=ystg[0:64, :]), "st", 16)
                ctx.rec("c3_yst", "st")

        ctx.rec("c3_hdone_%d" % h, "pe")

    ctx.rec("c3_allst", "st")

    for tt in range(NT):
        bi = tt % 2
        yv = xch[:, bi * 4096:bi * 4096 + 1024]
        xv = xt[:, bi * 1024:(bi + 1) * 1024]

        @block.sync
        def _(sync, tt=tt, yv=yv, xv=xv):
            ctx.wait(sync, "st", ctx.at("c3_allst"))
            ctx.wait(sync, "pe", ctx.at("c4_mm_%d" % (tt - 2)))
            ctx.wait(sync, "ve", ctx.at("c4_add_%d" % (tt - 2)))
            for k in range(8):
                ctx.inc(sync.dma_start(
                    out=yv[:, k * 128:(k + 1) * 128],
                    in_=yT_d[k * 128:(k + 1) * 128, tt * 128:(tt + 1) * 128]), "ld", 16)
            ctx.inc(sync.dma_start(out=xv, in_=x_in[tt * 128:(tt + 1) * 128, :]), "ld", 16)

        ld_now = ctx.cnt["ld"]

        for oc in range(2):
            @block.tensor
            def _(tensor, tt=tt, oc=oc, yv=yv, ld_now=ld_now):
                if oc == 0:
                    ctx.wait(tensor, "ld", ld_now)
                ctx.wait(tensor, "ve", ctx.at("c4_add_%d" % (tt - 1)))
                bank = p0 if oc == 0 else p1
                for k in range(8):
                    ii = tensor.matmul(bank[:],
                                       yv[:, k * 128:(k + 1) * 128],
                                       wbT[:, 3 * 8192 + k * 1024 + oc * 512:
                                           3 * 8192 + k * 1024 + (oc + 1) * 512],
                                       start=(k == 0), stop=(k == 7))
                ctx.inc(ii, "pe", 1)
                if oc == 1:
                    ctx.rec("c4_mm_%d" % tt, "pe")

        pe_now = ctx.cnt["pe"]

        @block.vector
        def _(vector, tt=tt, pe_now=pe_now, xv=xv):
            ctx.wait(vector, "pe", pe_now)
            ctx.wait(vector, "st", ctx.at("c4_store_%d" % (tt - 1)))
            bank = p0
            vector.scalar_tensor_tensor(stg[:, 0:512], p0[:],
                                        alph[:, 3:4], bpb[:, 0:512],
                                        ALU.mult, ALU.add)
            ctx.inc(vector.scalar_tensor_tensor(stg[:, 512:1024], p1[:],
                                                alph[:, 3:4], bpb[:, 512:1024],
                                                ALU.mult, ALU.add), "ve", 1)
            ctx.wait(vector, "ve")
            ctx.inc(vector.tensor_add(stg[:, 0:1024], stg[:, 0:1024], xv), "ve", 1)
            ctx.rec("c4_add_%d" % tt, "ve")

        ve_o = ctx.cnt["ve"]

        @block.sync
        def _(sync, tt=tt, ve_o=ve_o):
            ctx.wait(sync, "ve", ve_o)
            ctx.inc(sync.dma_start(out=x_out[tt * 128:(tt + 1) * 128, :],
                                   in_=stg[:, 0:1024]), "st", 16)
            ctx.rec("c4_store_%d" % tt, "st")


def alloc_C(ctx):
    ctx.sbuf("xt", [128, 2048], F32)
    ctx.sbuf("sq", [128, 1024], F32)
    ctx.sbuf("lnout", [128, 2048], BF16)
    ctx.sbuf("c0st", [128, 2048], BF16)
    ctx.sbuf("stat", [128, 8], F32)
    ctx.sbuf("idb", [128, 128], BF16)
    ctx.sbuf("wtile", [128, 256], F32)
    ctx.sbuf("wtb", [128, 256], BF16)
    ctx.sbuf("wbT", [128, 4 * 8192], BF16)
    ctx.sbuf("v_sb", [128, 32 * 1040], BF16)
    ctx.sbuf("xch", [128, 8192], BF16)
    ctx.sbuf("qkblk", [64, 8192], BF16)
    ctx.sbuf("attT", [128, 2048], BF16)
    ctx.sbuf("stg", [128, 1024], F32)
    ctx.sbuf("stgb", [128, 1024], BF16)
    ctx.sbuf("pmq", [128, 144], F32)
    ctx.sbuf("ccol", [128, 4], F32)
    ctx.sbuf("acolS", [128, 256], F32)
    ctx.sbuf("alph", [128, 8], F32)
    ctx.sbuf("bvb", [128, 1024], F32)
    ctx.sbuf("bpb", [128, 1024], F32)
    ctx.sbuf("bcol", [128, 24], F32)
    ctx.sbuf("onef", [128, 2], F32)
    ctx.sbuf("onefr", [1, 128], F32)
    ctx.sbuf("ystg", [64, 512], BF16)


def build_C():
    nc = bass.Bass("TRN2", target_bir_lowering=False)
    x_in = nc.dram_tensor("x_in", [S, D], F32, kind="ExternalInput").ap()
    aps = {}
    for w in ("Wq", "Wk", "Wv", "Wp"):
        aps[w] = nc.dram_tensor(w, [D, D], F32, kind="ExternalInput").ap()
    bps = {}
    for b in ("bq", "bk", "bv", "bp"):
        bps[b] = nc.dram_tensor(b, [D], F32, kind="ExternalInput").ap()
    ident_b = nc.dram_tensor("ident_b", [128, 128], BF16, kind="ExternalInput").ap()
    fconsts = nc.dram_tensor("fconsts", [1, 64], F32, kind="ExternalInput").ap()
    x_out = nc.dram_tensor("x_out", [S, D], F32, kind="ExternalOutput").ap()
    xlnT_d = nc.dram_tensor("xlnT_d", [D, S], BF16, kind="Internal").ap()
    qT_d = nc.dram_tensor("qT_d", [D, S], BF16, kind="Internal").ap()
    kT_d = nc.dram_tensor("kT_d", [D, S], BF16, kind="Internal").ap()
    yT_d = nc.dram_tensor("yT_d", [D, S], BF16, kind="Internal").ap()
    cscr = nc.dram_tensor("cscr", [128, 2], F32, kind="Internal").ap()

    ctx = Ctx(nc)
    with ctx.es:
        alloc_shared(ctx)
        alloc_C(ctx)
        cp = ctx.sb["cpool"]
        for i, v in enumerate(FCONSTS):
            nc.const_aps.register(F32, float(v), cp[:, i:i + 1])
        with nc.Block() as block:
            @block.sync
            def _(sync):
                fc_rep = bass.AP(fconsts.tensor, 0, [[0, 128], [1, len(FCONSTS)]])
                ctx.inc(sync.dma_start(out=cp[:, 0:len(FCONSTS)], in_=fc_rep), "ld", 16)

            emit_block_C(ctx, block, x_in, x_out, aps, bps,
                         {"ident_b": ident_b},
                         (xlnT_d, qT_d, kT_d, yT_d, cscr))

            @block.gpsimd
            def _(gpsimd):
                ctx.wait(gpsimd, "st")
    return nc



INNER = 512
HID2 = 5460
HID = 2730
UCH = [(0, 512), (512, 512), (1024, 512), (1536, 512), (2048, 512),
       (2560, 170), (2730, 342), (3072, 512), (3584, 512), (4096, 512),
       (4608, 512), (5120, 340)]


def emit_battn(ctx, block, sub, x_in, xi_in, xi_out, w_aps, consts, do_ln, yT_d):
    sb, ps = ctx.sb, ctx.ps
    xt = sb["xt"]; stat = sb["stat"]; idb = sb["idb"]; sq = sb["sq"]
    htile = sb["htile"]
    hbt = sb["hbt"]
    hT = sb["hT"]
    wqkv_sb = sb["wqkv_sb"]
    wo_sb = sb["wo_sb"]
    mcur = sb["mcur"]
    ropes = sb["ropes"]
    qsb = sb["qsb"]
    sqf = sb["sqf"]
    ssb = sb["ssb"]
    qn = sb["qn"]
    kn = sb["kn"]
    rot = sb["rot"]
    qrotT = sb["qrotT"]
    krotT = sb["krotT"]
    v_sb = sb["v_sb"]
    ystg = sb["ystg"]; ybuf = sb["ybuf"]
    satt = sb["satt"]
    attb = sb["attb"]
    attT = sb["attT"]
    den = sb["den"]
    obuf = sb["obuf"]
    onefr = sb["onefr"]
    qsrow = sb["qsrow"]
    p0, p1, p2, p3 = ps["p0"], ps["p1"], ps["p2"], ps["p3"]
    pb0, pb1 = ps["pb0"], ps["pb1"]

    @block.sync
    def _(sync):
        ctx.inc(sync.dma_start(out=idb[:], in_=consts["ident_b"]), "ld", 16)

    @block.gpsimd
    def _(gpsimd):
        for k in range(8):
            ctx.inc(gpsimd.dma_start(out=wqkv_sb[:, k * 1536:(k + 1) * 1536],
                                     in_=w_aps["wqkv"][k * 128:(k + 1) * 128, :]),
                    "gld", 16)
        for r in range(4):
            ctx.inc(gpsimd.dma_start(out=wo_sb[:, r * 1024:(r + 1) * 1024],
                                     in_=w_aps["wo"][r * 128:(r + 1) * 128, :]),
                    "gld", 16)
        qs_rep = bass.AP(w_aps["qs"].tensor, w_aps["qs"].offset,
                         [[0, 1], [0, 8], [1, 64]])
        ks_rep = bass.AP(w_aps["ks"].tensor, w_aps["ks"].offset,
                         [[0, 1], [0, 8], [1, 64]])
        ctx.inc(gpsimd.dma_start(out=qsrow[0:1, 0:512], in_=qs_rep), "gld", 16)
        ctx.inc(gpsimd.dma_start(out=qsrow[0:1, 512:1024], in_=ks_rep), "gld", 16)

    st_w = ctx.cnt["gld"]
    ld_w = ctx.cnt["ld"]

    @block.vector
    def _(vector):
        ctx.wait(vector, "ld", ld_w)
        ctx.inc(vector.memset(onefr[:], 1.0), "ve", 1)

    ve_one = ctx.cnt["ve"]

    @block.tensor
    def _(tensor):
        ctx.wait(tensor, "ve", ve_one)
        ctx.wait(tensor, "gld", st_w)
        tensor.matmul(p0[:], onefr[0:1, :], qsrow[0:1, 0:512], start=True, stop=True)
        ctx.inc(tensor.matmul(p1[:], onefr[0:1, :], qsrow[0:1, 512:1024],
                              start=True, stop=True), "pe", 1)

    pe_qs = ctx.cnt["pe"]

    @block.scalar
    def _(scalar):
        ctx.wait(scalar, "pe", pe_qs)
        scalar.copy(qsb[:, 0:512], p0[:])
        ctx.inc(scalar.copy(qsb[:, 512:1024], p1[:]), "se", 1)

    ctx.rec("ba_init", "se")

    for j in range(NT):
        @block.sync
        def _(sync, j=j):
            ctx.wait(sync, "ve", ctx.at("ba_xt_free_%d" % (j - 1)))
            if do_ln:
                ctx.inc(sync.dma_start(out=xt[:, 0:1024],
                                       in_=x_in[j * 128:(j + 1) * 128, :]), "ld", 16)
            else:
                ctx.inc(sync.dma_start(out=xt[:, 0:1024],
                                       in_=xi_in[j * 128:(j + 1) * 128, :]), "ld", 16)

        ld_now = ctx.cnt["ld"]

        if do_ln:
            emit_ln_stats(ctx, block, xt[:, 0:1024], sq[:, 0:1024], stat, 0, D,
                          ld_wait=ld_now)
            se_rs = ctx.cnt["se"]

            @block.vector
            def _(vector, se_rs=se_rs, j=j):
                ctx.wait(vector, "se", se_rs)
                ctx.wait(vector, "st", ctx.at("ba_xist_%d" % (j - 1)))
                ctx.inc(vector.tensor_scalar(xt[:, 1024:2048], xt[:, 0:1024],
                                             stat[:, 0:1], stat[:, 2:3],
                                             ALU.subtract, ALU.mult), "ve", 1)

            ve_xi = ctx.cnt["ve"]

            @block.sync
            def _(sync, j=j, ve_xi=ve_xi):
                ctx.wait(sync, "ve", ve_xi)
                ctx.inc(sync.dma_start(out=xi_in[j * 128:(j + 1) * 128, :],
                                       in_=xt[:, 1024:2048]), "st", 16)
                ctx.rec("ba_xist_%d" % j, "st")
            xiview = xt[:, 1024:2048]
        else:
            xiview = xt[:, 0:1024]

        @block.vector
        def _(vector, j=j, xiview=xiview):
            if do_ln:
                ctx.wait(vector, "ve")
            else:
                ctx.wait(vector, "ld", ctx.cnt["ld"])
            ctx.wait(vector, "pe", ctx.at("ba_htr_%d" % (j - 1)))
            if j == 0:
                vector.memset(sb["zcol"][:], 0.0)
            ctx.inc(vector.tensor_copy(hbt[:], xiview[:, 0:1024]), "ve", 1)
            ctx.rec("ba_xt_free_%d" % j, "ve")

        ve_h = ctx.cnt["ve"]

        @block.tensor
        def _(tensor, ve_h=ve_h, j=j):
            ctx.wait(tensor, "ve", ve_h)
            ctx.wait(tensor, "se", ctx.at("ba_pbev"))
            for k in range(8):
                ii = tensor.transpose(pb0[:, (k % 4) * 128:(k % 4) * 128 + 128] if k < 4
                                      else pb1[:, (k % 4) * 128:(k % 4) * 128 + 128],
                                      hbt[:, k * 128:(k + 1) * 128], idb[:])
            ctx.inc(ii, "pe", 1)
            ctx.rec("ba_htr_%d" % j, "pe")

        pe_tr = ctx.cnt["pe"]

        @block.scalar
        def _(scalar, pe_tr=pe_tr, j=j):
            ctx.wait(scalar, "pe", pe_tr)
            pc_r = sb["prevcol"][:, ((j + 1) % 2) * 4:((j + 1) % 2) * 4 + 4]
            pc_w = sb["prevcol"][:, (j % 2) * 4:(j % 2) * 4 + 4]
            for k in range(8):
                pbank = pb0 if k < 4 else pb1
                pslc = pbank[:, (k % 4) * 128:(k % 4) * 128 + 128]
                if k < 4:
                    ii = scalar.copy(hT[:, k * 128:(k + 1) * 128], pslc)
                else:
                    scalar.copy(hT[:, k * 128 + 1:(k + 1) * 128],
                                pbank[:, (k % 4) * 128:(k % 4) * 128 + 127])
                    if j == 0:
                        scalar.copy(hT[:, k * 128:k * 128 + 1], sb["zcol"][:])
                    else:
                        scalar.copy(hT[:, k * 128:k * 128 + 1],
                                    pc_r[:, k - 4:k - 3])
                    ii = scalar.copy(pc_w[:, k - 4:k - 3],
                                     pbank[:, (k % 4) * 128 + 127:(k % 4) * 128 + 128])
            ctx.inc(ii, "se", 1)
            ctx.rec("ba_pbev", "se")

        se_hT = ctx.cnt["se"]

        @block.sync
        def _(sync, j=j):
            ctx.wait(sync, "ve", ctx.at("ba_rope_used_%d" % (j - 1)))
            for ri, nm in enumerate(("rope_cq", "rope_sq", "rope_ck", "rope_sk")):
                ap = consts[nm]
                rep = bass.AP(ap.tensor, ap.offset + j * 128 * 64,
                              [[64, 128], [0, 8], [1, 64]])
                ctx.inc(sync.dma_start(out=ropes[:, ri * 512:(ri + 1) * 512], in_=rep),
                        "ld", 16)

        ld_rope = ctx.cnt["ld"]

        @block.tensor
        def _(tensor, j=j, se_hT=se_hT):
            if j == 0:
                ctx.wait(tensor, "se", ctx.at("ba_init"))
            ctx.wait(tensor, "se", se_hT)
            ctx.wait(tensor, "ve", ctx.at("ba_qk_done_%d" % (j - 1)))
            ctx.wait(tensor, "ve", ctx.at("ba_v_done_%d" % (j - 1)))
            for wi, bank in ((0, p0), (1, p1), (2, p2)):
                for k in range(8):
                    ii = tensor.matmul(bank[:], hT[:, k * 128:(k + 1) * 128],
                                       wqkv_sb[:, k * 1536 + wi * 512:
                                               k * 1536 + (wi + 1) * 512],
                                       start=(k == 0), stop=(k == 7))
            ctx.inc(ii, "pe", 1)

        pe_qkv = ctx.cnt["pe"]

        @block.vector
        def _(vector, j=j, pe_qkv=pe_qkv):
            ctx.wait(vector, "pe", pe_qkv)
            ctx.inc(vector.tensor_copy(v_sb[:, j * 512:(j + 1) * 512], p2[:]), "ve", 1)
            ctx.rec("ba_v_done_%d" % j, "ve")

        for qk in range(2):
            bank = p0 if qk == 0 else p1
            nbuf = qn if qk == 0 else kn
            dstT = qrotT if qk == 0 else krotT

            se_base_qk = ctx.cnt["se"]
            ve_base_qk = ctx.cnt["ve"]

            @block.scalar
            def _(scalar, bank=bank, pe_qkv=pe_qkv, qk=qk, j=j, ve_base_qk=ve_base_qk):
                ctx.wait(scalar, "pe", pe_qkv)
                if qk == 0:
                    ctx.wait(scalar, "ve", ctx.at("ba_sqf_used_%d_%d" % (j - 1, 1)))
                else:
                    ctx.wait(scalar, "ve", ctx.at("ba_sqf_used_%d_%d" % (j, 0)))
                ctx.inc(scalar.activation(sqf[:], bank[:], AF.Square,
                                          bias=0.0, scale=1.0), "se", 1)
                sc = qk * 8
                scalar.wait_ge(ctx.sems["ve"], ve_base_qk + 2)
                ctx.inc(scalar.activation(ssb[:, 16 + sc:24 + sc], ssb[:, sc:sc + 8],
                                          AF.Ln, bias=0.0, scale=1.0), "se", 1)
                ctx.wait(scalar, "se")
                ctx.inc(scalar.activation(ssb[:, 16 + sc:24 + sc], ssb[:, 16 + sc:24 + sc],
                                          AF.Exp, bias=0.0, scale=-0.5), "se", 1)

            se_nr = ctx.cnt["se"]

            @block.vector
            def _(vector, qk=qk, se_nr=se_nr, bank=bank, nbuf=nbuf, ld_rope=ld_rope,
                  se_base_qk=se_base_qk, j=j):
                vector.wait_ge(ctx.sems["se"], se_base_qk + 1)
                sc = qk * 8
                ctx.inc(vector.tensor_reduce(ssb[:, sc:sc + 8],
                                             sqf[:].rearrange("p (a b) -> p a b", b=64),
                                             AX.X, ALU.add), "ve", 1)
                ctx.rec("ba_sqf_used_%d_%d" % (j, qk), "ve")
                ctx.wait(vector, "ve")
                ctx.inc(vector.tensor_scalar_max(ssb[:, sc:sc + 8],
                                                 ssb[:, sc:sc + 8], 1e-24), "ve", 1)
                vector.wait_ge(ctx.sems["se"], se_nr)
                for h in range(8):
                    ii = vector.tensor_scalar_mul(nbuf[:, h * 64:(h + 1) * 64],
                                                  bank[:, h * 64:(h + 1) * 64],
                                                  ssb[:, 16 + sc + h:17 + sc + h])
                ctx.inc(ii, "ve", 1)
                ctx.rec("ba_qk_done_%d" % 0, "ve")
                ctx.wait(vector, "ve")
                ctx.inc(vector.tensor_mul(nbuf[:], nbuf[:], qsb[:, qk * 512:(qk + 1) * 512]),
                        "ve", 1)
                ctx.wait(vector, "ve")
                ctx.wait(vector, "ld", ld_rope)
                for h in range(8):
                    vector.tensor_scalar_mul(rot[:, h * 64:h * 64 + 32],
                                             nbuf[:, h * 64 + 32:h * 64 + 64], -1.0)
                    ii = vector.tensor_copy(rot[:, h * 64 + 32:h * 64 + 64],
                                            nbuf[:, h * 64:h * 64 + 32])
                ctx.inc(ii, "ve", 1)
                ctx.wait(vector, "ve")
                co = ropes[:, (qk * 2) * 512:(qk * 2 + 1) * 512]
                si = ropes[:, (qk * 2 + 1) * 512:(qk * 2 + 2) * 512]
                vector.tensor_mul(satt[:, 0:512], nbuf[:], co)
                ctx.inc(vector.tensor_mul(rot[:], rot[:], si), "ve", 1)
                ctx.wait(vector, "ve")
                ctx.wait(vector, "pe", ctx.at("ba_attbtr"))
                ctx.inc(vector.tensor_add(attb[:, 0:512], satt[:, 0:512], rot[:]),
                        "ve", 1)

            ve_rope = ctx.cnt["ve"]

            @block.tensor
            def _(tensor, qk=qk, j=j, ve_rope=ve_rope):
                ctx.wait(tensor, "ve", ve_rope)
                ctx.wait(tensor, "se", ctx.at("ba_pbev"))
                for r in range(4):
                    ii = tensor.transpose(pb0[:, r * 128:(r + 1) * 128],
                                          attb[:, r * 128:(r + 1) * 128], idb[:])
                ctx.inc(ii, "pe", 1)
                ctx.rec("ba_attbtr", "pe")

            pe_qtr = ctx.cnt["pe"]

            @block.scalar
            def _(scalar, qk=qk, j=j, pe_qtr=pe_qtr, dstT=dstT):
                ctx.wait(scalar, "pe", pe_qtr)
                for r in range(4):
                    ii = scalar.copy(dstT[:, r * 4096 + j * 128:r * 4096 + (j + 1) * 128],
                                     pb0[:, r * 128:(r + 1) * 128])
                ctx.inc(ii, "se", 1)
                ctx.rec("ba_pbev", "se")

        ctx.rec("ba_qk_done_%d" % j, "ve")
        ctx.rec("ba_rope_used_%d" % j, "ve")
        ctx.rec("ba_attb_pe", "pe")

    ctx.rec("ba_prep_done_se", "se")
    ctx.rec("ba_prep_done_ve", "ve")

    p4, p5 = ctx.ps["p4"], ctx.ps["p5"]
    it = 0
    for tt in range(NT):
        w = tt // 4
        qp = tt % 4
        w0 = (w == 0)
        mb = tt % 2

        @block.sync
        def _(sync, tt=tt, qp=qp, w0=w0, mb=mb):
            ctx.wait(sync, "ve", ctx.at("ba_mask_used_%d" % (tt - 2)))
            mi = (4 + qp) if w0 else qp
            ctx.inc(sync.dma_start(out=mcur[:, mb * 1024:(mb + 1) * 1024],
                                   in_=consts["lmask_b"][mi * 128:(mi + 1) * 128, :]),
                    "ld", 16)

        ld_mask = ctx.cnt["ld"]

        for h in range(8):
            r, po = h // 2, (h % 2) * 64
            P = it % 2
            sA = p0 if P == 0 else p4
            sB = p1 if P == 0 else p5
            sat = satt[:, P * 1024:(P + 1) * 1024]
            ab = attb[:, P * 1024:(P + 1) * 1024]
            aT = attT[:, P * 1024:(P + 1) * 1024]
            dn_ = den[:, P * 4:P * 4 + 4]
            yg = ystg[0:64, P * 128:(P + 1) * 128]
            B = ctx.cnt["pe"]
            V = ctx.cnt["ve"]
            SS = ctx.cnt["se"]
            T = ctx.cnt["st"]
            first2 = (it < 2)

            @block.tensor
            def _(tensor, tt=tt, h=h, r=r, po=po, w=w, w0=w0, B=B, V=V, SS=SS,
                  it=it, first2=first2, sA=sA, sB=sB, ab=ab):
                if it == 0:
                    ctx.wait(tensor, "se", ctx.at("ba_prep_done_se"))
                    ctx.wait(tensor, "ve", ctx.at("ba_prep_done_ve"))
                if not first2:
                    tensor.wait_ge(ctx.sems["ve"], V - 7)
                if not w0:
                    tensor.matmul(sA[:], qrotT[po:po + 64, r * 4096 + tt * 128:
                                               r * 4096 + (tt + 1) * 128],
                                  krotT[po:po + 64, r * 4096 + (w - 1) * 512:
                                        r * 4096 + w * 512],
                                  start=True, stop=True)
                ii = tensor.matmul(sB[:], qrotT[po:po + 64, r * 4096 + tt * 128:
                                                r * 4096 + (tt + 1) * 128],
                                   krotT[po:po + 64, r * 4096 + w * 512:
                                         r * 4096 + (w + 1) * 512],
                                   start=True, stop=True)
                ctx.inc(ii, "pe", 1)
                tensor.wait_ge(ctx.sems["ve"], V + 4)
                tensor.wait_ge(ctx.sems["se"],
                               ctx.at("ba_pbev") if it == 0 else max(SS - 1, 0))
                for s_ in range(8):
                    ii = tensor.transpose(pb0[:, (s_ % 4) * 128:(s_ % 4) * 128 + 128] if s_ < 4
                                          else pb1[:, (s_ % 4) * 128:(s_ % 4) * 128 + 128],
                                          ab[:, s_ * 128:(s_ + 1) * 128], idb[:])
                ctx.inc(ii, "pe", 1)
                tensor.wait_ge(ctx.sems["se"], SS + 2)
                srange = range(4, 8) if w0 else range(8)
                first = 4 if w0 else 0
                for s_ in srange:
                    kt = (w - 1) * 4 + s_
                    ii = tensor.matmul(p3[0:64, 0:128],
                                       v_sb[:, kt * 512 + h * 64:kt * 512 + (h + 1) * 64],
                                       aT[:, s_ * 128:(s_ + 1) * 128],
                                       start=(s_ == first), stop=(s_ == 7))
                ctx.inc(ii, "pe", 1)

            @block.vector
            def _(vector, qp=qp, w0=w0, mb=mb, ld_mask=ld_mask, tt=tt, h=h,
                  B=B, V=V, SS=SS, it=it, first2=first2, sA=sA, sB=sB,
                  sat=sat, ab=ab, dn_=dn_):
                vector.wait_ge(ctx.sems["pe"], B + 1)
                ctx.wait(vector, "ld", ld_mask)
                if not first2:
                    vector.wait_ge(ctx.sems["se"], max(SS - 5, 0))
                if not w0:
                    vector.tensor_add(sat[:, 0:512], sA[:],
                                      mcur[:, mb * 1024:mb * 1024 + 512])
                ii = vector.tensor_add(sat[:, 512:1024], sB[:],
                                       mcur[:, mb * 1024 + 512:(mb + 1) * 1024])
                ctx.inc(ii, "ve", 1)
                ctx.rec("ba_mask_used_%d" % tt, "ve")
                ctx.wait(vector, "ve")
                lo = 512 if w0 else 0
                ctx.inc(vector.tensor_reduce(dn_[:, 0:1], sat[:, lo:1024], AX.X,
                                             ALU.max, negate=True), "ve", 1)
                vector.wait_ge(ctx.sems["se"], SS + 1)
                ctx.inc(vector.reciprocal(dn_[:, 2:3], dn_[:, 1:2]), "ve", 1)
                ctx.wait(vector, "ve")
                if not first2:
                    vector.wait_ge(ctx.sems["pe"], max(B - 4, 0))
                if w0:
                    vector.memset(ab[:, 0:512], 0.0)
                ctx.inc(vector.tensor_scalar_mul(ab[:, lo:1024], sat[:, lo:1024],
                                                 dn_[:, 2:3]), "ve", 1)

            @block.scalar
            def _(scalar, tt=tt, h=h, w0=w0, B=B, V=V, SS=SS, T=T,
                  sat=sat, aT=aT, dn_=dn_, yg=yg):
                lo = 512 if w0 else 0
                scalar.wait_ge(ctx.sems["ve"], V + 2)
                ctx.inc(scalar.activation(sat[:, lo:1024], sat[:, lo:1024], AF.Exp,
                                          bias=dn_[:, 0:1], scale=1.0,
                                          accum_out=dn_[:, 1:2]), "se", 1)
                scalar.wait_ge(ctx.sems["pe"], B + 2)
                for s_ in range(8):
                    pbank = pb0 if s_ < 4 else pb1
                    ii = scalar.copy(aT[:, s_ * 128:(s_ + 1) * 128],
                                     pbank[:, (s_ % 4) * 128:(s_ % 4) * 128 + 128])
                ctx.inc(ii, "se", 1)
                scalar.wait_ge(ctx.sems["pe"], B + 3)
                scalar.wait_ge(ctx.sems["st"], max(T - 1, 0))
                ctx.inc(scalar.copy(yg, p3[0:64, 0:128]), "se", 1)

            se_y = ctx.cnt["se"]

            @block.sync
            def _(sync, tt=tt, h=h, se_y=se_y, yg=yg):
                ctx.wait(sync, "se", se_y)
                ctx.inc(sync.dma_start(
                    out=yT_d[h * 64:(h + 1) * 64, tt * 128:(tt + 1) * 128],
                    in_=yg), "st", 16)
                ctx.rec("ba_yst_st", "st")
            it += 1

    ctx.rec("ba_attn_done_st", "st")

    for tt in range(NT):
        bb = tt % 2

        @block.sync
        def _(sync, tt=tt, bb=bb):
            ctx.wait(sync, "st", ctx.at("ba_attn_done_st"))
            ctx.wait(sync, "ve", ctx.at("ba_ores_%d" % (tt - 1)))
            ctx.wait(sync, "pe", ctx.at("ba_womm_%d" % (tt - 2)))
            for r in range(4):
                ctx.inc(sync.dma_start(
                    out=ybuf[:, bb * 512 + r * 128:bb * 512 + (r + 1) * 128],
                    in_=yT_d[r * 128:(r + 1) * 128, tt * 128:(tt + 1) * 128]), "ld", 16)
            ctx.inc(sync.dma_start(out=xt[:, bb * 1024:(bb + 1) * 1024],
                                   in_=xi_in[tt * 128:(tt + 1) * 128, :]), "ld", 16)

        ld_now = ctx.cnt["ld"]

        for oc in range(2):
            @block.tensor
            def _(tensor, tt=tt, oc=oc, ld_now=ld_now, bb=bb):
                if oc == 0:
                    ctx.wait(tensor, "ld", ld_now)
                ctx.wait(tensor, "ve", ctx.at("ba_ores_%d" % (tt - 1)))
                bank = p0 if oc == 0 else p1
                for r in range(4):
                    ii = tensor.matmul(bank[:],
                                       ybuf[:, bb * 512 + r * 128:bb * 512 + (r + 1) * 128],
                                       wo_sb[:, r * 1024 + oc * 512:r * 1024 + (oc + 1) * 512],
                                       start=(r == 0), stop=(r == 3))
                ctx.inc(ii, "pe", 1)
                if oc == 1:
                    ctx.rec("ba_womm_%d" % tt, "pe")

        pe_wo = ctx.cnt["pe"]

        @block.vector
        def _(vector, tt=tt, pe_wo=pe_wo, ld_now=ld_now, bb=bb):
            ctx.wait(vector, "pe", pe_wo)
            ctx.wait(vector, "ld", ld_now)
            ctx.wait(vector, "st", ctx.at("ba_ost_%d" % (tt - 1)))
            vector.tensor_add(obuf[:, 0:512], p0[:], xt[:, bb * 1024:bb * 1024 + 512])
            ctx.inc(vector.tensor_add(obuf[:, 512:1024], p1[:],
                                      xt[:, bb * 1024 + 512:(bb + 1) * 1024]),
                    "ve", 1)
            ctx.rec("ba_ores_%d" % tt, "ve")

        ve_o = ctx.cnt["ve"]

        @block.sync
        def _(sync, tt=tt, ve_o=ve_o):
            ctx.wait(sync, "ve", ve_o)
            ctx.inc(sync.dma_start(out=xi_out[tt * 128:(tt + 1) * 128, :],
                                   in_=obuf[:, 0:1024]), "st", 16)
            ctx.rec("ba_ost_%d" % tt, "st")


def alloc_battn(ctx):
    ctx.sbuf("xt", [128, 2048], F32)
    ctx.sbuf("sq", [128, 1024], F32)
    ctx.sbuf("stat", [128, 8], F32)
    ctx.sbuf("idb", [128, 128], BF16)
    ctx.sbuf("htile", [128, 1024], F32)
    ctx.sbuf("hbt", [128, 1024], BF16)
    ctx.sbuf("hT", [128, 1024], BF16)
    ctx.sbuf("wqkv_sb", [128, 8 * 1536], BF16)
    ctx.sbuf("wo_sb", [128, 4 * 1024], BF16)
    ctx.sbuf("mcur", [128, 2048], BF16)
    ctx.sbuf("ropes", [128, 4 * 512], F32)
    ctx.sbuf("qsb", [128, 1024], F32)
    ctx.sbuf("sqf", [128, 512], F32)
    ctx.sbuf("ssb", [128, 32], F32)
    ctx.sbuf("qn", [128, 512], F32)
    ctx.sbuf("kn", [128, 512], F32)
    ctx.sbuf("rot", [128, 512], F32)
    ctx.sbuf("qrotT", [128, 4 * 4096], BF16)
    ctx.sbuf("krotT", [128, 4 * 4096], BF16)
    ctx.sbuf("v_sb", [128, 32 * 512], BF16)
    ctx.sbuf("ystg", [64, 256], BF16)
    ctx.sbuf("ybuf", [128, 1024], BF16)
    ctx.sbuf("satt", [128, 2048], F32)
    ctx.sbuf("attb", [128, 2048], BF16)
    ctx.sbuf("attT", [128, 2048], BF16)
    ctx.sbuf("den", [128, 8], F32)
    ctx.sbuf("obuf", [128, 1024], F32)
    ctx.sbuf("onefr", [1, 128], F32)
    ctx.sbuf("qsrow", [1, 1024], F32)
    ctx.sbuf("prevcol", [128, 8], BF16)
    ctx.sbuf("zcol", [128, 1], BF16)


DEBUG_BA = False

def build_battn(sub, do_ln):
    nc = bass.Bass("TRN2", target_bir_lowering=False)
    x_in = nc.dram_tensor("x_in", [S, D], F32, kind="ExternalInput").ap()
    xi_in = nc.dram_tensor("xi_in", [S, D], F32,
                           kind="ExternalInput" if not do_ln else "ExternalOutput").ap()
    wqkv = nc.dram_tensor("wqkv", [D, 1536], F32, kind="ExternalInput").ap()
    qs = nc.dram_tensor("qs", [64], F32, kind="ExternalInput").ap()
    ks = nc.dram_tensor("ks", [64], F32, kind="ExternalInput").ap()
    wo = nc.dram_tensor("wo", [INNER, D], F32, kind="ExternalInput").ap()
    ident_b = nc.dram_tensor("ident_b", [128, 128], BF16, kind="ExternalInput").ap()
    lmask_b = nc.dram_tensor("lmask_b", [1024, 1024], BF16, kind="ExternalInput").ap()
    rope_cq = nc.dram_tensor("rope_cq", [S, 64], F32, kind="ExternalInput").ap()
    rope_sq = nc.dram_tensor("rope_sq", [S, 64], F32, kind="ExternalInput").ap()
    rope_ck = nc.dram_tensor("rope_ck", [S, 64], F32, kind="ExternalInput").ap()
    rope_sk = nc.dram_tensor("rope_sk", [S, 64], F32, kind="ExternalInput").ap()
    fconsts = nc.dram_tensor("fconsts", [1, 64], F32, kind="ExternalInput").ap()
    xi_out = nc.dram_tensor("xi_out", [S, D], F32, kind="ExternalOutput").ap()
    yT_d = nc.dram_tensor("yT_d", [INNER, S], BF16,
                          kind="ExternalOutput" if DEBUG_BA else "Internal").ap()
    if DEBUG_BA:
        dbg_qT = nc.dram_tensor("dbg_qT", [128, 4 * 4096], BF16, kind="ExternalOutput").ap()
        dbg_kT = nc.dram_tensor("dbg_kT", [128, 4 * 4096], BF16, kind="ExternalOutput").ap()
        dbg_v = nc.dram_tensor("dbg_v", [128, 32 * 512], BF16, kind="ExternalOutput").ap()
        dbg_htile = nc.dram_tensor("dbg_htile", [128, 1024], F32, kind="ExternalOutput").ap()
        dbg_hbt = nc.dram_tensor("dbg_hbt", [128, 1024], BF16, kind="ExternalOutput").ap()

    ctx = Ctx(nc)
    with ctx.es:
        alloc_shared(ctx)
        alloc_battn(ctx)
        cp = ctx.sb["cpool"]
        for i, v in enumerate(FCONSTS):
            nc.const_aps.register(F32, float(v), cp[:, i:i + 1])
        with nc.Block() as block:
            @block.sync
            def _(sync):
                fc_rep = bass.AP(fconsts.tensor, 0, [[0, 128], [1, len(FCONSTS)]])
                ctx.inc(sync.dma_start(out=cp[:, 0:len(FCONSTS)], in_=fc_rep), "ld", 16)

            emit_battn(ctx, block, sub, x_in, xi_in, xi_out,
                       {"wqkv": wqkv, "qs": qs, "ks": ks, "wo": wo},
                       {"ident_b": ident_b, "lmask_b": lmask_b,
                        "rope_cq": rope_cq, "rope_sq": rope_sq,
                        "rope_ck": rope_ck, "rope_sk": rope_sk,
                        "dbg_htile": dbg_htile if DEBUG_BA else None,
                        "dbg_hbt": dbg_hbt if DEBUG_BA else None}, do_ln, yT_d)

            @block.gpsimd
            def _(gpsimd):
                ctx.wait(gpsimd, "st")
                ctx.wait(gpsimd, "ve")
                ctx.wait(gpsimd, "se")
                if DEBUG_BA:
                    ctx.inc(gpsimd.dma_start(out=dbg_qT, in_=ctx.sb["qrotT"][:]), "st", 16)
                    ctx.inc(gpsimd.dma_start(out=dbg_kT, in_=ctx.sb["krotT"][:]), "st", 16)
                    ctx.inc(gpsimd.dma_start(out=dbg_v, in_=ctx.sb["v_sb"][:]), "st", 16)
                    ctx.wait(gpsimd, "st")
    return nc



def emit_bffn(ctx, block, sub, xi_in, xi_out, x1_in, w_aps, consts, final):
    sb, ps = ctx.sb, ctx.ps
    xt = sb["xt"]; stat = sb["stat"]; idb = sb["idb"]; sq = sb["sq"]
    htile = sb["htile"]; hbt = sb["hbt"]; hT = sb["hT"]
    w1_sb = sb["w1_sb"]; w2_sb = sb["w2_sb"]
    b1b = sb["b1b"]; lgb = sb["lgb"]; lbb = sb["lbb"]; b2b = sb["b2b"]
    u = sb["u"]; hmb = sb["hmb"]; hmT = sb["hmT"]
    obuf = sb["obuf"]; onefr = sb["onefr"]; brow = sb["brow"]
    p0, p1, p2, p3 = ps["p0"], ps["p1"], ps["p2"], ps["p3"]
    pb0, pb1 = ps["pb0"], ps["pb1"]

    @block.sync
    def _(sync):
        ctx.inc(sync.dma_start(out=idb[:], in_=consts["ident_b"]), "ld", 16)

    @block.gpsimd
    def _(gpsimd):
        for k in range(8):
            ctx.inc(gpsimd.dma_start(out=w1_sb[:, k * HID2:(k + 1) * HID2],
                                     in_=w_aps["w1"][k * 128:(k + 1) * 128, :]),
                    "gld", 16)
        for k in range(21):
            ctx.inc(gpsimd.dma_start(out=w2_sb[:, k * 1024:(k + 1) * 1024],
                                     in_=w_aps["w2"][k * 128:(k + 1) * 128, :]),
                    "gld", 16)
        ctx.inc(gpsimd.dma_start(out=w2_sb[0:42, 21 * 1024:22 * 1024],
                                 in_=w_aps["w2"][21 * 128:2730, :]), "gld", 16)

    st_w = ctx.cnt["gld"]

    @block.vector
    def _(vector):
        ctx.inc(vector.memset(onefr[:], 1.0), "ve", 1)

    ve_one = ctx.cnt["ve"]

    bjobs = []
    for (u0, ulen) in UCH:
        bjobs.append((w_aps["b1"], u0, ulen, b1b, u0))
    for oc in range(2):
        bjobs.append((w_aps["lg"], oc * 512, 512, lgb, oc * 512))
        bjobs.append((w_aps["lb"], oc * 512, 512, lbb, oc * 512))
        bjobs.append((w_aps["b2"], oc * 512, 512, b2b, oc * 512))
    for bi, (ap, off, ln_, dstb, doff) in enumerate(bjobs):
        @block.sync
        def _(sync, ap=ap, off=off, ln_=ln_, bi=bi):
            ctx.wait(sync, "pe", ctx.at("bf_bmm_%d" % (bi - 1)))
            ctx.inc(sync.dma_start(out=brow[0:1, 0:ln_], in_=ap[off:off + ln_]),
                    "ld", 16)

        ld_b = ctx.cnt["ld"]

        @block.tensor
        def _(tensor, bi=bi, ln_=ln_, ld_b=ld_b):
            ctx.wait(tensor, "ld", ld_b)
            if bi == 0:
                ctx.wait(tensor, "ve", ve_one)
            ctx.wait(tensor, "se", ctx.at("bf_bev_%d" % (bi - 1)))
            ctx.inc(tensor.matmul(p0[:, 0:ln_], onefr[0:1, :], brow[0:1, 0:ln_],
                                  start=True, stop=True), "pe", 1)
            ctx.rec("bf_bmm_%d" % bi, "pe")

        pe_b = ctx.cnt["pe"]

        @block.scalar
        def _(scalar, bi=bi, dstb=dstb, doff=doff, ln_=ln_, pe_b=pe_b):
            ctx.wait(scalar, "pe", pe_b)
            ctx.inc(scalar.copy(dstb[:, doff:doff + ln_], p0[:, 0:ln_]), "se", 1)
            ctx.rec("bf_bev_%d" % bi, "se")

    ctx.rec("bf_init_se", "se")

    for j in range(NT):
        @block.sync
        def _(sync, j=j):
            ctx.wait(sync, "ve", ctx.at("bf_xt_used_%d" % (j - 1)))
            ctx.inc(sync.dma_start(out=xt[:, 0:1024],
                                   in_=xi_in[j * 128:(j + 1) * 128, :]), "ld", 16)
            if j == 0:
                ctx.inc(sync.dma_start(out=htile[1:128, 512:1024],
                                       in_=xi_in[0:127, 512:1024]), "ld", 16)
            else:
                ctx.inc(sync.dma_start(out=htile[0:128, 512:1024],
                                       in_=xi_in[j * 128 - 1:j * 128 + 127, 512:1024]),
                        "ld", 16)
            if final:
                ctx.inc(sync.dma_start(out=xt[:, 1024:2048],
                                       in_=x1_in[j * 128:(j + 1) * 128, :]), "ld", 16)

        ld_now = ctx.cnt["ld"]

        @block.vector
        def _(vector, j=j, ld_now=ld_now):
            ctx.wait(vector, "ld", ld_now)
            ctx.wait(vector, "pe", ctx.at("bf_htr_%d" % (j - 1)))
            if j == 0:
                vector.memset(htile[0:1, 512:1024], 0.0)
            ctx.inc(vector.tensor_copy(htile[:, 0:512], xt[:, 0:512]), "ve", 1)

        ve_ht = ctx.cnt["ve"]
        emit_ln_stats(ctx, block, htile[:], sq[:, 0:1024], stat, 0, D, ve_wait=ve_ht)
        se_rs = ctx.cnt["se"]

        @block.vector
        def _(vector, j=j, se_rs=se_rs):
            ctx.wait(vector, "se", se_rs)
            ctx.inc(vector.tensor_scalar(sq[:, 0:1024], htile[:], stat[:, 0:1],
                                         stat[:, 2:3], ALU.subtract, ALU.mult),
                    "ve", 1)
            ctx.wait(vector, "ve")
            ctx.inc(vector.tensor_mul(sq[:, 0:1024], sq[:, 0:1024], lgb[:]), "ve", 1)
            ctx.wait(vector, "ve")
            ctx.inc(vector.tensor_add(hbt[:], sq[:, 0:1024], lbb[:]), "ve", 1)

        ve_h = ctx.cnt["ve"]

        @block.tensor
        def _(tensor, ve_h=ve_h, j=j):
            ctx.wait(tensor, "ve", ve_h)
            ctx.wait(tensor, "se", ctx.at("bf_pbev"))
            for k in range(8):
                ii = tensor.transpose(pb0[:, (k % 4) * 128:(k % 4) * 128 + 128] if k < 4
                                      else pb1[:, (k % 4) * 128:(k % 4) * 128 + 128],
                                      hbt[:, k * 128:(k + 1) * 128], idb[:])
            ctx.inc(ii, "pe", 1)
            ctx.rec("bf_htr_%d" % j, "pe")

        pe_tr = ctx.cnt["pe"]

        @block.scalar
        def _(scalar, pe_tr=pe_tr):
            ctx.wait(scalar, "pe", pe_tr)
            for k in range(8):
                pbank = pb0 if k < 4 else pb1
                ii = scalar.copy(hT[:, k * 128:(k + 1) * 128],
                                 pbank[:, (k % 4) * 128:(k % 4) * 128 + 128])
            ctx.inc(ii, "se", 1)
            ctx.rec("bf_pbev", "se")

        se_hT = ctx.cnt["se"]

        for ci, (u0, ulen) in enumerate(UCH):
            @block.tensor
            def _(tensor, ci=ci, u0=u0, ulen=ulen, se_hT=se_hT, j=j):
                if ci == 0:
                    ctx.wait(tensor, "se", se_hT)
                    if j == 0:
                        ctx.wait(tensor, "se", ctx.at("bf_init_se"))
                bank = p0 if (ci % 2 == 0) else p1
                ctx.wait(tensor, "ve", ctx.at("bf_uev_%s" % ("e" if ci % 2 == 0 else "o")))
                if ci == 0 and j == 0:
                    ctx.wait(tensor, "gld", st_w)
                for k in range(8):
                    ii = tensor.matmul(bank[:, 0:ulen],
                                       hT[:, k * 128:(k + 1) * 128],
                                       w1_sb[:, k * HID2 + u0:k * HID2 + u0 + ulen],
                                       start=(k == 0), stop=(k == 7))
                ctx.inc(ii, "pe", 1)

            pe_u = ctx.cnt["pe"]

            @block.vector
            def _(vector, ci=ci, u0=u0, ulen=ulen, pe_u=pe_u):
                ctx.wait(vector, "pe", pe_u)
                bank = p0 if (ci % 2 == 0) else p1
                ctx.inc(vector.scalar_tensor_tensor(u[:, u0:u0 + ulen], bank[:, 0:ulen],
                                                    1.0, b1b[:, u0:u0 + ulen],
                                                    ALU.bypass, ALU.add), "ve", 1)
                ctx.rec("bf_uev_%s" % ("e" if ci % 2 == 0 else "o"), "ve")

        ve_u = ctx.cnt["ve"]

        @block.scalar
        def _(scalar, ve_u=ve_u):
            ctx.wait(scalar, "ve", ve_u)
            ctx.inc(scalar.activation(u[:, HID:HID2], u[:, HID:HID2], AF.Gelu,
                                      bias=0.0, scale=1.0), "se", 1)

        se_g = ctx.cnt["se"]

        @block.vector
        def _(vector, se_g=se_g):
            ctx.wait(vector, "se", se_g)
            ctx.wait(vector, "pe", ctx.at("bf_hmtr_0"))
            ctx.inc(vector.tensor_mul(hmb[:], u[:, 0:HID], u[:, HID:HID2]), "ve", 1)

        ve_hm = ctx.cnt["ve"]

        for g0 in range(0, 22, 8):
            gt = list(range(g0, min(g0 + 8, 22)))

            @block.tensor
            def _(tensor, gt=gt, g0=g0, ve_hm=ve_hm, j=j):
                if g0 == 0:
                    ctx.wait(tensor, "ve", ve_hm)
                ctx.wait(tensor, "se", ctx.at("bf_pbev"))
                for k in gt:
                    w_ = min(128, HID - k * 128)
                    slot = (k % 8) // 2
                    pbank = pb0 if k % 2 == 0 else pb1
                    ii = tensor.transpose(pbank[0:w_, slot * 128:slot * 128 + 128],
                                          hmb[:, k * 128:k * 128 + w_], idb[:])
                ctx.inc(ii, "pe", 1)
                if g0 == 16:
                    ctx.rec("bf_hmtr_0", "pe")

            pe_g = ctx.cnt["pe"]

            @block.scalar
            def _(scalar, gt=gt, pe_g=pe_g):
                ctx.wait(scalar, "pe", pe_g)
                for k in gt:
                    w_ = min(128, HID - k * 128)
                    slot = (k % 8) // 2
                    pbank = pb0 if k % 2 == 0 else pb1
                    ii = scalar.copy(hmT[0:w_, k * 128:(k + 1) * 128],
                                     pbank[0:w_, slot * 128:slot * 128 + 128])
                ctx.inc(ii, "se", 1)
                ctx.rec("bf_pbev", "se")

        se_hmT = ctx.cnt["se"]

        for oc in range(2):
            @block.tensor
            def _(tensor, oc=oc, se_hmT=se_hmT):
                if oc == 0:
                    ctx.wait(tensor, "se", se_hmT)
                ctx.wait(tensor, "ve", ctx.at("bf_res_ve"))
                bank = p2 if oc == 0 else p3
                for k in range(22):
                    kw = min(128, HID - k * 128)
                    ii = tensor.matmul(bank[:],
                                       hmT[0:kw, k * 128:(k + 1) * 128],
                                       w2_sb[0:kw, k * 1024 + oc * 512:
                                             k * 1024 + (oc + 1) * 512],
                                       start=(k == 0), stop=(k == 21))
                ctx.inc(ii, "pe", 1)

        pe_w2 = ctx.cnt["pe"]

        @block.vector
        def _(vector, j=j, pe_w2=pe_w2, ld_now=ld_now):
            ctx.wait(vector, "pe", pe_w2)
            ctx.wait(vector, "st", ctx.at("bf_ost_%d" % (j - 1)))
            vector.scalar_tensor_tensor(obuf[:, 0:512], p2[:], 1.0, b2b[:, 0:512],
                                        ALU.bypass, ALU.add)
            ctx.inc(vector.scalar_tensor_tensor(obuf[:, 512:1024], p3[:], 1.0,
                                                b2b[:, 512:1024], ALU.bypass, ALU.add),
                    "ve", 1)
            ctx.wait(vector, "ve")
            ctx.inc(vector.tensor_add(obuf[:, 0:1024], obuf[:, 0:1024], xt[:, 0:1024]),
                    "ve", 1)
            if final:
                ctx.wait(vector, "ve")
                ctx.inc(vector.tensor_add(obuf[:, 0:1024], obuf[:, 0:1024],
                                          xt[:, 1024:2048]), "ve", 1)
            ctx.rec("bf_res_ve", "ve")
            ctx.rec("bf_xt_used_%d" % j, "ve")

        ve_o = ctx.cnt["ve"]

        @block.sync
        def _(sync, j=j, ve_o=ve_o):
            ctx.wait(sync, "ve", ve_o)
            ctx.inc(sync.dma_start(out=xi_out[j * 128:(j + 1) * 128, :],
                                   in_=obuf[:, 0:1024]), "st", 16)
            ctx.rec("bf_ost_%d" % j, "st")


def alloc_bffn(ctx):
    ctx.sbuf("xt", [128, 2048], F32)
    ctx.sbuf("sq", [128, 1024], F32)
    ctx.sbuf("stat", [128, 8], F32)
    ctx.sbuf("idb", [128, 128], BF16)
    ctx.sbuf("htile", [128, 1024], F32)
    ctx.sbuf("hbt", [128, 1024], BF16)
    ctx.sbuf("hT", [128, 1024], BF16)
    ctx.sbuf("w1_sb", [128, 8 * HID2], BF16)
    ctx.sbuf("w2_sb", [128, 22 * 1024], BF16)
    ctx.sbuf("b1b", [128, HID2], BF16)
    ctx.sbuf("lgb", [128, 1024], BF16)
    ctx.sbuf("lbb", [128, 1024], BF16)
    ctx.sbuf("b2b", [128, 1024], BF16)
    ctx.sbuf("u", [128, HID2], F32)
    ctx.sbuf("hmb", [128, HID], BF16)
    ctx.sbuf("hmT", [128, 22 * 128], BF16)
    ctx.sbuf("obuf", [128, 1024], F32)
    ctx.sbuf("onefr", [1, 128], F32)
    ctx.sbuf("brow", [1, 512], F32)


def build_bffn(sub, final):
    nc = bass.Bass("TRN2", target_bir_lowering=False)
    xi_in = nc.dram_tensor("xi_in", [S, D], F32, kind="ExternalInput").ap()
    x1_in = nc.dram_tensor("x1_in", [S, D], F32, kind="ExternalInput").ap()
    w1 = nc.dram_tensor("w1", [D, HID2], F32, kind="ExternalInput").ap()
    b1 = nc.dram_tensor("b1", [HID2], F32, kind="ExternalInput").ap()
    w2 = nc.dram_tensor("w2", [HID, D], F32, kind="ExternalInput").ap()
    b2 = nc.dram_tensor("b2", [D], F32, kind="ExternalInput").ap()
    lg = nc.dram_tensor("lg", [D], F32, kind="ExternalInput").ap()
    lb = nc.dram_tensor("lb", [D], F32, kind="ExternalInput").ap()
    ident_b = nc.dram_tensor("ident_b", [128, 128], BF16, kind="ExternalInput").ap()
    fconsts = nc.dram_tensor("fconsts", [1, 64], F32, kind="ExternalInput").ap()
    xi_out = nc.dram_tensor("xi_out", [S, D], F32, kind="ExternalOutput").ap()

    ctx = Ctx(nc)
    with ctx.es:
        alloc_shared(ctx)
        alloc_bffn(ctx)
        cp = ctx.sb["cpool"]
        for i, v in enumerate(FCONSTS):
            nc.const_aps.register(F32, float(v), cp[:, i:i + 1])
        with nc.Block() as block:
            @block.sync
            def _(sync):
                fc_rep = bass.AP(fconsts.tensor, 0, [[0, 128], [1, len(FCONSTS)]])
                ctx.inc(sync.dma_start(out=cp[:, 0:len(FCONSTS)], in_=fc_rep), "ld", 16)

            emit_bffn(ctx, block, sub, xi_in, xi_out, x1_in,
                      {"w1": w1, "b1": b1, "w2": w2, "b2": b2, "lg": lg, "lb": lb},
                      {"ident_b": ident_b}, final)

            @block.gpsimd
            def _(gpsimd):
                ctx.wait(gpsimd, "st")
    return nc


def emit_boundary(ctx, block, idx):
    tot = {s: ctx.cnt[s] for s in ("ld", "st", "pe", "ve", "se")}

    @block.sync
    def _(sync):
        for s in ("ld", "st", "pe", "ve", "se"):
            if tot[s] > 0:
                sync.wait_ge(ctx.sems[s], tot[s])
        if ctx.cnt["gld"] > 0:
            sync.wait_ge(ctx.sems["gld"], ctx.cnt["gld"])
        for s in ("ld", "st", "pe", "ve", "se"):
            sync.sem_clear(ctx.sems[s])
        sync.sem_inc(ctx.sems["gp"], 1)

    ctx.cnt["gp"] += 1
    gval = ctx.cnt["gp"]

    for engname in ("tensor", "vector", "scalar", "gpsimd"):
        def mk(eng, gval=gval):
            eng.wait_ge(ctx.sems["gp"], gval)
        getattr(block, engname)(mk)
    ctx.reset_counts()


def build_fused(start=0, upto=7, nofinal=False):
    nc = bass.Bass("TRN2", target_bir_lowering=False)
    g = {}
    def ein(name, shape, dtype=F32):
        g[name] = nc.dram_tensor(name, shape, dtype, kind="ExternalInput").ap()
        return g[name]

    def state(name, produced_in, consumed_in):
        prod = [s for s in produced_in if start <= s < upto]
        cons = [s for s in consumed_in if start <= s < upto]
        if not prod and not cons:
            kind = None
        elif prod and (max(consumed_in, default=-1) >= upto or not cons):
            kind = "ExternalOutput"
        elif not prod and cons:
            kind = "ExternalInput"
        else:
            kind = "Internal"
        if kind:
            g[name] = nc.dram_tensor(name, [S, D], F32, kind=kind).ap()

    if start <= 0 < upto or start <= 1 < upto:
        pass
    ein("x_in", [S, D])
    if start <= 0 < upto:
        for w in ("Wg", "Wv", "Wd"):
            ein(w, [D, D])
        for b in ("bg", "bv", "bd"):
            ein(b, [D])
    for sub in range(2):
        if start <= 1 + 2 * sub < upto:
            ein("wqkv%d" % sub, [D, 1536]); ein("qs%d" % sub, [64]); ein("ks%d" % sub, [64])
            ein("wo%d" % sub, [INNER, D])
        if start <= 2 + 2 * sub < upto:
            ein("w1_%d" % sub, [D, HID2]); ein("b1_%d" % sub, [HID2])
            ein("w2_%d" % sub, [HID, D]); ein("b2_%d" % sub, [D])
            ein("lg%d" % sub, [D]); ein("lb%d" % sub, [D])
    if start <= 5 < upto:
        for w in ("CWq", "CWk", "CWv", "CWp"):
            ein(w, [D, D])
        for b in ("cbq", "cbk", "cbv", "cbp"):
            ein(b, [D])
    if start <= 6 < upto:
        ein("wg", [IC, D]); ein("wu", [IC, D]); ein("wd", [D, IC])
    ein("ident_b", [128, 128], BF16); ein("ident_f", [128, 128])
    ein("fconsts", [1, 64]); ein("lmask_b", [1024, 1024], BF16)
    for r in ("rope_cq", "rope_sq", "rope_ck", "rope_sk"):
        ein(r, [S, 64])

    state("xs0", [0], [1, 4])
    state("xi_a", [1, 2], [1, 3])
    state("xi_b", [1, 3], [2, 4])
    state("x2_d", [4], [5])
    state("x3_d", [5], [6])
    if upto >= 7:
        x_out = nc.dram_tensor("x_out", [S, D], F32, kind="ExternalOutput").ap()
    else:
        x_out = None

    def intl(name, shape, dtype=F32):
        g[name] = nc.dram_tensor(name, shape, dtype, kind="Internal").ap()
    if start <= 5 < upto:
        intl("xlnT_d", [D, S], BF16); intl("qT_d", [D, S], BF16)
        intl("kT_d", [D, S], BF16); intl("yT_dC", [D, S], BF16)
        intl("cscr", [128, 2])
    if any(start <= s < upto for s in (1, 3)):
        intl("yT_dB", [INNER, S], BF16)

    ctx = Ctx(nc)
    with ctx.es:
        alloc_shared(ctx)
        cp = ctx.sb["cpool"]
        for i, v in enumerate(FCONSTS):
            nc.const_aps.register(F32, float(v), cp[:, i:i + 1])
        with nc.Block() as block:
            @block.sync
            def _(sync):
                fc_rep = bass.AP(g["fconsts"].tensor, 0, [[0, 128], [1, len(FCONSTS)]])
                ctx.inc(sync.dma_start(out=cp[:, 0:len(FCONSTS)], in_=fc_rep), "ld", 16)

            first = True
            bno = 0

            def boundary():
                nonlocal first, bno
                if not first:
                    pass
                bno += 1
                emit_boundary(ctx, block, bno)

            if start <= 0 < upto:
                ctx.push_scope()
                alloc_common(ctx)
                emit_block_A(ctx, block, g["x_in"], g["xs0"],
                             {"Wg": g["Wg"], "Wv": g["Wv"], "Wd": g["Wd"]},
                             {"bg": g["bg"], "bv": g["bv"], "bd": g["bd"]},
                             {"ident_b": g["ident_b"], "ident_f": g["ident_f"]})
                ctx.pop_scope()
                if upto > 1:
                    boundary()

            for sub in range(2):
                if start <= 1 + 2 * sub < upto:
                    bcst = {"ident_b": g["ident_b"], "lmask_b": g["lmask_b"],
                            "rope_cq": g["rope_cq"], "rope_sq": g["rope_sq"],
                            "rope_ck": g["rope_ck"], "rope_sk": g["rope_sk"],
                            "dbg_htile": None, "dbg_hbt": None}
                    ctx.push_scope()
                    alloc_battn(ctx)
                    emit_battn(ctx, block, sub,
                               g.get("xs0"), g["xi_a"], g["xi_b"],
                               {"wqkv": g["wqkv%d" % sub], "qs": g["qs%d" % sub],
                                "ks": g["ks%d" % sub], "wo": g["wo%d" % sub]},
                               bcst, sub == 0, g["yT_dB"])
                    ctx.pop_scope()
                    if upto > 2 + 2 * sub:
                        boundary()
                if start <= 2 + 2 * sub < upto:
                    ctx.push_scope()
                    alloc_bffn(ctx)
                    emit_bffn(ctx, block, sub, g["xi_b"],
                              g["xi_a"] if sub == 0 else g["x2_d"],
                              g.get("xs0"),
                              {"w1": g["w1_%d" % sub], "b1": g["b1_%d" % sub],
                               "w2": g["w2_%d" % sub], "b2": g["b2_%d" % sub],
                               "lg": g["lg%d" % sub], "lb": g["lb%d" % sub]},
                              {"ident_b": g["ident_b"]},
                              (sub == 1) and not nofinal)
                    ctx.pop_scope()
                    if upto > 3 + 2 * sub:
                        boundary()

            if start <= 5 < upto:
                ctx.push_scope()
                alloc_C(ctx)
                emit_block_C(ctx, block, g["x2_d"], g["x3_d"],
                             {"Wq": g["CWq"], "Wk": g["CWk"], "Wv": g["CWv"], "Wp": g["CWp"]},
                             {"bq": g["cbq"], "bk": g["cbk"], "bv": g["cbv"], "bp": g["cbp"]},
                             {"ident_b": g["ident_b"]},
                             (g["xlnT_d"], g["qT_d"], g["kT_d"], g["yT_dC"], g["cscr"]))
                ctx.pop_scope()
                if upto > 6:
                    boundary()

            if start <= 6 < upto:
                ctx.push_scope()
                alloc_D(ctx)
                emit_block_D(ctx, block, g["x3_d"], x_out,
                             {"wg": g["wg"], "wu": g["wu"], "wd": g["wd"]},
                             {"ident_b": g["ident_b"]})
                ctx.pop_scope()

            @block.sync
            def _(sync):
                ctx.wait(sync, "st")
                ctx.wait(sync, "ld")
    return nc


def kernel(**inputs):
    consts = host_constants()
    fc64 = np.zeros((1, 64), dtype=np.float32)
    fc64[0, :len(FCONSTS)] = np.array(FCONSTS, dtype=np.float32)
    lmask_b = consts["lmask"].astype(np.float32).astype(BF)
    x = np.ascontiguousarray(inputs["L0_attn_x"]).astype(np.float32)
    cores = list(range(8))
    base = {
        "Wg": inputs["L0_attn_Wg"], "Wv": inputs["L0_attn_Wv"], "Wd": inputs["L0_attn_Wd"],
        "bg": inputs["L0_attn_bg"], "bv": inputs["L0_attn_bv"], "bd": inputs["L0_attn_bd"],
        "CWq": inputs["L1_attn_Wq"], "CWk": inputs["L1_attn_Wk"],
        "CWv": inputs["L1_attn_Wv"], "CWp": inputs["L1_attn_Wp"],
        "cbq": inputs["L1_attn_bq"], "cbk": inputs["L1_attn_bk"],
        "cbv": inputs["L1_attn_bv"], "cbp": inputs["L1_attn_bp"],
        "wg": inputs["L1_ffn_w_gate"], "wu": inputs["L1_ffn_w_up"],
        "wd": inputs["L1_ffn_w_down"],
        "ident_b": consts["ident_b"], "ident_f": consts["ident_f"],
        "fconsts": fc64, "lmask_b": lmask_b,
        "rope_cq": consts["rope_cq"], "rope_sq": consts["rope_sq"],
        "rope_ck": consts["rope_ck"], "rope_sk": consts["rope_sk"],
    }
    for sub in range(2):
        base["wqkv%d" % sub] = np.ascontiguousarray(inputs["L0_ffn_wqkv"][sub])
        base["qs%d" % sub] = np.ascontiguousarray(inputs["L0_ffn_q_scale"][sub])
        base["ks%d" % sub] = np.ascontiguousarray(inputs["L0_ffn_k_scale"][sub])
        base["wo%d" % sub] = np.ascontiguousarray(inputs["L0_ffn_wo"][sub])
        base["w1_%d" % sub] = np.ascontiguousarray(inputs["L0_ffn_w1"][sub])
        base["b1_%d" % sub] = np.ascontiguousarray(inputs["L0_ffn_b1"][sub])
        base["w2_%d" % sub] = np.ascontiguousarray(inputs["L0_ffn_w2"][sub])
        base["b2_%d" % sub] = np.ascontiguousarray(inputs["L0_ffn_b2"][sub])
        base["lg%d" % sub] = np.ascontiguousarray(inputs["L0_ffn_ln_g"][sub])
        base["lb%d" % sub] = np.ascontiguousarray(inputs["L0_ffn_ln_b"][sub])
    nc = build_fused(0, 7)
    km = [dict(base, x_in=x[c]) for c in cores]
    res = run_bass_kernel_spmd(nc, km, core_ids=cores)
    return np.stack([r["x_out"] for r in res.results], axis=0)
